# version 01
# baseline (speedup 1.0000x reference)
import numpy as np
import contextlib
from contextlib import contextmanager
import concourse.bass as bass
import concourse.mybir as mybir
from concourse.bass_utils import run_bass_kernel_spmd
from concourse.expressions_rust import RuntimeValue

F32 = mybir.dt.float32
BF16 = mybir.dt.bfloat16
AF = mybir.ActivationFunctionType
OP = mybir.AluOpType
AX = mybir.AxisListType
ts = bass.ts

S = 8192
D = 1024
E0 = 2048
N0 = 16
DTR = 128
NH, DKH, DVH = 12, 64, 128
KD, VD = NH * DKH, NH * DVH
FF = 4096
NEXP = 8
TC = 128
NCH = S // TC


def set_seq(s):
    global S, NCH
    S = s
    NCH = S // TC


class Builder:
    PHASES = ["wcast", "ln0", "zin", "conv0", "xproj", "dtproj", "scan", "wout",
              "tern", "ffn0", "resid1", "qkvg", "convq", "delta", "route", "moe"]

    def __init__(self, nc):
        self.nc = nc
        self._semn = {}
        self._pool = [nc.alloc_semaphore('gp%d' % j) for j in range(20)]
        self._newsems = []
        self._stack = None
        self._loopregs = []
        self._cq_r1 = {}
        self._cq_qh = {}
        self._cq_tapv = 0

    def begin_phase(self):
        self._stack = contextlib.ExitStack()
        self._semmap = {}
        self._semidx = 0

    def end_phase(self):
        if self._stack is not None:
            self._stack.close()
            self._stack = None
        for s in self._pool:
            self.nc.gpsimd.sem_clear(s)
        self.bar()

    def sem(self, name):
        if name not in self._semmap:
            assert self._semidx < len(self._pool), "sem pool exhausted"
            self._semmap[name] = self._pool[self._semidx]
            self._semidx += 1
        return self._semmap[name]

    def phase_sems(self, *names):
        return tuple(self.sem(n) for n in names)

    def sbp(self, name, shape, dtype):
        return self.nc.alloc_sbuf_tensor(name, shape, dtype)

    def sb(self, name, shape, dtype):
        if self._stack is None:
            return self.nc.alloc_sbuf_tensor(name, shape, dtype)
        return self._stack.enter_context(self.nc.sbuf_tensor(name, shape, dtype))

    def ps(self, name, shape, dtype=F32):
        if self._stack is None:
            return self.nc.alloc_psum_tensor(name, shape, dtype)
        return self._stack.enter_context(self.nc.psum_tensor(name, shape, dtype))

    def bar(self):
        self.nc.all_engine_barrier()

    @contextmanager
    def fori(self, start, end, step=1, depth=0):
        nc = self.nc
        while len(self._loopregs) <= depth:
            j = len(self._loopregs)
            self._loopregs.append((
                nc.alloc_registers("kf_i%d" % j, engines=mybir.ALL_ENGINES),
                nc.alloc_registers("kf_s%d" % j, engines=mybir.ALL_ENGINES)))
        R, SN = self._loopregs[depth]
        lid = nc.next_id()
        loop_start, loop_end = "kf%d_loop" % lid, "kf%d_end" % lid
        nc.regs_mov(R, start)
        nc.br(loop_start, engines=mybir.ALL_ENGINES)
        with nc.body(loop_start, valid_engines=mybir.ALL_ENGINES):
            nc.regs_mov(SN, R)
            yield RuntimeValue(SN, min_val=start, max_val=end - step)
            nc.regs_alu(R, R, step, op=OP.add)
            nc.br_lt(R, end, on_true=loop_start, on_false=loop_end,
                     engines=mybir.ALL_ENGINES)
        nc.switch_bb(loop_end)

    def clears(self, *sems):
        self.bar()
        for s in sems:
            self.nc.gpsimd.sem_clear(s)
        self.bar()

    def declare_io(self):
        nc = self.nc
        dram = lambda n, s, d=F32, k="ExternalInput": nc.dram_tensor(n, s, d, kind=k).ap()
        self.x = dram("L0_attn_x", [S, D])
        self.W_in = dram("L0_attn_W_in", [D, 2 * E0])
        self.b_in = dram("L0_attn_b_in", [2 * E0])
        self.conv_w = dram("L0_attn_conv_w", [E0, 1, 4])
        self.conv_b = dram("L0_attn_conv_b", [E0])
        self.x_proj_w = dram("L0_attn_x_proj_w", [E0, DTR + 2 * N0])
        self.dt_proj_w = dram("L0_attn_dt_proj_w", [DTR, E0])
        self.dt_bias = dram("L0_attn_dt_bias", [E0])
        self.A_log = dram("L0_attn_A_log", [E0, N0])
        self.D_param = dram("L0_attn_D_param", [E0])
        self.W_out = dram("L0_attn_W_out", [E0, D])
        self.b_out = dram("L0_attn_b_out", [D])
        self.fW1 = dram("L0_ffn_W1", [FF, D])
        self.fb1 = dram("L0_ffn_b1", [FF])
        self.fW2 = dram("L0_ffn_W2", [D, FF])
        self.fb2 = dram("L0_ffn_b2", [D])
        self.Wq = dram("L1_attn_Wq", [D, KD])
        self.Wk = dram("L1_attn_Wk", [D, KD])
        self.Wv = dram("L1_attn_Wv", [D, VD])
        self.Wg = dram("L1_attn_Wg", [D, VD])
        self.Wo = dram("L1_attn_Wo", [VD, D])
        self.conv_q = dram("L1_attn_conv_q", [KD, 4])
        self.conv_k = dram("L1_attn_conv_k", [KD, 4])
        self.conv_v = dram("L1_attn_conv_v", [VD, 4])
        self.Wgk = dram("L1_attn_Wgk", [D, NH])
        self.Wb = dram("L1_attn_Wb", [D, NH])
        self.b_b = dram("L1_attn_b_b", [NH])
        self.A1_log = dram("L1_attn_A_log", [NH])
        self.dt1_bias = dram("L1_attn_dt_bias", [NH])
        self.gnw = dram("L1_attn_g_norm_weight", [DVH])
        self.Wr = dram("L1_ffn_Wr", [D, NEXP])
        self.br = dram("L1_ffn_br", [NEXP])
        self.mW1 = dram("L1_ffn_W1", [NEXP, D, FF])
        self.mb1 = dram("L1_ffn_b1", [NEXP, FF])
        self.mW2 = dram("L1_ffn_W2", [NEXP, FF, D])
        self.mb2 = dram("L1_ffn_b2", [NEXP, D])
        self.identf = dram("c_identf", [128, 128])
        self.identb = dram("c_identb", [128, 128], BF16)
        self.negtI = dram("c_negtI", [128, 128])
        self.negtS = dram("c_negtS", [128, 128])
        self.oneh16 = dram("c_oneh16", [16, 16 * 128])
        self.oneh12 = dram("c_oneh12", [12, 12 * 128])
        self.oneh8 = dram("c_oneh8", [8, 8 * 128])
        self.eye12 = dram("c_eye12", [12, 12])
        self.ones1 = dram("c_ones1", [1, 128])
        self.ones128c = dram("c_ones128c", [128, 1])
        self.sel2 = dram("c_sel2", [128, 2])
        self.bc2 = dram("c_bc2", [2, 128])
        self.out = dram("out", [S, D], F32, "ExternalOutput")

        scr = lambda n, s, d: nc.dram_tensor(n, s, d, kind="Internal").ap()
        self.xnz = scr("s_xnz", [S // 512, 128, 8 * 512], BF16)
        self.zz = scr("s_zz", [S // 512, 128, 32 * 512], F32)
        self.xc32 = scr("s_xc32", [16, 128, S], F32)
        self.delta32 = scr("s_delta32", [16, 128, S], F32)
        self.yT = scr("s_yT", [16, 128, S], BF16)
        self.s0T = scr("s_s0T", [8, 128, S], F32)
        self.g0z = scr("s_g0z", [S // 512, 128, 8 * 512], F32)
        self.x1s = scr("s_x1", [S, D], F32)
        self.xn1z = scr("s_xn1z", [S // 512, 128, 8 * 512], BF16)
        self.qkrz = scr("s_qkrz", [S // 512, 128, 12 * 512], BF16)
        self.vrz = scr("s_vrz", [S // 512, 128, 12 * 512], BF16)
        self.qkhz = scr("s_qkhz", [S // 512, 128, 12 * 512], BF16)
        self.vgz = scr("s_vgz", [S // 512, 128, 24 * 512], BF16)
        self.gl_d = scr("s_gl", [32, S], F32)
        self.BC = scr("s_BC", [16, 2 * S], F32)
        self.dtT = scr("s_dtT", [128, S], BF16)
        self.apack = scr("s_apack", [16, 128, 17], F32)
        self.mbboth_d = scr("s_mbboth", [128, NEXP, 40], F32)
        self.s1z = scr("s_s1z", [S // 512, 128, 8 * 512], F32)
        self.weT_d = scr("s_weT", [NEXP, S], F32)
        self.W_in_b = scr("w_in_b", [8, 128, 2 * E0], BF16)
        self.xproj_b = scr("w_xproj_b", [16, 128, DTR + 2 * N0], BF16)
        self.dtproj_b = scr("w_dtproj_b", [1, 128, E0], BF16)
        self.Wout_b = scr("w_wout_b", [16, 128, D], BF16)
        self.W1t_b = scr("w_w1t_b", [8, 128, FF], BF16)
        self.W2t_b = scr("w_w2t_b", [32, 128, D], BF16)
        self.Wq_b = scr("w_wq_b", [8, 128, KD], BF16)
        self.Wk_b = scr("w_wk_b", [8, 128, KD], BF16)
        self.Wv_b = scr("w_wv_b", [8, 128, VD], BF16)
        self.Wg_b = scr("w_wg_b", [8, 128, VD], BF16)
        self.Wo_b = scr("w_wo_b", [12, 128, D], BF16)
        self.Wgk_b = scr("w_wgk_b", [8, 128, NH], BF16)
        self.Wb_b = scr("w_wb_b", [8, 128, NH], BF16)
        self.Wr_b = scr("w_wr_b", [8, 128, NEXP], BF16)
        self.mW1_b = scr("w_mw1_b", [NEXP * 8, 128, FF], BF16)
        self.mW2_b = scr("w_mw2_b", [NEXP * 32, 128, D], BF16)

        sb = self.sbp
        self.c_identf = sb("k_identf", [128, 128], F32)
        self.c_identb = sb("k_identb", [128, 128], BF16)
        self.c_negtI = sb("k_negtI", [128, 128], F32)
        self.c_negtS = sb("k_negtS", [128, 128], F32)
        self.c_oneh16 = sb("k_oneh16", [16, 16 * 128], F32)
        self.c_oneh12 = sb("k_oneh12", [12, 12 * 128], F32)
        self.c_oneh8 = sb("k_oneh8", [8, 8 * 128], F32)
        self.c_eye12 = sb("k_eye12", [12, 12], F32)
        self.c_ones1 = sb("k_ones1", [1, 128], F32)
        self.c_ones128c = sb("k_ones128c", [128, 1], F32)
        self.c_sel2 = sb("k_sel2", [128, 2], F32)
        self.c_bc2 = sb("k_bc2", [2, 128], F32)
        self.bc_b_in = sb("bc_b_in", [128, 32], F32)
        self.bc_conv_b = sb("bc_conv_b", [128, 16], F32)
        self.bc_dt_bias = sb("bc_dt_bias", [128, 16], F32)
        self.bc_D = sb("bc_D", [128, 16], F32)
        self.bc_b_out = sb("bc_b_out", [128, 8], F32)
        self.bc_fb1 = sb("bc_fb1", [128, 32], F32)
        self.bc_fb2 = sb("bc_fb2", [128, 8], F32)
        self.bc_mb1 = sb("bc_mb1", [128, NEXP * 32], F32)
        self.bc_mb2 = sb("bc_mb2", [128, NEXP * 8], F32)
        self.bc_gnw = sb("bc_gnw", [128, 1], F32)
        self.bc_cw0 = sb("bc_cw0", [128, 16 * 4], F32)
        self.bc_cwq = sb("bc_cwq", [128, 6 * 4], F32)
        self.bc_cwk = sb("bc_cwk", [128, 6 * 4], F32)
        self.bc_cwv = sb("bc_cwv", [128, 12 * 4], F32)
        self.A0col = sb("A0col", [128, 16 * 16], F32)
        self.r_bb = sb("r_bb", [12, 1], F32)
        self.r_dtb = sb("r_dtb", [12, 1], F32)
        self.r_negA = sb("r_negA", [12, 1], F32)
        self.Sst = sb("Sst", [64, NH * DVH], F32)

    def _bias_cols(self, dst, src1d, ncols, col0=0):
        nc = self.nc
        st, pt = self._wc_stage, self._wc_pt
        sL, sPt, sPv = self.sem("wcL"), self.sem("wcPt"), self.sem("wcPv")
        self._wcl += 1
        nc.sync.wait_ge(sL, max(0, 16 * self._wcl - 16))
        nc.sync.dma_start(out=st[0:ncols, 0:128],
                          in_=src1d.rearrange("(a b) -> a b", b=128)).then_inc(sL, 16)
        nc.tensor.wait_ge(sL, 16 * self._wcl)
        nc.tensor.transpose(pt[0:128, 0:ncols], st[0:ncols, 0:128],
                            self.c_identf[0:ncols, 0:ncols]).then_inc(sPt, 1)
        self._wcpt += 1
        nc.vector.wait_ge(sPt, self._wcpt)
        nc.vector.tensor_copy(dst[:, col0:col0 + ncols], pt[0:128, 0:ncols]).then_inc(sPv, 1)
        self._wcpv += 1
        nc.tensor.wait_ge(sPv, self._wcpv)
        nc.sync.wait_ge(sPv, self._wcpv)

    def _cast_unroll(self, src2d, dst3d, ktiles, w):
        nc = self.nc
        sC = self.sem("wcC")
        for k in range(ktiles):
            c = self._wcc
            buf = self._wc_cin[c % 2]
            obuf = self._wc_cout[c % 2]
            if c >= 1:
                nc.sync.wait_ge(sC, (c - 1) * 48 + 32)
            nc.sync.dma_start(out=buf[:, 0:w], in_=src2d[k * 128:(k + 1) * 128, 0:w]).then_inc(sC, 16)
            nc.vector.wait_ge(sC, c * 48 + 16)
            nc.vector.tensor_copy(obuf[:, 0:w], buf[:, 0:w]).then_inc(sC, 16)
            nc.scalar.wait_ge(sC, c * 48 + 32)
            nc.scalar.dma_start(out=dst3d[k, :, 0:w], in_=obuf[:, 0:w]).then_inc(sC, 16)
            self._wcc += 1

    def ph_wcast(self):
        nc = self.nc
        sL = self.phase_sems("wcL", "wcPt", "wcPv", "wcPa", "wcC", "wcCV", "wcCS")[0]
        self._wcl = 0
        self._wcpt = 0
        self._wcpv = 0
        self._wcpa = 0
        self._wcc = 0
        for dst, src in [(self.c_identf, self.identf), (self.c_identb, self.identb),
                         (self.c_negtI, self.negtI), (self.c_negtS, self.negtS),
                         (self.c_oneh16, self.oneh16), (self.c_oneh12, self.oneh12),
                         (self.c_oneh8, self.oneh8), (self.c_eye12, self.eye12),
                         (self.c_ones1, self.ones1), (self.c_ones128c, self.ones128c),
                         (self.c_sel2, self.sel2), (self.c_bc2, self.bc2)]:
            nc.sync.wait_ge(sL, max(0, 16 * self._wcl - 16))
            nc.sync.wait_ge(sL, max(0, 16 * self._wcl))
            nc.sync.dma_start(out=dst[:], in_=src[:]).then_inc(sL, 16)
            self._wcl += 1
        self._wc_stage = self.sb("wc_stage", [128, 128], F32)
        self._wc_pt = self.ps("wc_pt", [128, 128], F32)
        nc.tensor.wait_ge(sL, 16 * self._wcl)
        self._bias_cols(self.bc_b_in, self.b_in, 32)
        self._bias_cols(self.bc_conv_b, self.conv_b, 16)
        self._bias_cols(self.bc_dt_bias, self.dt_bias, 16)
        self._bias_cols(self.bc_D, self.D_param, 16)
        self._bias_cols(self.bc_b_out, self.b_out, 8)
        self._bias_cols(self.bc_fb1, self.fb1, 32)
        self._bias_cols(self.bc_fb2, self.fb2, 8)
        for e in range(NEXP):
            self._bias_cols(self.bc_mb1, self.mb1[e, :], 32, e * 32)
            self._bias_cols(self.bc_mb2, self.mb2[e, :], 8, e * 8)
        sPv = self.sem("wcPv")
        for (dst, src, ntile) in [(self.bc_cw0, self.conv_w.rearrange("c o j -> (c o) j"), 16),
                                  (self.bc_cwq, self.conv_q, 6), (self.bc_cwk, self.conv_k, 6),
                                  (self.bc_cwv, self.conv_v, 12)]:
            for t in range(ntile):
                self._wcl += 1
                nc.sync.wait_ge(sL, max(0, 16 * self._wcl - 16))
                nc.sync.dma_start(out=self._wc_stage[0:128, 0:4],
                                  in_=src[t * 128:(t + 1) * 128, :]).then_inc(sL, 16)
                nc.vector.wait_ge(sL, 16 * self._wcl)
                nc.vector.tensor_copy(dst[:, t * 4:(t + 1) * 4],
                                      self._wc_stage[0:128, 0:4]).then_inc(sPv, 1)
                self._wcpv += 1
                nc.sync.wait_ge(sPv, self._wcpv)
        self._wcl += 1
        nc.sync.wait_ge(sL, max(0, 16 * self._wcl - 16))
        nc.sync.dma_start(out=self.bc_gnw[:, 0:1],
                          in_=self.gnw.rearrange("(a b) -> a b", b=1)).then_inc(sL, 16)
        a0s = self.sb("wc_a0", [128, 16 * 16], F32)
        for t in range(16):
            self._wcl += 1
            nc.sync.wait_ge(sL, max(0, 16 * self._wcl - 16))
            nc.sync.dma_start(out=a0s[:, t * 16:(t + 1) * 16],
                              in_=self.A_log[t * 128:(t + 1) * 128, :]).then_inc(sL, 16)
        sPa = self.sem("wcPa")
        nc.scalar.wait_ge(sL, 16 * self._wcl)
        nc.scalar.activation(self.A0col[:], a0s[:], AF.Exp, bias=0.0, scale=1.0).then_inc(sPa, 1)
        self._wcpa += 1
        nc.vector.wait_ge(sPa, self._wcpa)
        nc.vector.tensor_scalar_mul(self.A0col[:], self.A0col[:], -1.0).then_inc(sPv, 1)
        self._wcpv += 1
        nc.vector.wait_ge(sPv, self._wcpv)
        s12 = self.sb("wc_s12", [12, 3], F32)
        for j, src in enumerate([self.b_b, self.dt1_bias, self.A1_log]):
            self._wcl += 1
            nc.sync.wait_ge(sL, max(0, 16 * self._wcl - 16))
            nc.sync.dma_start(out=s12[0:12, j:j + 1],
                              in_=src.rearrange("(a b) -> a b", b=1)).then_inc(sL, 16)
        nc.vector.wait_ge(sL, 16 * self._wcl)
        nc.vector.tensor_copy(self.r_bb[:], s12[0:12, 0:1])
        nc.vector.tensor_copy(self.r_dtb[:], s12[0:12, 1:2]).then_inc(sPv, 1)
        self._wcpv += 1
        nc.scalar.wait_ge(sPv, self._wcpv)
        nc.scalar.activation(self.r_negA[:], s12[0:12, 2:3], AF.Exp, bias=0.0, scale=1.0).then_inc(sPa, 1)
        self._wcpa += 1
        nc.vector.wait_ge(sPa, self._wcpa)
        nc.vector.tensor_scalar_mul(self.r_negA[:], self.r_negA[:], -1.0).then_inc(sPv, 1)
        self._wcpv += 1
        nc.sync.wait_ge(sPv, self._wcpv)
        st17 = self.sb("wc_a17", [128, 17], F32)
        for e in range(16):
            if e > 0:
                nc.vector.wait_ge(sL, 16 * self._wcl)
            nc.vector.tensor_copy(st17[:, 0:16], self.A0col[:, e * 16:(e + 1) * 16])
            nc.vector.tensor_copy(st17[:, 16:17], self.bc_D[:, e:e + 1]).then_inc(sPv, 1)
            self._wcpv += 1
            nc.sync.wait_ge(sPv, self._wcpv)
            nc.sync.dma_start(out=self.apack[e, :, :], in_=st17[:]).then_inc(sL, 16)
            self._wcl += 1
        for e in range(NEXP):
            nc.sync.wait_ge(sL, 16 * self._wcl)
            nc.sync.dma_start(out=self.mbboth_d[:, e, 0:32],
                              in_=self.bc_mb1[:, e * 32:(e + 1) * 32]).then_inc(sL, 16)
            nc.sync.wait_ge(sL, 16 * self._wcl + 16)
            nc.sync.dma_start(out=self.mbboth_d[:, e, 32:40],
                              in_=self.bc_mb2[:, e * 8:(e + 1) * 8]).then_inc(sL, 16)
            self._wcl += 2
        nc.sync.wait_ge(sL, 16 * self._wcl)
        self.bar()
        self._wc_cin = [self.sb("wc_cin%d" % j, [128, 4096], F32) for j in range(2)]
        self._wc_cout = [self.sb("wc_cout%d" % j, [128, 4096], BF16) for j in range(2)]
        U = self._cast_unroll
        U(self.W_in, self.W_in_b, 8, 2 * E0)
        U(self.x_proj_w, self.xproj_b, 16, DTR + 2 * N0)
        U(self.dt_proj_w, self.dtproj_b, 1, E0)
        U(self.W_out, self.Wout_b, 16, D)
        U(self.Wq, self.Wq_b, 8, KD)
        U(self.Wk, self.Wk_b, 8, KD)
        U(self.Wv, self.Wv_b, 8, VD)
        U(self.Wg, self.Wg_b, 8, VD)
        U(self.Wo, self.Wo_b, 12, D)
        U(self.Wgk, self.Wgk_b, 8, NH)
        U(self.Wb, self.Wb_b, 8, NH)
        U(self.Wr, self.Wr_b, 8, NEXP)
        nc.vector.wait_ge(self.sem("wcC"), self._wcc * 48)
        self.bar()
        U(self.mW1.rearrange("e d f -> (e d) f"), self.mW1_b, NEXP * 8, FF)
        U(self.mW2.rearrange("e f d -> (e f) d"), self.mW2_b, NEXP * 32, D)
        nc.vector.wait_ge(self.sem("wcC"), self._wcc * 48)

    def _ln_tile(self, xt, xn, dump, stats, sV, sA, vb, ab):
        nc = self.nc
        nc.scalar.activation(dump[:, 0:D], xt[:], AF.Square, bias=0.0, scale=1.0,
                             accum_out=stats[:, 0:1]).then_inc(sA, 1)
        nc.scalar.wait_ge(sA, ab + 1)
        nc.scalar.activation(dump[:, D:2 * D], xt[:], AF.Identity, bias=0.0, scale=1.0,
                             accum_out=stats[:, 7:8]).then_inc(sA, 1)
        nc.vector.wait_ge(sA, ab + 2)
        nc.vector.tensor_scalar(stats[:, 1:2], stats[:, 7:8], stats[:, 7:8],
                                -1.0 / D, OP.mult, OP.mult).then_inc(sV, 1)
        nc.vector.wait_ge(sV, vb + 1)
        nc.vector.scalar_tensor_tensor(stats[:, 2:3], stats[:, 1:2], 1.0,
                                       stats[:, 0:1], OP.mult, OP.add).then_inc(sV, 1)
        nc.vector.wait_ge(sV, vb + 2)
        nc.vector.tensor_scalar(stats[:, 3:4], stats[:, 2:3], 1.0 / D, 1e-5,
                                OP.mult, OP.add).then_inc(sV, 1)
        nc.scalar.wait_ge(sV, vb + 3)
        nc.scalar.activation(stats[:, 4:5], stats[:, 3:4], AF.Ln, bias=0.0, scale=1.0).then_inc(sA, 1)
        nc.scalar.wait_ge(sA, ab + 3)
        nc.scalar.activation(stats[:, 5:6], stats[:, 4:5], AF.Exp, bias=0.0, scale=-0.5).then_inc(sA, 1)
        nc.vector.wait_ge(sA, ab + 4)
        nc.vector.tensor_scalar(stats[:, 6:7], stats[:, 7:8], stats[:, 5:6],
                                -1.0 / D, OP.mult, OP.mult).then_inc(sV, 1)
        nc.scalar.wait_ge(sV, vb + 4)
        nc.scalar.activation(xn[:], xt[:], AF.Identity, bias=stats[:, 6:7],
                             scale=stats[:, 5:6]).then_inc(sA, 1)

    def ph_ln0(self):
        nc = self.nc
        xt = self.sb("l0_xt", [128, D], F32)
        dump = self.sb("l0_dump", [128, 2 * D], F32)
        xn = self.sb("l0_xn", [128, D], BF16)
        stg = self.sb("l0_stg", [128, 8 * 128], BF16)
        stats = self.sb("l0_st", [128, 8], F32)
        pb = [self.ps("l0_pb%d" % j, [128, 128], BF16) for j in range(2)]
        sL, sV, sA, sT, sE, sO = self.phase_sems(*["l0_" + c for c in "LVATEO"])
        for i in range(S // 128):
            nc.sync.wait_ge(sA, i * 5)
            nc.sync.wait_ge(sL, i * 16)
            nc.sync.dma_start(out=xt[:], in_=self.x[i * 128:(i + 1) * 128, :]).then_inc(sL, 16)
            nc.vector.wait_ge(sL, i * 16 + 16)
            nc.scalar.wait_ge(sL, i * 16 + 16)
            self._ln_tile(xt, xn, dump, stats, sV, sA, i * 4, i * 5)
            nc.tensor.wait_ge(sA, i * 5 + 5)
            for d in range(8):
                j = i * 8 + d
                if j >= 2:
                    nc.tensor.wait_ge(sE, j - 1)
                nc.tensor.transpose(pb[d % 2][:], xn[:, d * 128:(d + 1) * 128],
                                    self.c_identb[:]).then_inc(sT, 1)
                nc.vector.wait_ge(sT, i * 8 + d + 1)
                if d == 0:
                    nc.vector.wait_ge(sO, i * 16)
                nc.vector.tensor_copy(stg[:, d * 128:(d + 1) * 128], pb[d % 2][:]).then_inc(sE, 1)
            nc.sync.wait_ge(sE, i * 8 + 8)
            nc.sync.wait_ge(sO, i * 16)
            nc.sync.dma_start(
                out=self.xnz[i // 4, :, :].rearrange("p (d s) -> p d s", d=8)[:, :, (i % 4) * 128:(i % 4 + 1) * 128],
                in_=stg[:].rearrange("p (d s) -> p d s", d=8)).then_inc(sO, 16)
        nc.sync.wait_ge(sO, (S // 128) * 16)

    def ph_zin(self):
        nc = self.nc
        wz = self.sb("z_w", [128, 8 * 4096], BF16)
        sW, = self.phase_sems("z_W")
        for k in range(8):
            nc.scalar.dma_start(out=wz[:, k * 4096:(k + 1) * 4096],
                                in_=self.W_in_b[k, :, :]).then_inc(sW, 16)
        nc.tensor.wait_ge(sW, 8 * 16)
        xch = self.sb("z_x", [128, 8 * 512], BF16)
        ostg = self.sb("z_o", [128, 32 * 512], F32)
        pz = [self.ps("z_p%d" % j, [128, 512], F32) for j in range(4)]
        sL, sT, sE = self.phase_sems("z_L", "z_T", "z_E")
        with self.fori(0, S // 512) as i:
            nc.sync.dma_start(out=xch[:], in_=self.xnz[ts(i, 1), :, :]).then_inc(sL, 16)
            nc.tensor.wait_ge(sL, 16)
            for m in range(32):
                b = m % 4
                if m >= 4:
                    nc.tensor.wait_ge(sE, m - 3)
                for k in range(8):
                    mm = nc.tensor.matmul(pz[b][:], wz[:, k * 4096 + m * 128:k * 4096 + (m + 1) * 128],
                                          xch[:, k * 512:(k + 1) * 512],
                                          start=(k == 0), stop=(k == 7))
                mm.then_inc(sT, 1)
                nc.scalar.wait_ge(sT, m + 1)
                fn = AF.Identity if m < 16 else AF.Silu
                nc.scalar.activation(ostg[:, m * 512:(m + 1) * 512], pz[b][:], fn,
                                     bias=self.bc_b_in[:, m:m + 1], scale=1.0).then_inc(sE, 1)
            nc.scalar.wait_ge(sE, 32)
            nc.scalar.dma_start(out=self.zz[ts(i, 1), :, :], in_=ostg[:]).then_inc(sZO := self.sem("z_O"), 16)
            nc.gpsimd.wait_ge(sL, 16)
            nc.gpsimd.wait_ge(sZO, 16)
            self.clears(sL, sT, sE, sZO)

    def ph_conv0(self):
        nc = self.nc
        wk = self.sb("c0_wk", [128, S + 3], F32)
        a1 = self.sb("c0_a1", [128, S], F32)
        a2 = self.sb("c0_a2", [128, S], F32)
        sL, sV, sA, sO = self.phase_sems("c0_L", "c0_V", "c0_A", "c0_O")
        for e in range(16):
            if e > 0:
                nc.sync.wait_ge(sV, e * 4)
            nc.sync.wait_ge(sL, e * 32)
            half = (S // 512) // 2
            for g in range(2):
                nc.sync.dma_start(
                    out=wk[:, 3:S + 3].rearrange("p (c s) -> p c s", c=S // 512)[:, g * half:(g + 1) * half, :],
                    in_=self.zz.rearrange("c p m -> p c m")[:, g * half:(g + 1) * half,
                                                           e * 512:(e + 1) * 512]).then_inc(sL, 16)
            if e == 0:
                nc.vector.memset(wk[:, 0:3], 0.0).then_inc(self.sem("c0_M"), 1)
                nc.vector.wait_ge(self.sem("c0_M"), 1)
            nc.vector.wait_ge(sL, e * 32 + 32)
            if e > 0:
                nc.vector.wait_ge(sA, e * 1)
                nc.vector.wait_ge(sO, e * 16)
            w = self.bc_cw0
            nc.vector.tensor_scalar(a1[:], wk[:, 0:S], w[:, 4 * e:4 * e + 1], None,
                                    OP.mult).then_inc(sV, 1)
            nc.vector.wait_ge(sV, e * 4 + 1)
            nc.vector.scalar_tensor_tensor(a2[:], wk[:, 1:S + 1], w[:, 4 * e + 1:4 * e + 2],
                                           a1[:], OP.mult, OP.add).then_inc(sV, 1)
            nc.vector.wait_ge(sV, e * 4 + 2)
            nc.vector.scalar_tensor_tensor(a1[:], wk[:, 2:S + 2], w[:, 4 * e + 2:4 * e + 3],
                                           a2[:], OP.mult, OP.add).then_inc(sV, 1)
            nc.vector.wait_ge(sV, e * 4 + 3)
            nc.vector.scalar_tensor_tensor(a2[:], wk[:, 3:S + 3], w[:, 4 * e + 3:4 * e + 4],
                                           a1[:], OP.mult, OP.add).then_inc(sV, 1)
            nc.scalar.wait_ge(sV, e * 4 + 4)
            if e > 0:
                nc.scalar.wait_ge(sO, e * 16)
            nc.scalar.activation(a1[:], a2[:], AF.Silu,
                                 bias=self.bc_conv_b[:, e:e + 1], scale=1.0).then_inc(sA, 1)
            nc.scalar.wait_ge(sA, e + 1)
            nc.scalar.wait_ge(sO, e * 16)
            nc.scalar.dma_start(out=self.xc32[e, :, :], in_=a1[:]).then_inc(sO, 16)
        nc.sync.wait_ge(sO, 16 * 16)

    def ph_xproj(self):
        nc = self.nc
        wxp = self.sb("xp_w", [128, 16 * 160], BF16)
        sW, = self.phase_sems("xp_W")
        nc.scalar.dma_start(out=wxp[:].rearrange("p (k m) -> p k m", k=16),
                            in_=self.xproj_b.rearrange("k p m -> p k m")).then_inc(sW, 16)
        nc.tensor.wait_ge(sW, 16)
        nc.vector.wait_ge(sW, 16)
        xcf = self.sb("xp_xf", [128, 16 * 512], F32)
        xcb = self.sb("xp_xb", [128, 16 * 512], BF16)
        dstg = self.sb("xp_d", [128, 512], BF16)
        bstg = self.sb("xp_bc", [16, 1024], F32)
        pd = self.ps("xp_pd", [128, 512], F32)
        pb = self.ps("xp_pb", [16, 512], F32)
        pc = self.ps("xp_pc", [16, 512], F32)
        sL, sT, sEv, sEa, sO = self.phase_sems("xp_L", "xp_T", "xp_Ev", "xp_Ea", "xp_O")
        for i in range(S // 512):
            if i > 0:
                nc.sync.wait_ge(sEv, i * 3 - 2)
            nc.sync.wait_ge(sL, i * 32)
            for g in range(2):
                nc.sync.dma_start(
                    out=xcf[:, g * 8 * 512:(g + 1) * 8 * 512].rearrange("p (e s) -> p e s", e=8),
                    in_=self.xc32.rearrange("e p s -> p e s")[:, g * 8:(g + 1) * 8,
                                                             i * 512:(i + 1) * 512]).then_inc(sL, 16)
            nc.vector.wait_ge(sL, i * 32 + 32)
            if i > 0:
                nc.vector.wait_ge(sT, i * 3)
            nc.vector.tensor_copy(xcb[:], xcf[:]).then_inc(sEv, 1)
            nc.tensor.wait_ge(sEv, i * 3 + 1)
            if i > 0:
                nc.tensor.wait_ge(sEv, i * 3)
                nc.tensor.wait_ge(sEa, i)
            for k in range(16):
                mm = nc.tensor.matmul(pd[:], wxp[:, k * 160:k * 160 + 128],
                                      xcb[:, k * 512:(k + 1) * 512],
                                      start=(k == 0), stop=(k == 15))
            mm.then_inc(sT, 1)
            for k in range(16):
                mm = nc.tensor.matmul(pb[:], wxp[:, k * 160 + 128:k * 160 + 144],
                                      xcb[:, k * 512:(k + 1) * 512],
                                      start=(k == 0), stop=(k == 15))
            mm.then_inc(sT, 1)
            for k in range(16):
                mm = nc.tensor.matmul(pc[:], wxp[:, k * 160 + 144:k * 160 + 160],
                                      xcb[:, k * 512:(k + 1) * 512],
                                      start=(k == 0), stop=(k == 15))
            mm.then_inc(sT, 1)
            nc.scalar.wait_ge(sT, i * 3 + 1)
            if i > 0:
                nc.scalar.wait_ge(sO, i * 48)
            nc.scalar.copy(dstg[:], pd[:]).then_inc(sEa, 1)
            nc.vector.wait_ge(sT, i * 3 + 2)
            if i > 0:
                nc.vector.wait_ge(sO, i * 48)
            nc.vector.tensor_copy(bstg[0:16, 0:512], pb[:]).then_inc(sEv, 1)
            nc.vector.wait_ge(sT, i * 3 + 3)
            nc.vector.wait_ge(sEv, i * 3 + 2)
            nc.vector.tensor_copy(bstg[0:16, 512:1024], pc[:]).then_inc(sEv, 1)
            nc.scalar.wait_ge(sEv, i * 3 + 3)
            nc.scalar.wait_ge(sEa, i + 1)
            nc.scalar.wait_ge(sO, i * 48)
            nc.scalar.dma_start(out=self.dtT[:, i * 512:(i + 1) * 512], in_=dstg[:]).then_inc(sO, 16)
            nc.scalar.dma_start(out=self.BC[:, i * 512:(i + 1) * 512],
                                in_=bstg[:, 0:512]).then_inc(sO, 16)
            nc.scalar.dma_start(out=self.BC[:, S + i * 512:S + (i + 1) * 512],
                                in_=bstg[:, 512:1024]).then_inc(sO, 16)
        nc.sync.wait_ge(sO, (S // 512) * 48)

    def ph_dtproj(self):
        nc = self.nc
        wdt = self.sb("dp_w", [128, E0], BF16)
        sW, = self.phase_sems("dp_W")
        nc.scalar.dma_start(out=wdt[:], in_=self.dtproj_b[0, :, :]).then_inc(sW, 16)
        nc.tensor.wait_ge(sW, 16)
        dch = self.sb("dp_x", [128, 512], BF16)
        ustg = self.sb("dp_u", [128, 16 * 512], F32)
        pz = [self.ps("dp_p%d" % j, [128, 512], F32) for j in range(4)]
        sL, sT, sV, sA, sO = self.phase_sems("dp_L", "dp_T", "dp_V", "dp_A", "dp_O")
        for i in range(S // 512):
            if i > 0:
                nc.sync.wait_ge(sT, i * 16)
            nc.sync.wait_ge(sL, i * 16)
            nc.sync.dma_start(out=dch[:], in_=self.dtT[:, i * 512:(i + 1) * 512]).then_inc(sL, 16)
            nc.tensor.wait_ge(sL, i * 16 + 16)
            for m in range(16):
                b = m % 4
                gm = i * 16 + m
                if gm >= 4:
                    nc.tensor.wait_ge(sV, gm - 3)
                nc.tensor.matmul(pz[b][:], wdt[:, m * 128:(m + 1) * 128], dch[:],
                                 start=True, stop=True).then_inc(sT, 1)
                nc.vector.wait_ge(sT, gm + 1)
                if i > 0 and m == 0:
                    nc.vector.wait_ge(sO, i * 32)
                nc.vector.tensor_scalar(ustg[:, m * 512:(m + 1) * 512], pz[b][:],
                                        self.bc_dt_bias[:, m:m + 1], 80.0,
                                        OP.add, OP.min).then_inc(sV, 1)
            nc.scalar.wait_ge(sV, i * 16 + 16)
            for m in range(16):
                sl = ustg[:, m * 512:(m + 1) * 512]
                nc.scalar.activation(sl, sl, AF.Exp, bias=0.0, scale=1.0).then_inc(sA, 1)
                nc.scalar.wait_ge(sA, i * 32 + 2 * m + 1)
                nc.scalar.activation(sl, sl, AF.Ln, bias=1.0, scale=1.0).then_inc(sA, 1)
                nc.scalar.wait_ge(sA, i * 32 + 2 * m + 2)
            nc.scalar.wait_ge(sO, i * 32)
            for g in range(2):
                nc.scalar.dma_start(
                    out=self.delta32.rearrange("e p s -> p e s")[:, g * 8:(g + 1) * 8,
                                                                 i * 512:(i + 1) * 512],
                    in_=ustg[:, g * 8 * 512:(g + 1) * 8 * 512].rearrange("p (e s) -> p e s", e=8)).then_inc(sO, 16)
        nc.sync.wait_ge(sO, (S // 512) * 32)

    def ph_scan(self):
        nc = self.nc
        drow = self.sb("sc_d", [128, S], F32)
        urow = self.sb("sc_u", [128, S], F32)
        yrow = self.sb("sc_y", [128, S], BF16)
        acur = self.sb("sc_a", [128, 17], F32)
        bct = [self.sb("sc_bc%d" % j, [16, 1024], F32) for j in range(2)]
        dab = [self.sb("sc_da%d" % j, [128, 512], F32) for j in range(2)]
        dub = self.sb("sc_du", [128, 512], F32)
        dbub = self.sb("sc_db", [128, 512], F32)
        hall = self.sb("sc_h", [128, 16 * 512], F32)
        ybuf = self.sb("sc_yb", [128, 512], F32)
        carry = self.sb("sc_c", [128, 16], F32)
        pB = [self.ps("sc_pB%d" % j, [128, 512], F32) for j in range(2)]
        pC = [self.ps("sc_pC%d" % j, [128, 512], F32) for j in range(2)]
        sL, sT, sV, sA, sO = self.phase_sems("sc_L", "sc_T", "sc_V", "sc_A", "sc_O")
        sBC = self.sem("sc_BC")
        NCHK = S // 512
        with self.fori(0, 16) as e:
            nc.sync.dma_start(out=drow[:], in_=self.delta32[ts(e, 1), :, :]).then_inc(sL, 16)
            nc.sync.dma_start(out=urow[:], in_=self.xc32[ts(e, 1), :, :]).then_inc(sL, 16)
            nc.scalar.dma_start(out=acur[:], in_=self.apack[ts(e, 1), :, :]).then_inc(sLA := self.sem("sc_LA"), 16)
            nc.vector.memset(carry[:], 0.0).then_inc(sV, 1)
            vv, tt, aa = 1, 0, 0
            dbuv = {}
            pv = {}
            scv = {}
            nc.vector.wait_ge(sL, 32)
            nc.scalar.wait_ge(sL, 32)
            nc.tensor.wait_ge(sL, 32)
            nc.vector.wait_ge(sLA, 16)
            nc.scalar.wait_ge(sLA, 16)
            nc.tensor.wait_ge(sLA, 16)
            for c in range(NCHK):
                csl = slice(c * 512, (c + 1) * 512)
                if c >= 1:
                    nc.sync.wait_ge(sBC, c * 32)
                if c >= 2:
                    nc.sync.wait_ge(sT, (c - 1) * 32)
                nc.sync.dma_start(out=bct[c % 2][:, 0:512],
                                  in_=self.BC[:, c * 512:(c + 1) * 512]).then_inc(sBC, 16)
                nc.sync.dma_start(out=bct[c % 2][:, 512:1024],
                                  in_=self.BC[:, S + c * 512:S + (c + 1) * 512]).then_inc(sBC, 16)
                nc.vector.tensor_mul(dub[:], drow[:, csl], urow[:, csl]).then_inc(sV, 1)
                vv += 1
                duv = vv
                nc.tensor.wait_ge(sBC, (c + 1) * 32)
                for n in range(16):
                    g = c * 16 + n
                    b = n % 2
                    if g >= 2:
                        nc.tensor.wait_ge(sV, dbuv[g - 2])
                    nc.tensor.matmul(pB[b][:], self.c_oneh16[:, n * 128:(n + 1) * 128],
                                     bct[c % 2][0:16, 0:512], start=True, stop=True).then_inc(sT, 1)
                    tt += 1
                    tB = tt
                    if g >= 2:
                        nc.scalar.wait_ge(sV, scv[g - 2])
                    nc.scalar.activation(dab[b][:], drow[:, csl], AF.Exp, bias=0.0,
                                         scale=acur[:, n:n + 1]).then_inc(sA, 1)
                    aa += 1
                    nc.vector.wait_ge(sT, tB)
                    nc.vector.wait_ge(sA, aa)
                    nc.vector.wait_ge(sV, duv)
                    nc.vector.tensor_mul(dbub[:], dub[:], pB[b][:]).then_inc(sV, 1)
                    vv += 1
                    dbuv[g] = vv
                    nc.vector.wait_ge(sV, vv)
                    nc.vector.tensor_tensor_scan(hall[:, n * 512:(n + 1) * 512], dab[b][:],
                                                 dbub[:], carry[:, n:n + 1],
                                                 OP.mult, OP.add).then_inc(sV, 1)
                    vv += 1
                    scv[g] = vv
                    nc.vector.wait_ge(sV, vv)
                    nc.vector.tensor_copy(carry[:, n:n + 1],
                                          hall[:, n * 512 + 511:n * 512 + 512]).then_inc(sV, 1)
                    vv += 1
                    if g >= 2:
                        nc.tensor.wait_ge(sV, pv[g - 2])
                    nc.tensor.matmul(pC[b][:], self.c_oneh16[:, n * 128:(n + 1) * 128],
                                     bct[c % 2][0:16, 512:1024], start=True, stop=True).then_inc(sT, 1)
                    tt += 1
                    nc.vector.wait_ge(sT, tt)
                    nc.vector.wait_ge(sV, vv)
                    nc.vector.tensor_mul(hall[:, n * 512:(n + 1) * 512], hall[:, n * 512:(n + 1) * 512],
                                         pC[b][:]).then_inc(sV, 1)
                    vv += 1
                    pv[g] = vv
                nc.vector.wait_ge(sV, vv)
                nc.vector.tensor_reduce(ybuf[:], hall[:].rearrange("p (n t) -> p t n", n=16),
                                        AX.X, OP.add).then_inc(sV, 1)
                vv += 1
                nc.vector.wait_ge(sV, vv)
                nc.vector.scalar_tensor_tensor(yrow[:, csl], urow[:, csl], acur[:, 16:17],
                                               ybuf[:], OP.mult, OP.add).then_inc(sV, 1)
                vv += 1
            nc.scalar.wait_ge(sV, vv)
            nc.scalar.dma_start(out=self.yT[ts(e, 1), :, :], in_=yrow[:]).then_inc(sO, 16)
            nc.gpsimd.wait_ge(sO, 16)
            self.clears(sL, sT, sV, sA, sO, sBC, self.sem("sc_LA"))

    def ph_wout(self):
        nc = self.nc
        wwo = self.sb("wo_w", [128, 16 * D], BF16)
        sW, = self.phase_sems("wo_W")
        for g in range(2):
            nc.scalar.dma_start(out=wwo[:, g * 8 * D:(g + 1) * 8 * D].rearrange("p (k m) -> p k m", k=8),
                                in_=self.Wout_b.rearrange("k p m -> p k m")[:, g * 8:(g + 1) * 8, :]).then_inc(sW, 16)
        nc.tensor.wait_ge(sW, 32)
        ych = self.sb("wo_y", [128, 16 * 512], BF16)
        xrch = self.sb("wo_xr", [128, 16 * 512], F32)
        yx = self.sb("wo_yx", [128, 16 * 512], BF16)
        ostg = self.sb("wo_o", [128, 8 * 512], F32)
        pz = [self.ps("wo_p%d" % j, [128, 512], F32) for j in range(4)]
        sL, sT, sV, sE, sO = self.phase_sems("wo_L", "wo_T", "wo_V", "wo_E", "wo_O")
        for i in range(S // 512):
            if i > 0:
                nc.sync.wait_ge(sV, i)
            nc.sync.wait_ge(sL, i * 48)
            for g in range(2):
                nc.sync.dma_start(
                    out=ych[:, g * 8 * 512:(g + 1) * 8 * 512].rearrange("p (e s) -> p e s", e=8),
                    in_=self.yT.rearrange("e p s -> p e s")[:, g * 8:(g + 1) * 8,
                                                           i * 512:(i + 1) * 512]).then_inc(sL, 16)
            nc.sync.dma_start(out=xrch[:],
                              in_=self.zz[i, :, 16 * 512:32 * 512]).then_inc(sL, 16)
            nc.vector.wait_ge(sL, i * 48 + 48)
            if i > 0:
                nc.vector.wait_ge(sT, i * 8)
            nc.vector.tensor_mul(yx[:], ych[:], xrch[:]).then_inc(sV, 1)
            nc.tensor.wait_ge(sV, i + 1)
            for m in range(8):
                b = m % 4
                gm = i * 8 + m
                if gm >= 4:
                    nc.tensor.wait_ge(sE, gm - 3)
                for k in range(16):
                    mm = nc.tensor.matmul(pz[b][:], wwo[:, k * D + m * 128:k * D + (m + 1) * 128],
                                          yx[:, k * 512:(k + 1) * 512],
                                          start=(k == 0), stop=(k == 15))
                mm.then_inc(sT, 1)
                nc.scalar.wait_ge(sT, gm + 1)
                if i > 0 and m == 0:
                    nc.scalar.wait_ge(sO, i * 32)
                nc.scalar.activation(ostg[:, m * 512:(m + 1) * 512], pz[b][:], AF.Identity,
                                     bias=self.bc_b_out[:, m:m + 1], scale=1.0).then_inc(sE, 1)
            nc.scalar.wait_ge(sE, i * 8 + 8)
            nc.scalar.wait_ge(sO, i * 32)
            for g in range(2):
                nc.scalar.dma_start(
                    out=self.s0T.rearrange("e p s -> p e s")[:, g * 4:(g + 1) * 4,
                                                             i * 512:(i + 1) * 512],
                    in_=ostg[:, g * 4 * 512:(g + 1) * 4 * 512].rearrange("p (e s) -> p e s", e=4)).then_inc(sO, 16)
        nc.sync.wait_ge(sO, (S // 512) * 32)

    def ph_tern(self):
        nc = self.nc
        self._f_l = 0
        self._f_v = 0
        self._f_t = 0
        self._f_o = 0
        self._f_tr = 0
        tin = [self.sb("f_tin%d" % j, [128, 4096], F32) for j in range(2)]
        tstat = self.sb("f_ts", [128, 4], F32)
        absacc = self.sb("f_aa", [128, 1], F32)
        scb = self.sb("f_sc", [128, 1], F32)
        tern = self.sb("f_tb", [128, 4096], BF16)
        tstg = self.sb("f_tstg", [128, 32 * 128], BF16)
        ptr = [self.ps("f_pt%d" % j, [128, 128], BF16) for j in range(2)]
        psc = self.ps("f_psc", [128, 512], F32)
        sL, sV, sT, sE, sO = self.phase_sems("f_L", "f_V", "f_T", "f_E", "f_O")
        for wsel in range(2):
            ntile = 32 if wsel == 0 else 8
            wwid = 1024 if wsel == 0 else 4096
            srcw = self.fW1 if wsel == 0 else self.fW2
            dstw = self.W1t_b if wsel == 0 else self.W2t_b
            redv = {}
            ternv = {}
            nc.vector.wait_ge(sO, self._f_o * 16)
            nc.vector.memset(absacc[:], 0.0).then_inc(sV, 1)
            self._f_v += 1
            nc.vector.wait_ge(sV, self._f_v)
            for t in range(ntile):
                if t >= 2:
                    nc.sync.wait_ge(sV, redv[t - 2])
                else:
                    nc.sync.wait_ge(sV, self._f_v)
                nc.sync.wait_ge(sL, self._f_l * 16)
                nc.sync.dma_start(out=tin[t % 2][:, 0:wwid],
                                  in_=srcw[t * 128:(t + 1) * 128, :]).then_inc(sL, 16)
                self._f_l += 1
                nc.vector.wait_ge(sL, self._f_l * 16)
                nc.vector.wait_ge(sV, self._f_v)
                nc.vector.tensor_reduce(tstat[:, 0:1], tin[t % 2][:, 0:wwid], AX.X,
                                        OP.add, apply_absolute_value=True).then_inc(sV, 1)
                self._f_v += 1
                nc.vector.wait_ge(sV, self._f_v)
                nc.vector.tensor_add(absacc[:], absacc[:], tstat[:, 0:1]).then_inc(sV, 1)
                self._f_v += 1
                redv[t] = self._f_v
            nc.tensor.wait_ge(sV, self._f_v)
            nc.tensor.matmul(psc[0:1, 0:1], absacc[:], self.c_ones128c[:],
                             start=True, stop=True).then_inc(sT, 1)
            self._f_t += 1
            nc.vector.wait_ge(sT, self._f_t)
            nc.vector.tensor_scalar(tstat[0:1, 1:2], psc[0:1, 0:1],
                                    1.0 / (FF * D), 1e-8, OP.mult, OP.max).then_inc(sV, 1)
            self._f_v += 1
            nc.tensor.wait_ge(sV, self._f_v)
            nc.tensor.matmul(psc[0:128, 1:2], self.c_ones1[:], tstat[0:1, 1:2],
                             start=True, stop=True).then_inc(sT, 1)
            self._f_t += 1
            nc.vector.wait_ge(sT, self._f_t)
            nc.vector.tensor_copy(scb[:], psc[0:128, 1:2]).then_inc(sV, 1)
            self._f_v += 1
            nc.vector.wait_ge(sV, self._f_v)
            nc.vector.reciprocal(scb[:], scb[:]).then_inc(sV, 1)
            self._f_v += 1
            for t in range(ntile):
                if t >= 2:
                    nc.sync.wait_ge(sV, ternv[t - 2])
                else:
                    nc.sync.wait_ge(sV, self._f_v)
                nc.sync.wait_ge(sL, self._f_l * 16)
                nc.sync.dma_start(out=tin[t % 2][:, 0:wwid],
                                  in_=srcw[t * 128:(t + 1) * 128, :]).then_inc(sL, 16)
                self._f_l += 1
                nc.vector.wait_ge(sL, self._f_l * 16)
                nc.vector.wait_ge(sV, self._f_v)
                nc.vector.tensor_scalar(tin[t % 2][:, 0:wwid], tin[t % 2][:, 0:wwid],
                                        scb[:, 0:1], None, OP.mult).then_inc(sV, 1)
                self._f_v += 1
                nc.vector.wait_ge(sV, self._f_v)
                nc.vector.tensor_scalar(tin[t % 2][:, 0:wwid], tin[t % 2][:, 0:wwid],
                                        12582912.0, -12582912.0, OP.add, OP.add).then_inc(sV, 1)
                self._f_v += 1
                nc.vector.wait_ge(sV, self._f_v)
                nc.vector.wait_ge(sO, self._f_o * 16)
                nc.vector.tensor_scalar(tern[:, 0:wwid], tin[t % 2][:, 0:wwid],
                                        -1.0, 1.0, OP.max, OP.min).then_inc(sV, 1)
                self._f_v += 1
                ternv[t] = self._f_v
                ntr = wwid // 128
                nc.tensor.wait_ge(sV, self._f_v)
                base_tr = self._f_tr
                for d in range(ntr):
                    gidx = self._f_tr
                    if gidx >= 2:
                        nc.tensor.wait_ge(sE, gidx - 1)
                    nc.tensor.transpose(ptr[gidx % 2][:], tern[:, d * 128:(d + 1) * 128],
                                        self.c_identb[:]).then_inc(sT, 1)
                    self._f_t += 1
                    nc.vector.wait_ge(sT, self._f_t)
                    nc.vector.tensor_copy(tstg[:, d * 128:(d + 1) * 128],
                                          ptr[gidx % 2][:]).then_inc(sE, 1)
                    self._f_tr += 1
                nc.scalar.wait_ge(sE, self._f_tr)
                ng = max(1, ntr // 8)
                nc.scalar.wait_ge(sO, self._f_o * 16)
                for g in range(ng):
                    k1 = min((g + 1) * 8, ntr)
                    nc.scalar.dma_start(
                        out=dstw.rearrange("k p m -> p k m")[:, g * 8:k1, t * 128:(t + 1) * 128],
                        in_=tstg[:, g * 8 * 128:k1 * 128].rearrange("p (k m) -> p k m", m=128)).then_inc(sO, 16)
                    self._f_o += 1
        nc.vector.wait_ge(sO, self._f_o * 16)

    def ph_ffn0(self):
        nc = self.nc
        w1t = self.sb("f_w1", [128, 8 * FF], BF16)
        sW, = self.phase_sems("f_W")
        for g in range(2):
            nc.scalar.dma_start(out=w1t[:, g * 4 * FF:(g + 1) * 4 * FF].rearrange("p (k m) -> p k m", k=4),
                                in_=self.W1t_b.rearrange("k p m -> p k m")[:, g * 4:(g + 1) * 4, :]).then_inc(sW, 16)
        nc.tensor.wait_ge(sW, 32)
        xnb = self.sb("f_x", [128, 8 * 512], BF16)
        h1b = self.sb("f_h", [128, 32 * 512], BF16)
        w2g = [self.sb("f_w2%d" % j, [128, 8 * D], BF16) for j in range(2)]
        yac = self.sb("f_ya", [128, 8 * 512], F32)
        gstg = self.sb("f_g", [128, 8 * 512], F32)
        pz = [self.ps("f_p%d" % j, [128, 512], F32) for j in range(4)]
        p2 = [self.ps("f_q%d" % j, [128, 512], F32) for j in range(2)]
        sL2, sT2, sE2, sO2, sU = self.phase_sems("f_L2", "f_T2", "f_E2", "f_O2", "f_U")
        with self.fori(0, S // 512) as i:
            nc.sync.dma_start(out=xnb[:], in_=self.xnz[ts(i, 1), :, :]).then_inc(sL2, 16)
            nc.sync.dma_start(out=w2g[0][:].rearrange("p (k m) -> p k m", k=8),
                              in_=self.W2t_b.rearrange("k p m -> p k m")[:, 0:8, :]).then_inc(sU, 16)
            nc.tensor.wait_ge(sL2, 16)
            for m in range(32):
                b = m % 4
                if m >= 4:
                    nc.tensor.wait_ge(sE2, m - 3)
                for k in range(8):
                    mm = nc.tensor.matmul(pz[b][:], w1t[:, k * FF + m * 128:k * FF + (m + 1) * 128],
                                          xnb[:, k * 512:(k + 1) * 512],
                                          start=(k == 0), stop=(k == 7))
                mm.then_inc(sT2, 1)
                nc.scalar.wait_ge(sT2, m + 1)
                nc.scalar.activation(h1b[:, m * 512:(m + 1) * 512], pz[b][:], AF.Gelu,
                                     bias=self.bc_fb1[:, m:m + 1], scale=1.0).then_inc(sE2, 1)
            tcnt = 32
            sYs = self.sem("f_Ys")
            sYv = self.sem("f_Yv")
            sG = self.sem("f_G")
            evv = {}
            for kg in range(4):
                if kg < 3:
                    if kg >= 1:
                        nc.sync.wait_ge(sT2, 32 + kg * 8)
                    nc.sync.wait_ge(sU, (kg + 1) * 16)
                    nc.sync.dma_start(out=w2g[(kg + 1) % 2][:].rearrange("p (k m) -> p k m", k=8),
                                      in_=self.W2t_b.rearrange("k p m -> p k m")[:, (kg + 1) * 8:(kg + 2) * 8, :]).then_inc(sU, 16)
                nc.tensor.wait_ge(sU, kg * 16 + 16)
                nc.tensor.wait_ge(sE2, min(32, kg * 8 + 8))
                for m in range(8):
                    b = m % 2
                    gi = kg * 8 + m
                    if gi >= 2:
                        pk_, pv_ = evv[gi - 2]
                        if pk_ == 0:
                            nc.tensor.wait_ge(sYs, pv_)
                        else:
                            nc.tensor.wait_ge(sYv, pv_)
                    for k in range(8):
                        mm = nc.tensor.matmul(p2[b][:], w2g[kg % 2][:, k * D + m * 128:k * D + (m + 1) * 128],
                                              h1b[:, (kg * 8 + k) * 512:(kg * 8 + k + 1) * 512],
                                              start=(k == 0), stop=(k == 7))
                    mm.then_inc(sT2, 1)
                    tcnt += 1
                    if kg == 0:
                        nc.scalar.wait_ge(sT2, tcnt)
                        nc.scalar.copy(yac[:, m * 512:(m + 1) * 512], p2[b][:]).then_inc(sYs, 1)
                    else:
                        nc.vector.wait_ge(sT2, tcnt)
                        nc.vector.tensor_add(yac[:, m * 512:(m + 1) * 512],
                                             yac[:, m * 512:(m + 1) * 512], p2[b][:]).then_inc(sYv, 1)
                    evv[gi] = (0, m + 1) if kg == 0 else (1, (kg - 1) * 8 + m + 1)
            for m in range(8):
                nc.scalar.wait_ge(sYv, 16 + m + 1)
                nc.scalar.activation(gstg[:, m * 512:(m + 1) * 512], yac[:, m * 512:(m + 1) * 512],
                                     AF.Silu, bias=self.bc_fb2[:, m:m + 1], scale=1.0).then_inc(sG, 1)
            nc.scalar.wait_ge(sG, 8)
            nc.scalar.dma_start(out=self.g0z[ts(i, 1), :, :], in_=gstg[:]).then_inc(sO2, 16)
            nc.gpsimd.wait_ge(sO2, 16)
            self.clears(sL2, sT2, sE2, sO2, sU, sYs, sYv, sG)


    def ph_resid1(self):
        nc = self.nc
        s0c = self.sb("r_s", [128, 8 * 512], F32)
        g0c = self.sb("r_g", [128, 8 * 512], F32)
        psl = self.sb("r_p", [128, 8 * 512], F32)
        xtc = self.sb("r_x", [128, 4 * 1024], F32)
        ptok = self.sb("r_pt", [128, 4 * 1024], F32)
        x1c = self.sb("r_x1", [128, 4 * 1024], F32)
        xn1 = self.sb("r_xn", [128, 1024], BF16)
        xstg = self.sb("r_xs", [128, 8 * 512], BF16)
        dump = self.sb("r_du", [128, 2 * D], F32)
        stats = self.sb("r_st", [128, 8], F32)
        ptf = [self.ps("r_pf%d" % j, [128, 128], F32) for j in range(2)]
        ptb = [self.ps("r_pb%d" % j, [128, 128], BF16) for j in range(2)]
        sL, sV, sA, sTf, sEf, sTb, sEb, sO = self.phase_sems(
            "r_L", "r_V", "r_A", "r_Tf", "r_Ef", "r_Tb", "r_Eb", "r_O")
        ntrf = 0
        ntrb = 0
        for i in range(S // 512):
            if i > 0:
                nc.sync.wait_ge(sV, i * 21)
            nc.sync.wait_ge(sL, i * 48)
            nc.sync.dma_start(out=s0c[:].rearrange("p (e s) -> p e s", e=8),
                              in_=self.s0T.rearrange("e p s -> p e s")[:, :,
                                                                      i * 512:(i + 1) * 512]).then_inc(sL, 16)
            nc.sync.dma_start(out=g0c[:], in_=self.g0z[i, :, :]).then_inc(sL, 16)
            nc.sync.dma_start(out=xtc[:].rearrange("p (t d) -> p t d", t=4),
                              in_=self.x.rearrange("(n p) d -> p n d", p=128)[:, i * 4:(i + 1) * 4, :]).then_inc(sL, 16)
            nc.vector.wait_ge(sL, i * 48 + 48)
            nc.vector.tensor_mul(psl[:], s0c[:], g0c[:]).then_inc(sV, 1)
            vv = i * 21 + 1
            nc.tensor.wait_ge(sV, vv)
            for tt in range(4):
                for e in range(8):
                    if ntrf >= 2:
                        nc.tensor.wait_ge(sEf, ntrf - 1)
                    nc.tensor.transpose(ptf[ntrf % 2][:], psl[:, e * 512 + tt * 128:e * 512 + (tt + 1) * 128],
                                        self.c_identf[:]).then_inc(sTf, 1)
                    nc.scalar.wait_ge(sTf, ntrf + 1)
                    nc.scalar.copy(ptok[:, tt * 1024 + e * 128:tt * 1024 + (e + 1) * 128],
                                   ptf[ntrf % 2][:]).then_inc(sEf, 1)
                    ntrf += 1
            aa = i * 20
            for tt in range(4):
                nc.vector.wait_ge(sEf, i * 32 + (tt + 1) * 8)
                if i > 0 and tt == 0:
                    nc.vector.wait_ge(sO, i * 32)
                nc.vector.tensor_add(x1c[:, tt * 1024:(tt + 1) * 1024],
                                     xtc[:, tt * 1024:(tt + 1) * 1024],
                                     ptok[:, tt * 1024:(tt + 1) * 1024]).then_inc(sV, 1)
                vv += 1
                nc.scalar.wait_ge(sV, vv)
                if ntrb >= 8:
                    nc.scalar.wait_ge(sEb, ntrb)
                nc.vector.wait_ge(sV, vv)
                self._ln_tile(x1c[:, tt * 1024:(tt + 1) * 1024], xn1, dump, stats,
                              sV, sA, vv, aa)
                vv += 4
                aa += 5
                nc.tensor.wait_ge(sA, aa)
                for d in range(8):
                    if ntrb >= 2:
                        nc.tensor.wait_ge(sEb, ntrb - 1)
                    nc.tensor.transpose(ptb[ntrb % 2][:], xn1[:, d * 128:(d + 1) * 128],
                                        self.c_identb[:]).then_inc(sTb, 1)
                    nc.vector.wait_ge(sTb, ntrb + 1)
                    nc.vector.tensor_copy(xstg[:, d * 512 + tt * 128:d * 512 + (tt + 1) * 128],
                                          ptb[ntrb % 2][:]).then_inc(sEb, 1)
                    ntrb += 1
            nc.scalar.wait_ge(sEb, ntrb)
            nc.scalar.wait_ge(sO, i * 32)
            nc.scalar.dma_start(out=self.x1s.rearrange("(n p) d -> p n d", p=128)[:, i * 4:(i + 1) * 4, :],
                                in_=x1c[:].rearrange("p (t d) -> p t d", t=4)).then_inc(sO, 16)
            nc.scalar.dma_start(out=self.xn1z[i, :, :], in_=xstg[:]).then_inc(sO, 16)
        nc.sync.wait_ge(sO, (S // 512) * 32)

    def ph_qkvg(self):
        nc = self.nc
        wq = self.sb("q_wq", [128, 8 * KD], BF16)
        wk_ = self.sb("q_wk", [128, 8 * KD], BF16)
        wv = self.sb("q_wv", [128, 8 * VD], BF16)
        wg = self.sb("q_wg", [128, 8 * VD], BF16)
        wgk = self.sb("q_wgk", [128, 8 * NH], BF16)
        wbb = self.sb("q_wb", [128, 8 * NH], BF16)
        wr = self.sb("q_wr", [128, 8 * NEXP], BF16)
        sW, = self.phase_sems("q_W")
        nw = 0
        for (dst, srcw) in [(wq, self.Wq_b), (wk_, self.Wk_b), (wv, self.Wv_b),
                            (wg, self.Wg_b), (wgk, self.Wgk_b), (wbb, self.Wb_b),
                            (wr, self.Wr_b)]:
            kk = srcw.shape[2]
            nc.scalar.dma_start(out=dst[:].rearrange("p (k m) -> p k m", k=8),
                                in_=srcw.rearrange("k p m -> p k m")).then_inc(sW, 16)
            nw += 1
        nc.tensor.wait_ge(sW, nw * 16)
        xnb = self.sb("q_x", [128, 8 * 512], BF16)
        qkstg = self.sb("q_qk", [128, 12 * 512], BF16)
        vgstg = self.sb("q_vg", [128, 24 * 512], BF16)
        glstg = self.sb("q_gl", [32, 512], F32)
        bstg12 = self.sb("q_b12", [12, 512], F32)
        lstg8 = self.sb("q_l8", [8, 512], F32)
        pz = [self.ps("q_p%d" % j, [128, 512], F32) for j in range(2)]
        pgk = self.ps("q_pgk", [12, 512], F32)
        pbt = self.ps("q_pbt", [12, 512], F32)
        plg = self.ps("q_plg", [8, 512], F32)
        sL, sT, sE, sV, sA, sO = self.phase_sems("q_L", "q_T", "q_E", "q_V", "q_A", "q_O")
        with self.fori(0, S // 512) as i:
            nc.sync.dma_start(out=xnb[:], in_=self.xn1z[ts(i, 1), :, :]).then_inc(sL, 16)
            nc.tensor.wait_ge(sL, 16)
            grp = 0
            plan = ([("q", wq, mt, qkstg, mt, AF.Identity) for mt in range(6)] +
                    [("k", wk_, mt, qkstg, 6 + mt, AF.Identity) for mt in range(6)] +
                    [("v", wv, mt, vgstg, mt, AF.Identity) for mt in range(12)] +
                    [("g", wg, mt, vgstg, 12 + mt, AF.Silu) for mt in range(12)])
            for (_nm, wsrc, mt, dstg, dcol, fn) in plan:
                b = grp % 2
                if grp >= 2:
                    nc.tensor.wait_ge(sE, grp - 1)
                wid = wsrc.shape[1] // 8
                for k in range(8):
                    mm = nc.tensor.matmul(pz[b][:], wsrc[:, k * wid + mt * 128:k * wid + (mt + 1) * 128],
                                          xnb[:, k * 512:(k + 1) * 512],
                                          start=(k == 0), stop=(k == 7))
                mm.then_inc(sT, 1)
                grp += 1
                nc.scalar.wait_ge(sT, grp)
                nc.scalar.activation(dstg[:, dcol * 512:(dcol + 1) * 512], pz[b][:], fn,
                                     bias=0.0, scale=1.0).then_inc(sE, 1)
            for k in range(8):
                mm = nc.tensor.matmul(pgk[:], wgk[:, k * NH:(k + 1) * NH], xnb[:, k * 512:(k + 1) * 512],
                                      start=(k == 0), stop=(k == 7))
            mm.then_inc(sT, 1)
            for k in range(8):
                mm = nc.tensor.matmul(pbt[:], wbb[:, k * NH:(k + 1) * NH], xnb[:, k * 512:(k + 1) * 512],
                                      start=(k == 0), stop=(k == 7))
            mm.then_inc(sT, 1)
            for k in range(8):
                mm = nc.tensor.matmul(plg[:], wr[:, k * NEXP:(k + 1) * NEXP], xnb[:, k * 512:(k + 1) * 512],
                                      start=(k == 0), stop=(k == 7))
            mm.then_inc(sT, 1)
            nc.vector.wait_ge(sT, 37)
            nc.vector.tensor_scalar(glstg[0:12, :], pgk[:], self.r_dtb[:, 0:1], 80.0,
                                    OP.add, OP.min).then_inc(sV, 1)
            nc.scalar.wait_ge(sV, 1)
            nc.scalar.activation(glstg[0:12, :], glstg[0:12, :], AF.Exp, bias=0.0, scale=1.0).then_inc(sA, 1)
            nc.scalar.wait_ge(sA, 1)
            nc.scalar.activation(glstg[0:12, :], glstg[0:12, :], AF.Ln, bias=1.0, scale=1.0).then_inc(sA, 1)
            nc.vector.wait_ge(sA, 2)
            nc.vector.tensor_scalar(glstg[0:12, :], glstg[0:12, :], self.r_negA[:, 0:1], None,
                                    OP.mult).then_inc(sV, 1)
            nc.scalar.wait_ge(sT, 38)
            nc.scalar.activation(bstg12[:], pbt[:], AF.Sigmoid,
                                 bias=self.r_bb[:, 0:1], scale=1.0).then_inc(sA, 1)
            nc.scalar.wait_ge(sA, 3)
            nc.scalar.dma_start(out=glstg[12:24, :], in_=bstg12[:]).then_inc(sA, 16)
            nc.vector.wait_ge(sT, 39)
            nc.vector.tensor_copy(lstg8[:], plg[:]).then_inc(sV, 1)
            nc.scalar.wait_ge(sV, 3)
            nc.scalar.dma_start(out=glstg[24:32, :], in_=lstg8[:]).then_inc(sA, 16)
            nc.scalar.wait_ge(sE, 36)
            nc.scalar.wait_ge(sV, 3)
            nc.scalar.wait_ge(sA, 35)
            nc.scalar.dma_start(out=self.qkrz[ts(i, 1), :, :], in_=qkstg[:]).then_inc(sO, 16)
            nc.scalar.dma_start(out=self.vrz[ts(i, 1), :, :],
                                in_=vgstg[:, 0:12 * 512]).then_inc(sO, 16)
            nc.scalar.dma_start(out=self.vgz[ts(i, 1), :, 12 * 512:24 * 512],
                                in_=vgstg[:, 12 * 512:24 * 512]).then_inc(sO, 16)
            nc.scalar.dma_start(out=self.gl_d[:, ts(i, 512)], in_=glstg[:]).then_inc(sO, 16)
            nc.gpsimd.wait_ge(sO, 4 * 16)
            self.clears(sL, sT, sE, sV, sA, sO)

    def ph_convq(self):
        nc = self.nc
        wk2 = self.sb("cq_wk", [128, S + 3], BF16)
        a1 = self.sb("cq_a1", [128, S], F32)
        a2 = self.sb("cq_a2", [128, S], F32)
        q2 = self.sb("cq_q2", [128, S], F32)
        rbuf = self.sb("cq_r", [2, 512], F32)
        ostg = self.sb("cq_o", [128, S], BF16)
        psq = [self.ps("cq_ps%d" % j, [2, 512], F32) for j in range(2)]
        pbc = [self.ps("cq_pb%d" % j, [128, 512], F32) for j in range(2)]
        sL, sV, sA, sT, sO = self.phase_sems("cq_L", "cq_V", "cq_A", "cq_T", "cq_O")
        nc.vector.memset(wk2[:, 0:3], 0.0).then_inc(sV, 1)
        nc.vector.wait_ge(sV, 1)
        vv, aa, tt, oo, ll = 1, 0, 0, 0, 0
        for tile in range(24):
            is_v = tile >= 12
            if is_v:
                srcz, srccol = self.vrz, tile - 12
                cw, ci = self.bc_cwv, tile - 12
            elif tile < 6:
                srcz, srccol = self.qkrz, tile
                cw, ci = self.bc_cwq, tile
            else:
                srcz, srccol = self.qkrz, tile
                cw, ci = self.bc_cwk, tile - 6
            if tile > 0:
                nc.sync.wait_ge(sV, self._cq_tapv)
            nc.sync.wait_ge(sL, ll * 16)
            half = (S // 512) // 2
            for g in range(2):
                nc.sync.dma_start(
                    out=wk2[:, 3:S + 3].rearrange("p (c s) -> p c s", c=S // 512)[:, g * half:(g + 1) * half, :],
                    in_=srcz.rearrange("c p m -> p c m")[:, g * half:(g + 1) * half,
                                                         srccol * 512:(srccol + 1) * 512]).then_inc(sL, 16)
            ll += 2
            nc.vector.wait_ge(sL, ll * 16)
            nc.vector.wait_ge(sV, vv)
            if tile > 0:
                nc.vector.wait_ge(sA, aa)
            nc.vector.tensor_scalar(a1[:], wk2[:, 0:S], cw[:, 4 * ci:4 * ci + 1], None,
                                    OP.mult).then_inc(sV, 1)
            vv += 1
            nc.vector.wait_ge(sV, vv)
            nc.vector.scalar_tensor_tensor(a2[:], wk2[:, 1:S + 1], cw[:, 4 * ci + 1:4 * ci + 2],
                                           a1[:], OP.mult, OP.add).then_inc(sV, 1)
            vv += 1
            nc.vector.wait_ge(sV, vv)
            nc.vector.scalar_tensor_tensor(a1[:], wk2[:, 2:S + 2], cw[:, 4 * ci + 2:4 * ci + 3],
                                           a2[:], OP.mult, OP.add).then_inc(sV, 1)
            vv += 1
            nc.vector.wait_ge(sV, vv)
            nc.vector.scalar_tensor_tensor(a2[:], wk2[:, 3:S + 3], cw[:, 4 * ci + 3:4 * ci + 4],
                                           a1[:], OP.mult, OP.add).then_inc(sV, 1)
            vv += 1
            self._cq_tapv = vv
            nc.scalar.wait_ge(sV, vv)
            if tile > 0:
                nc.scalar.wait_ge(sO, oo * 16)
            if is_v:
                nc.scalar.activation(ostg[:], a2[:], AF.Silu, bias=0.0, scale=1.0).then_inc(sA, 1)
                aa += 1
                nc.scalar.wait_ge(sA, aa)
                nc.scalar.wait_ge(sO, oo * 16)
                half = (S // 512) // 2
                for g in range(2):
                    nc.scalar.dma_start(
                        out=self.vgz.rearrange("c p m -> p c m")[:, g * half:(g + 1) * half,
                                                                 (tile - 12) * 512:(tile - 11) * 512],
                        in_=ostg[:].rearrange("p (c s) -> p c s", c=S // 512)[:, g * half:(g + 1) * half, :]).then_inc(sO, 16)
                    oo += 1
            else:
                nc.scalar.activation(a1[:], a2[:], AF.Silu, bias=0.0, scale=1.0).then_inc(sA, 1)
                aa += 1
                nc.scalar.wait_ge(sA, aa)
                nc.scalar.activation(q2[:], a1[:], AF.Square, bias=0.0, scale=1.0).then_inc(sA, 1)
                aa += 1
                nc.tensor.wait_ge(sA, aa)
                nc.vector.wait_ge(sO, oo * 16)
                scl = 0.125 if tile < 6 else 1.0
                for sub in range(S // 512):
                    ssl = slice(sub * 512, (sub + 1) * 512)
                    bq = sub % 2
                    if sub >= 2:
                        nc.tensor.wait_ge(sV, self._cq_r1[sub - 2])
                    nc.tensor.matmul(psq[bq][:], self.c_sel2[:], q2[:, ssl],
                                     start=True, stop=True).then_inc(sT, 1)
                    tt += 1
                    nc.vector.wait_ge(sT, tt)
                    nc.vector.tensor_scalar(rbuf[:], psq[bq][:], 1.0, 1e-6,
                                            OP.mult, OP.add).then_inc(sV, 1)
                    vv += 1
                    self._cq_r1[sub] = vv
                    nc.scalar.wait_ge(sV, vv)
                    nc.scalar.sqrt(rbuf[:], rbuf[:]).then_inc(sA, 1)
                    aa += 1
                    nc.vector.wait_ge(sA, aa)
                    nc.vector.reciprocal(rbuf[:], rbuf[:]).then_inc(sV, 1)
                    vv += 1
                    nc.tensor.wait_ge(sV, vv)
                    if sub >= 2:
                        nc.tensor.wait_ge(sV, self._cq_qh[sub - 2])
                    nc.tensor.matmul(pbc[bq][:], self.c_bc2[:], rbuf[:],
                                     start=True, stop=True).then_inc(sT, 1)
                    tt += 1
                    nc.vector.wait_ge(sT, tt)
                    nc.vector.scalar_tensor_tensor(ostg[:, ssl], a1[:, ssl], scl,
                                                   pbc[bq][:], OP.mult, OP.mult).then_inc(sV, 1)
                    vv += 1
                    self._cq_qh[sub] = vv
                nc.scalar.wait_ge(sV, vv)
                nc.scalar.wait_ge(sO, oo * 16)
                half = (S // 512) // 2
                for g in range(2):
                    nc.scalar.dma_start(
                        out=self.qkhz.rearrange("c p m -> p c m")[:, g * half:(g + 1) * half,
                                                                  tile * 512:(tile + 1) * 512],
                        in_=ostg[:].rearrange("p (c s) -> p c s", c=S // 512)[:, g * half:(g + 1) * half, :]).then_inc(sO, 16)
                    oo += 1
        nc.sync.wait_ge(sO, oo * 16)

    def ph_delta(self):
        nc = self.nc
        woB = self.sb("d_wo", [128, 12 * D], BF16)
        onesg = self.sb("d_one", [12, TC], F32)
        sW, = self.phase_sems("d_W")
        for g in range(2):
            nc.scalar.dma_start(out=woB[:, g * 6 * D:(g + 1) * 6 * D].rearrange("p (k m) -> p k m", k=6),
                                in_=self.Wo_b.rearrange("k p m -> p k m")[:, g * 6:(g + 1) * 6, :]).then_inc(sW, 16)
        nc.vector.memset(onesg[:], 1.0)
        nc.vector.memset(self.Sst[:], 0.0).then_inc(sW, 1)
        nc.tensor.wait_ge(sW, 33)
        nc.gpsimd.wait_ge(sW, 33)
        self.bar()
        gkb = self.sb("d_gkb", [24, 512], F32)
        gkbb = self.sb("d_gkbb", [12, 512], F32)
        qkc = self.sb("d_qk", [128, 12 * 512], BF16)
        vgc = self.sb("d_vg", [128, 24 * 512], BF16)
        gam = self.sb("d_gam", [12, TC], F32)
        ngam = self.sb("d_ng", [12, TC], F32)
        dec = self.sb("d_dec", [12, TC], F32)
        expdec = self.sb("d_ed", [12, TC], F32)
        lnb = self.sb("d_lnb", [12, TC], F32)
        expg = self.sb("d_eg", [12, TC], F32)
        gTb = self.sb("d_gT", [128, 12], F32)
        bTb = self.sb("d_bT", [128, 12], F32)
        e1b = self.sb("d_e1", [128, TC], F32)
        e2b = self.sb("d_e2", [128, TC], F32)
        Wb = self.sb("d_Wb", [128, TC], BF16)
        NTq = [self.sb("d_nt%d" % j, [128, TC], F32) for j in range(2)]
        Tq = self.sb("d_tq", [128, TC], F32)
        Uj = [self.sb("d_u%d" % j, [128, DVH], F32) for j in range(2)]
        Ub = self.sb("d_ub", [128, DVH], BF16)
        Kb = self.sb("d_kb", [64, TC], BF16)
        Qb = self.sb("d_qb", [64, TC], BF16)
        S0b = self.sb("d_s0", [64, DVH], BF16)
        Vt = self.sb("d_vt", [128, DVH], BF16)
        dcs = self.sb("d_dc", [128, 1], F32)
        eGb = self.sb("d_egb", [128, 1], F32)
        KtT = self.sb("d_kt", [128, 64], BF16)
        o2b = self.sb("d_o2", [128, TC], F32)
        osb = self.sb("d_os", [128, TC], BF16)
        onb = self.sb("d_on", [128, TC], F32)
        rb2 = self.sb("d_rb", [1, TC], F32)
        ostage = self.sb("d_ost", [128, 12 * TC], BF16)
        s1stg = self.sb("d_s1", [128, 8 * 512], F32)
        pk = [self.ps("d_p%d" % j, [128, 512], F32) for j in range(6)]
        pX = self.ps("d_px", [128, 512], BF16)
        pY = self.ps("d_py", [128, 512], F32)
        sL, sT, sV, sA, sO = self.phase_sems("d_L", "d_T", "d_V", "d_A", "d_O")
        PA, PB, PC, PD, PE, PF = range(6)

        with self.fori(0, S // 512) as c:
            bk = {}

            def guard(bank):
                if bank in bk:
                    sem, val = bk[bank]
                    nc.tensor.wait_ge(sem, val)

            nc.sync.dma_start(out=gkb[:], in_=self.gl_d[0:24, ts(c, 512)]).then_inc(sL, 16)
            nc.sync.dma_start(out=qkc[:], in_=self.qkhz[ts(c, 1), :, :]).then_inc(sL, 16)
            nc.scalar.dma_start(out=vgc[:], in_=self.vgz[ts(c, 1), :, :]).then_inc(sLA2 := self.sem("d_LA"), 16)
            nc.scalar.wait_ge(sL, 32)
            nc.scalar.dma_start(out=gkbb[0:12, :], in_=gkb[12:24, :]).then_inc(sLA2, 16)
            vv = tt = aa = 0
            nc.vector.wait_ge(sL, 32)
            nc.scalar.wait_ge(sL, 32)
            nc.tensor.wait_ge(sL, 32)
            nc.vector.wait_ge(sLA2, 32)
            nc.scalar.wait_ge(sLA2, 32)
            nc.tensor.wait_ge(sLA2, 32)
            for cb in range(4):
              csl = slice(cb * 128, (cb + 1) * 128)
              if True:
                nc.vector.wait_ge(sT, tt)
                nc.vector.tensor_tensor_scan(gam[:], onesg[:], gkb[0:12, csl], 0.0,
                                             OP.mult, OP.add).then_inc(sV, 1)
              vv += 1
              nc.vector.wait_ge(sV, vv)
              nc.vector.tensor_scalar_mul(ngam[:], gam[:], -1.0).then_inc(sV, 1)
              vv += 1
              nc.vector.wait_ge(sV, vv)
              nc.vector.tensor_scalar(dec[:], gam[:], gam[:, TC - 1:TC], None,
                                      OP.subtract).then_inc(sV, 1)
              vv += 1
              nc.scalar.wait_ge(sV, vv)
              nc.scalar.activation(expdec[:], dec[:], AF.Exp, bias=0.0, scale=-1.0).then_inc(sA, 1)
              nc.scalar.activation(lnb[:], gkbb[0:12, csl], AF.Ln, bias=0.0, scale=1.0).then_inc(sA, 1)
              nc.scalar.activation(expg[:], gam[:], AF.Exp, bias=0.0, scale=1.0).then_inc(sA, 1)
              aa += 3
              nc.tensor.wait_ge(sV, vv)
              nc.tensor.transpose(pY[0:128, 0:12], gam[:], self.c_identf[0:12, 0:12]).then_inc(sT, 1)
              tt += 1
              nc.scalar.wait_ge(sT, tt)
              nc.scalar.copy(gTb[:], pY[0:128, 0:12]).then_inc(sA, 1)
              aa += 1
              nc.tensor.wait_ge(sA, aa)
              nc.tensor.transpose(pY[0:128, 0:12], gkbb[0:12, csl], self.c_identf[0:12, 0:12]).then_inc(sT, 1)
              tt += 1
              nc.scalar.wait_ge(sT, tt)
              nc.scalar.copy(bTb[:], pY[0:128, 0:12]).then_inc(sA, 1)
              aa += 1
              bk_pY = ("sA", aa)
              for h in range(12):
                  oh = self.c_oneh12[:, h * 128:(h + 1) * 128]
                  rq = (h % 2) * 64
                  qt = (h // 2)
                  qh = qkc[rq:rq + 64, qt * 512 + cb * 128:qt * 512 + (cb + 1) * 128]
                  kh = qkc[rq:rq + 64, (qt + 6) * 512 + cb * 128:(qt + 6) * 512 + (cb + 1) * 128]
                  vh = vgc[:, h * 512 + cb * 128:h * 512 + (cb + 1) * 128]
                  gh = vgc[:, (12 + h) * 512 + cb * 128:(12 + h) * 512 + (cb + 1) * 128]
                  guard(PA)
                  nc.tensor.matmul(pk[PA][:, 0:TC], oh, gam[:], start=True, stop=False)
                  nc.tensor.matmul(pk[PA][:, 0:TC], ngam[:], oh, start=False, stop=False)
                  nc.tensor.matmul(pk[PA][:, 0:TC], self.c_negtI[:], self.c_identf[:],
                                   start=False, stop=True).then_inc(sT, 1)
                  tt += 1
                  nc.scalar.wait_ge(sT, tt)
                  nc.scalar.activation(e1b[:], pk[PA][:, 0:TC], AF.Exp, bias=0.0, scale=1.0).then_inc(sA, 1)
                  aa += 1
                  bk[PA] = (sA, aa)
                  guard(PB)
                  nc.tensor.matmul(pk[PB][:, 0:TC], oh, gam[:], start=True, stop=False)
                  nc.tensor.matmul(pk[PB][:, 0:TC], ngam[:], oh, start=False, stop=False)
                  nc.tensor.matmul(pk[PB][:, 0:TC], oh, lnb[:], start=False, stop=False)
                  nc.tensor.matmul(pk[PB][:, 0:TC], self.c_negtS[:], self.c_identf[:],
                                   start=False, stop=True).then_inc(sT, 1)
                  tt += 1
                  nc.scalar.wait_ge(sT, tt)
                  nc.scalar.activation(e2b[:], pk[PB][:, 0:TC], AF.Exp, bias=0.0, scale=1.0).then_inc(sA, 1)
                  aa += 1
                  bk[PB] = (sA, aa)
                  guard(PC)
                  nc.tensor.matmul(pk[PC][:, 0:TC], kh, qh, start=True, stop=True).then_inc(sT, 1)
                  tt += 1
                  guard(PD)
                  nc.tensor.matmul(pk[PD][:, 0:TC], kh, kh, start=True, stop=True).then_inc(sT, 1)
                  tt += 1
                  nc.vector.wait_ge(sT, tt)
                  nc.vector.wait_ge(sA, aa)
                  nc.vector.tensor_mul(Wb[:], pk[PC][:, 0:TC], e1b[:]).then_inc(sV, 1)
                  vv += 1
                  bk[PC] = (sV, vv)
                  nc.vector.wait_ge(sV, vv)
                  nc.vector.scalar_tensor_tensor(NTq[0][:], pk[PD][:, 0:TC], -1.0, e2b[:],
                                                 OP.mult, OP.mult).then_inc(sV, 1)
                  vv += 1
                  bk[PD] = (sV, vv)
                  guard(PE)
                  nc.tensor.matmul(pk[PE][:, 0:TC], oh, expg[:], start=True, stop=True).then_inc(sT, 1)
                  tt += 1
                  nc.vector.wait_ge(sT, tt)
                  nc.vector.tensor_mul(Kb[0:64, :], kh, pk[PE][rq:rq + 64, 0:TC]).then_inc(sV, 1)
                  vv += 1
                  nc.vector.wait_ge(sV, vv)
                  nc.vector.tensor_mul(Qb[0:64, :], qh, pk[PE][rq:rq + 64, 0:TC]).then_inc(sV, 1)
                  vv += 1
                  bk[PE] = (sV, vv)
                  nc.scalar.wait_ge(sV, vv)
                  nc.scalar.copy(S0b[:], self.Sst[:, h * DVH:(h + 1) * DVH]).then_inc(sA, 1)
                  aa += 1
                  guard(PF)
                  nc.tensor.wait_ge(sV, vv)
                  nc.tensor.wait_ge(sA, aa)
                  nc.tensor.matmul(pk[PF][:, 0:DVH], Kb[0:64, :], S0b[:], start=True, stop=True).then_inc(sT, 1)
                  tt += 1
                  nc.tensor.transpose(pX[:, 0:TC], vh, self.c_identb[:]).then_inc(sT, 1)
                  tt += 1
                  nc.scalar.wait_ge(sT, tt)
                  nc.scalar.copy(Vt[:], pX[:, 0:TC]).then_inc(sA, 1)
                  aa += 1
                  nc.vector.wait_ge(sT, tt)
                  nc.vector.wait_ge(sA, aa)
                  nc.vector.tensor_sub(Uj[0][:], Vt[:], pk[PF][:, 0:DVH]).then_inc(sV, 1)
                  vv += 1
                  bk[PF] = (sV, vv)
                  nc.vector.wait_ge(sV, vv)
                  nc.vector.tensor_scalar(Uj[0][:], Uj[0][:], bTb[:, h:h + 1], None,
                                          OP.mult).then_inc(sV, 1)
                  vv += 1
                  upar = 0
                  qpar = 0
                  for j in range(7):
                      nc.tensor.wait_ge(sV, vv)
                      nc.tensor.wait_ge(sA, aa)
                      nc.tensor.matmul(pY[:, 0:DVH], NTq[qpar][:], Uj[upar][:],
                                       start=True, stop=False)
                      nc.tensor.matmul(pY[:, 0:DVH], self.c_identf[:], Uj[upar][:],
                                       start=False, stop=True).then_inc(sT, 1)
                      tt += 1
                      nc.scalar.wait_ge(sT, tt)
                      nc.scalar.copy(Uj[1 - upar][:], pY[:, 0:DVH]).then_inc(sA, 1)
                      aa += 1
                      upar = 1 - upar
                      if j < 6:
                          nc.tensor.wait_ge(sA, aa)
                          nc.tensor.transpose(pY[:, 0:TC], NTq[qpar][:], self.c_identf[:]).then_inc(sT, 1)
                          tt += 1
                          nc.scalar.wait_ge(sT, tt)
                          nc.scalar.copy(Tq[:], pY[:, 0:TC]).then_inc(sA, 1)
                          aa += 1
                          nc.tensor.wait_ge(sA, aa)
                          nc.tensor.matmul(pY[:, 0:TC], Tq[:], NTq[qpar][:],
                                           start=True, stop=True).then_inc(sT, 1)
                          tt += 1
                          nc.scalar.wait_ge(sT, tt)
                          nc.scalar.copy(NTq[1 - qpar][:], pY[:, 0:TC]).then_inc(sA, 1)
                          aa += 1
                          qpar = 1 - qpar
                  nc.vector.wait_ge(sA, aa)
                  nc.vector.tensor_copy(Ub[:], Uj[upar][:]).then_inc(sV, 1)
                  vv += 1
                  guard(PA)
                  nc.tensor.wait_ge(sV, vv)
                  nc.tensor.matmul(pk[PA][:, 0:TC], Ub[:], Wb[:], start=True, stop=False)
                  nc.tensor.matmul(pk[PA][:, 0:TC], S0b[:], Qb[0:64, :],
                                   start=False, stop=True).then_inc(sT, 1)
                  tt += 1
                  nc.scalar.wait_ge(sT, tt)
                  nc.scalar.activation(o2b[:], pk[PA][:, 0:TC], AF.Square, bias=0.0, scale=1.0).then_inc(sA, 1)
                  aa += 1
                  nc.scalar.wait_ge(sA, aa)
                  nc.scalar.copy(osb[:], pk[PA][:, 0:TC]).then_inc(sA, 1)
                  aa += 1
                  bk[PA] = (sA, aa)
                  guard(PB)
                  nc.tensor.wait_ge(sA, aa)
                  nc.tensor.matmul(pk[PB][0:1, 0:TC], self.c_ones128c[:], o2b[:],
                                   start=True, stop=True).then_inc(sT, 1)
                  tt += 1
                  nc.vector.wait_ge(sT, tt)
                  nc.vector.tensor_scalar(rb2[:], pk[PB][0:1, 0:TC], 1.0 / DVH, 1e-5,
                                          OP.mult, OP.add).then_inc(sV, 1)
                  vv += 1
                  bk[PB] = (sV, vv)
                  nc.scalar.wait_ge(sV, vv)
                  nc.scalar.sqrt(rb2[:], rb2[:]).then_inc(sA, 1)
                  aa += 1
                  nc.vector.wait_ge(sA, aa)
                  nc.vector.reciprocal(rb2[:], rb2[:]).then_inc(sV, 1)
                  vv += 1
                  guard(PC)
                  nc.tensor.wait_ge(sV, vv)
                  nc.tensor.matmul(pk[PC][:, 0:TC], self.c_ones1[:], rb2[:],
                                   start=True, stop=True).then_inc(sT, 1)
                  tt += 1
                  nc.vector.wait_ge(sT, tt)
                  nc.vector.scalar_tensor_tensor(onb[:], osb[:], self.bc_gnw[:, 0:1],
                                                 pk[PC][:, 0:TC], OP.mult, OP.mult).then_inc(sV, 1)
                  vv += 1
                  bk[PC] = (sV, vv)
                  nc.vector.wait_ge(sV, vv)
                  nc.vector.tensor_mul(ostage[:, h * TC:(h + 1) * TC], onb[:], gh).then_inc(sV, 1)
                  vv += 1
                  guard(PD)
                  nc.tensor.matmul(pk[PD][:, 0:1], expdec[:], self.c_eye12[:, h:h + 1],
                                   start=True, stop=True).then_inc(sT, 1)
                  tt += 1
                  nc.scalar.wait_ge(sT, tt)
                  nc.scalar.copy(dcs[:], pk[PD][:, 0:1]).then_inc(sA, 1)
                  aa += 1
                  bk[PD] = (sA, aa)
                  nc.tensor.wait_ge(sA, aa)
                  nc.tensor.transpose(pX[:, 0:64], kh, self.c_identb[rq:rq + 64, rq:rq + 64]).then_inc(sT, 1)
                  tt += 1
                  nc.vector.wait_ge(sT, tt)
                  nc.vector.tensor_scalar(KtT[:], pX[:, 0:64], dcs[:, 0:1], None,
                                          OP.mult).then_inc(sV, 1)
                  vv += 1
                  guard(PF)
                  nc.tensor.wait_ge(sV, vv)
                  nc.tensor.matmul(pk[PF][0:64, 0:DVH], KtT[:], Ub[:], start=True, stop=True).then_inc(sT, 1)
                  tt += 1
                  guard(PE)
                  nc.tensor.matmul(pk[PE][:, 0:1], oh, expg[:, TC - 1:TC], start=True, stop=True).then_inc(sT, 1)
                  tt += 1
                  nc.scalar.wait_ge(sT, tt)
                  nc.scalar.copy(eGb[:], pk[PE][:, 0:1]).then_inc(sA, 1)
                  aa += 1
                  bk[PE] = (sA, aa)
                  nc.vector.wait_ge(sT, tt)
                  nc.vector.wait_ge(sA, aa)
                  nc.vector.scalar_tensor_tensor(self.Sst[:, h * DVH:(h + 1) * DVH],
                                                 self.Sst[:, h * DVH:(h + 1) * DVH],
                                                 eGb[0:64, 0:1], pk[PF][0:64, 0:DVH],
                                                 OP.mult, OP.add).then_inc(sV, 1)
                  vv += 1
                  bk[PF] = (sV, vv)
              nc.vector.wait_ge(sV, vv)
              wop = 0
              for m in range(8):
                  b = PA if m % 2 == 0 else PB
                  guard(b)
                  for k in range(12):
                      mm = nc.tensor.matmul(pk[b][:, 0:TC], woB[:, k * D + m * 128:k * D + (m + 1) * 128],
                                            ostage[:, k * TC:(k + 1) * TC],
                                            start=(k == 0), stop=(k == 11))
                  mm.then_inc(sT, 1)
                  tt += 1
                  nc.scalar.wait_ge(sT, tt)
                  nc.scalar.copy(s1stg[:, m * 512 + cb * 128:m * 512 + (cb + 1) * 128], pk[b][:, 0:TC]).then_inc(sA, 1)
                  aa += 1
                  bk[b] = (sA, aa)
            nc.scalar.wait_ge(sA, aa)
            nc.scalar.dma_start(out=self.s1z[ts(c, 1), :, :], in_=s1stg[:]).then_inc(sO, 16)
            nc.gpsimd.wait_ge(sO, 16)
            self.clears(sL, self.sem("d_LA"), sT, sV, sA, sO)

    def ph_route(self):
        nc = self.nc
        lg = self.sb("rt_lg", [8, 128], F32)
        pr = self.sb("rt_pr", [128, 8], F32)
        ex = self.sb("rt_ex", [128, 8], F32)
        t8 = self.sb("rt_t8", [128, 8], F32)
        msk = self.sb("rt_m", [128, 8 * 4], F32)
        dmp = self.sb("rt_d", [128, 8], F32)
        st_ = self.sb("rt_s", [128, 8], F32)
        ones8 = self.sb("rt_o8", [128, 8], F32)
        wes = self.sb("rt_we", [128, 8], F32)
        wstg = self.sb("rt_ws", [8, 128], F32)
        pR = self.ps("rt_p", [128, 512], F32)
        sL, sV, sA, sT, sO = self.phase_sems("rt_L", "rt_V", "rt_A", "rt_T", "rt_O")
        nc.vector.memset(ones8[:], 1.0).then_inc(sV, 1)
        NV = 24
        for i in range(S // 128):
            vv = 1 + i * NV
            aa = i * 2
            tb = i * 2
            if i > 0:
                nc.sync.wait_ge(sA, aa)
            nc.sync.wait_ge(sL, i * 16)
            nc.sync.dma_start(out=lg[:], in_=self.gl_d[24:32, i * 128:(i + 1) * 128]).then_inc(sL, 16)
            nc.tensor.wait_ge(sL, i * 16 + 16)
            if i > 0:
                nc.tensor.wait_ge(sA, aa - 1)
                nc.tensor.wait_ge(sV, vv)
            nc.tensor.transpose(pR[:, 0:8], lg[:], self.c_identf[0:8, 0:8]).then_inc(sT, 1)
            nc.scalar.wait_ge(sT, tb + 1)
            nc.scalar.copy(pr[:], pR[:, 0:8]).then_inc(sA, 1)
            aa += 1
            nc.vector.wait_ge(sA, aa)
            nc.vector.reduce_max(st_[:, 0:1], pr[:], AX.X, negate=True).then_inc(sV, 1)
            vv += 1
            nc.scalar.wait_ge(sV, vv)
            nc.scalar.activation(ex[:], pr[:], AF.Exp, bias=st_[:, 0:1], scale=1.0,
                                 accum_out=st_[:, 1:2]).then_inc(sA, 1)
            aa += 1
            nc.vector.wait_ge(sA, aa)
            nc.vector.reciprocal(st_[:, 2:3], st_[:, 1:2]).then_inc(sV, 1)
            vv += 1
            nc.vector.wait_ge(sV, vv)
            nc.vector.tensor_scalar(pr[:], ex[:], st_[:, 2:3], None, OP.mult).then_inc(sV, 1)
            vv += 1
            nc.vector.wait_ge(sV, vv)
            nc.vector.max(t8[:], pr[:]).then_inc(sV, 1)
            vv += 1
            nc.vector.wait_ge(sV, vv)
            nc.vector.tensor_scalar(msk[:, 0:8], pr[:], t8[:, 0:1], None, OP.is_equal).then_inc(sV, 1)
            vv += 1
            nc.vector.wait_ge(sV, vv)
            nc.vector.tensor_tensor_scan(dmp[:], ones8[:], msk[:, 0:8], 0.0,
                                         OP.mult, OP.add).then_inc(sV, 1)
            vv += 1
            nc.vector.wait_ge(sV, vv)
            nc.vector.tensor_scalar(dmp[:], dmp[:], 1.0, None, OP.is_equal).then_inc(sV, 1)
            vv += 1
            nc.vector.wait_ge(sV, vv)
            nc.vector.tensor_mul(msk[:, 8:16], msk[:, 0:8], dmp[:]).then_inc(sV, 1)
            vv += 1
            nc.vector.wait_ge(sV, vv)
            nc.vector.tensor_scalar(msk[:, 16:24], pr[:], t8[:, 1:2], None, OP.is_equal).then_inc(sV, 1)
            vv += 1
            nc.vector.wait_ge(sV, vv)
            nc.vector.scalar_tensor_tensor(msk[:, 16:24], msk[:, 8:16], -1.0, msk[:, 16:24],
                                           OP.mult, OP.add).then_inc(sV, 1)
            vv += 1
            nc.vector.wait_ge(sV, vv)
            nc.vector.tensor_tensor_scan(dmp[:], ones8[:], msk[:, 16:24], 0.0,
                                         OP.mult, OP.add).then_inc(sV, 1)
            vv += 1
            nc.vector.wait_ge(sV, vv)
            nc.vector.tensor_scalar(dmp[:], dmp[:], 1.0, None, OP.is_equal).then_inc(sV, 1)
            vv += 1
            nc.vector.wait_ge(sV, vv)
            nc.vector.tensor_mul(msk[:, 24:32], msk[:, 16:24], dmp[:]).then_inc(sV, 1)
            vv += 1
            nc.vector.wait_ge(sV, vv)
            nc.vector.tensor_add(st_[:, 3:4], t8[:, 0:1], t8[:, 1:2]).then_inc(sV, 1)
            vv += 1
            nc.vector.wait_ge(sV, vv)
            nc.vector.tensor_scalar(st_[:, 4:5], st_[:, 3:4], 1.0, 1e-9, OP.mult, OP.add).then_inc(sV, 1)
            vv += 1
            nc.vector.wait_ge(sV, vv)
            nc.vector.reciprocal(st_[:, 4:5], st_[:, 4:5]).then_inc(sV, 1)
            vv += 1
            nc.vector.wait_ge(sV, vv)
            nc.vector.tensor_scalar(st_[:, 5:6], st_[:, 3:4], st_[:, 4:5], None, OP.mult).then_inc(sV, 1)
            vv += 1
            nc.vector.wait_ge(sV, vv)
            nc.vector.tensor_scalar(st_[:, 6:7], st_[:, 5:6], 1.0, 1e-9, OP.mult, OP.add).then_inc(sV, 1)
            vv += 1
            nc.vector.wait_ge(sV, vv)
            nc.vector.reciprocal(st_[:, 6:7], st_[:, 6:7]).then_inc(sV, 1)
            vv += 1
            nc.vector.wait_ge(sV, vv)
            nc.vector.tensor_scalar(st_[:, 7:8], st_[:, 4:5], st_[:, 6:7], None, OP.mult).then_inc(sV, 1)
            vv += 1
            nc.vector.wait_ge(sV, vv)
            nc.vector.tensor_scalar(wes[:], msk[:, 8:16], t8[:, 0:1], None, OP.mult).then_inc(sV, 1)
            vv += 1
            nc.vector.wait_ge(sV, vv)
            nc.vector.scalar_tensor_tensor(wes[:], msk[:, 24:32], t8[:, 1:2], wes[:],
                                           OP.mult, OP.add).then_inc(sV, 1)
            vv += 1
            nc.vector.wait_ge(sV, vv)
            nc.vector.tensor_scalar(wes[:], wes[:], st_[:, 7:8], None, OP.mult).then_inc(sV, 1)
            vv += 1
            nc.tensor.wait_ge(sV, vv)
            nc.tensor.transpose(pR[0:8, 8:136], wes[:], self.c_identf[:]).then_inc(sT, 1)
            nc.vector.wait_ge(sT, tb + 2)
            if i > 0:
                nc.vector.wait_ge(sO, i * 16)
            nc.vector.tensor_copy(wstg[:], pR[0:8, 8:136]).then_inc(sV, 1)
            vv += 1
            assert vv == 1 + (i + 1) * NV, (vv, i)
            nc.scalar.wait_ge(sV, vv)
            nc.scalar.wait_ge(sO, i * 16)
            nc.scalar.dma_start(out=self.weT_d[:, i * 128:(i + 1) * 128], in_=wstg[:]).then_inc(sO, 16)
        nc.sync.wait_ge(sO, (S // 128) * 16)

    def ph_moe(self):
        nc = self.nc
        xnb = self.sb("m_x", [128, 8 * 512], BF16)
        s1c = self.sb("m_s1", [128, 8 * 512], F32)
        x1c = self.sb("m_x1", [128, 4 * 1024], F32)
        acc = self.sb("m_ac", [128, 8 * 512], F32)
        w1b = self.sb("m_w1", [128, 8 * 2048], BF16)
        w2b = self.sb("m_w2", [128, 32 * 512], BF16)
        h1b = self.sb("m_h1", [128, 32 * 512], BF16)
        webr = self.sb("m_wr", [1, 512], F32)
        weB = self.sb("m_wb", [128, 512], F32)
        mbc = self.sb("m_mb", [128, 40], F32)
        ptok = self.sb("m_pt", [128, 4 * 1024], F32)
        pz = [self.ps("m_p%d" % j, [128, 512], F32) for j in range(4)]
        p2 = [self.ps("m_q%d" % j, [128, 512], F32) for j in range(4)]
        oL, oV, oO = self.phase_sems("m_oL", "m_oV", "m_oO")
        iL, iT, iE, iV, iA, iW = self.phase_sems("m_iL", "m_iT", "m_iE", "m_iV", "m_iA", "m_iW")
        oTf, oEf, oA = self.phase_sems("m_oTf", "m_oEf", "m_oA")
        with self.fori(0, S // 512) as c:
            nc.sync.dma_start(out=xnb[:], in_=self.xn1z[ts(c, 1), :, :]).then_inc(oL, 16)
            nc.sync.dma_start(out=s1c[:], in_=self.s1z[ts(c, 1), :, :]).then_inc(oL, 16)
            nc.scalar.dma_start(out=x1c[:].rearrange("p (t d) -> p t d", t=4),
                                in_=self.x1s.rearrange("(n p) d -> p n d", p=128)[:, ts(c, 4), :]).then_inc(oLA := self.sem("m_oLA"), 16)
            nc.vector.wait_ge(oL, 32)
            nc.vector.memset(acc[:], 0.0).then_inc(oV, 1)
            with self.fori(0, NEXP, depth=1) as e:
                nc.sync.dma_start(out=webr[:], in_=self.weT_d[ts(e, 1), ts(c, 512)]).then_inc(iL, 16)
                nc.scalar.dma_start(out=mbc[:], in_=self.mbboth_d[:, ts(e, 1), :].rearrange("p e n -> p (e n)")).then_inc(iLA := self.sem("m_iLA"), 16)
                for mh in range(2):
                    if mh == 1:
                        nc.sync.wait_ge(iW, 16)
                    nc.sync.dma_start(out=w1b[:].rearrange("p (k m) -> p k m", k=8),
                                      in_=self.mW1_b[ts(e, 8), :, mh * 2048:(mh + 1) * 2048].rearrange("k p m -> p k m")).then_inc(iW, 16)
                    nc.tensor.wait_ge(iW, (mh + 1) * 16)
                    nc.tensor.wait_ge(oV, 1)
                    for mm_ in range(16):
                        m = mh * 16 + mm_
                        b = mm_ % 4
                        gi = m
                        if m >= 4:
                            nc.tensor.wait_ge(iE, m - 3)
                        for k in range(8):
                            mmi = nc.tensor.matmul(pz[b][:], w1b[:, k * 2048 + mm_ * 128:k * 2048 + (mm_ + 1) * 128],
                                                   xnb[:, k * 512:(k + 1) * 512],
                                                   start=(k == 0), stop=(k == 7))
                        mmi.then_inc(iT, 1)
                        nc.scalar.wait_ge(iT, mh * 16 + mm_ + 1)
                        nc.scalar.wait_ge(iLA, 16)
                        nc.scalar.activation(h1b[:, m * 512:(m + 1) * 512], pz[b][:], AF.Gelu,
                                             bias=mbc[:, m:m + 1], scale=1.0).then_inc(iE, 1)
                    if mh == 0:
                        nc.sync.wait_ge(iT, 16)
                nc.tensor.wait_ge(iL, 16)
                nc.tensor.matmul(p2[3][:], self.c_ones1[:], webr[:], start=True, stop=True).then_inc(iT, 1)
                nc.scalar.wait_ge(iT, 33)
                nc.scalar.copy(weB[:], p2[3][:]).then_inc(iA, 1)
                for m2h in range(2):
                    if m2h == 1:
                        nc.sync.wait_ge(iT, 37)
                    nc.sync.wait_ge(iW, 32 + m2h * 16)
                    nc.sync.dma_start(out=w2b[:].rearrange("p (k m) -> p k m", k=32),
                                      in_=self.mW2_b[ts(e, 32), :, m2h * 512:(m2h + 1) * 512].rearrange("k p m -> p k m")).then_inc(iW, 16)
                    nc.tensor.wait_ge(iW, 32 + (m2h + 1) * 16)
                    nc.tensor.wait_ge(iE, 32)
                    for m2 in range(4):
                        d = m2h * 4 + m2
                        b = m2 % 4
                        if d == 3:
                            nc.tensor.wait_ge(iA, 1)
                        if d >= 4:
                            nc.tensor.wait_ge(iV, 2 * d - 7)
                        for k in range(32):
                            mmi = nc.tensor.matmul(p2[b][:], w2b[:, k * 512 + m2 * 128:k * 512 + (m2 + 1) * 128],
                                                   h1b[:, k * 512:(k + 1) * 512],
                                                   start=(k == 0), stop=(k == 31))
                        mmi.then_inc(iT, 1)
                        nc.vector.wait_ge(iT, 33 + m2h * 4 + m2 + 1)
                        nc.vector.wait_ge(iA, 1)
                        if d >= 1:
                            nc.vector.wait_ge(iV, 2 * d)
                        nc.vector.scalar_tensor_tensor(ptok[:, 0:512], p2[b][:], mbc[:, 32 + d:33 + d],
                                                       weB[:], OP.add, OP.mult).then_inc(iV, 1)
                        nc.vector.wait_ge(iV, m2h * 8 + m2 * 2 + 1)
                        nc.vector.tensor_add(acc[:, d * 512:(d + 1) * 512],
                                             acc[:, d * 512:(d + 1) * 512], ptok[:, 0:512]).then_inc(iV, 1)
                nc.gpsimd.wait_ge(iV, 16)
                self.clears(iL, self.sem("m_iLA"), iT, iE, iV, iA, iW)
            nc.scalar.wait_ge(oV, 1)
            nc.scalar.activation(acc[:], acc[:], AF.Silu, bias=0.0, scale=1.0).then_inc(oA, 1)
            nc.vector.wait_ge(oA, 1)
            nc.vector.tensor_mul(s1c[:], acc[:], s1c[:]).then_inc(oV, 1)
            nc.tensor.wait_ge(oV, 2)
            ntr = 0
            for tt2 in range(4):
                for e2 in range(8):
                    b = ntr % 2
                    if ntr >= 2:
                        nc.tensor.wait_ge(oEf, ntr - 1)
                    nc.tensor.transpose(pz[b][:, 0:128],
                                        s1c[:, e2 * 512 + tt2 * 128:e2 * 512 + (tt2 + 1) * 128],
                                        self.c_identf[:]).then_inc(oTf, 1)
                    nc.scalar.wait_ge(oTf, ntr + 1)
                    nc.scalar.copy(ptok[:, tt2 * 1024 + e2 * 128:tt2 * 1024 + (e2 + 1) * 128],
                                   pz[b][:, 0:128]).then_inc(oEf, 1)
                    ntr += 1
            nc.vector.wait_ge(oEf, 32)
            nc.vector.wait_ge(self.sem("m_oLA"), 16)
            nc.vector.tensor_add(x1c[:], x1c[:], ptok[:]).then_inc(oV, 1)
            nc.scalar.wait_ge(oV, 3)
            nc.scalar.dma_start(out=self.out.rearrange("(n p) d -> p n d", p=128)[:, ts(c, 4), :],
                                in_=x1c[:].rearrange("p (t d) -> p t d", t=4)).then_inc(oO, 16)
            nc.gpsimd.wait_ge(oO, 16)
            self.clears(oL, self.sem("m_oLA"), oV, oO, oTf, oEf, oA)

    def finish(self, dbg=()):
        nc = self.nc
        self._semmap = {}
        self._semidx = 0
        sF = self.sem("finS")
        n = 0
        for name in dbg:
            t = getattr(self, name)
            o = nc.dram_tensor("dbg_" + name, list(t.shape), t.dtype, kind="ExternalOutput").ap()
            nc.sync.dma_start(out=o[:], in_=t[:]).then_inc(sF, 16)
            n += 1
        if n:
            nc.sync.wait_ge(sF, 16 * n)


def build(segment="all", dbg=()):
    nc = bass.Bass("TRN2", target_bir_lowering=False)
    b = Builder(nc)
    b.declare_io()
    phases = Builder.PHASES if segment == "all" else segment
    for s in b._pool:
        nc.gpsimd.sem_clear(s)
    b.bar()
    for p in phases:
        b.begin_phase()
        getattr(b, "ph_" + p)()
        b.bar()
        b.end_phase()
    b.finish(dbg)
    return nc


def _constants():
    import ml_dtypes
    c = {}
    c["c_identf"] = np.eye(128, dtype=np.float32)
    c["c_identb"] = np.eye(128, dtype=ml_dtypes.bfloat16)
    p = np.arange(128)[:, None]
    f = np.arange(128)[None, :]
    negI = np.where(p <= f, 0.0, -1e30).astype(np.float32)
    negS = np.where(p < f, 0.0, -1e30).astype(np.float32)
    c["c_negtI"] = negI.T.copy()
    c["c_negtS"] = negS.T.copy()
    for nm, n in [("c_oneh16", 16), ("c_oneh12", 12), ("c_oneh8", 8)]:
        oh = np.zeros((n, n * 128), np.float32)
        for h in range(n):
            oh[h, h * 128:(h + 1) * 128] = 1.0
        c[nm] = oh
    c["c_eye12"] = np.eye(12, dtype=np.float32)
    c["c_ones1"] = np.ones((1, 128), np.float32)
    c["c_ones128c"] = np.ones((128, 1), np.float32)
    s2 = np.zeros((128, 2), np.float32)
    s2[:64, 0] = 1.0
    s2[64:, 1] = 1.0
    c["c_sel2"] = s2
    b2 = np.zeros((2, 128), np.float32)
    b2[0, :64] = 1.0
    b2[1, 64:] = 1.0
    c["c_bc2"] = b2
    return c


def make_in_maps(inputs):
    consts = _constants()
    in_maps = []
    for b in range(8):
        m = {}
        for k, v in inputs.items():
            if k == "L0_attn_x":
                m[k] = np.ascontiguousarray(v[b])
            else:
                m[k] = np.ascontiguousarray(v)
        m.update(consts)
        in_maps.append(m)
    return in_maps


def kernel(**inputs):
    nc = build("all")
    res = run_bass_kernel_spmd(nc, make_in_maps(inputs), core_ids=list(range(8)))
    return np.stack([res.results[b]["out"] for b in range(8)], axis=0)


# version 03
# speedup vs baseline: 1.0060x; 1.0060x over previous
import numpy as np
import contextlib
from contextlib import contextmanager
import concourse.bass as bass
import concourse.mybir as mybir
from concourse.bass_utils import run_bass_kernel_spmd
from concourse.expressions_rust import RuntimeValue

F32 = mybir.dt.float32
BF16 = mybir.dt.bfloat16
AF = mybir.ActivationFunctionType
OP = mybir.AluOpType
AX = mybir.AxisListType
ts = bass.ts

S = 8192
D = 1024
E0 = 2048
N0 = 16
DTR = 128
NH, DKH, DVH = 12, 64, 128
KD, VD = NH * DKH, NH * DVH
FF = 4096
NEXP = 8
TC = 128
NCH = S // TC


def set_seq(s):
    global S, NCH
    S = s
    NCH = S // TC


class Builder:
    PHASES = ["wcast", "ln0", "zin", "conv0", "xproj", "dtproj", "scan", "wout",
              "tern", "ffn0", "resid1", "qkvg", "convq", "delta", "route", "moe"]

    def __init__(self, nc):
        self.nc = nc
        self._semn = {}
        self._pool = [nc.alloc_semaphore('gp%d' % j) for j in range(20)]
        self._newsems = []
        self._stack = None
        self._loopregs = []
        self._cq_r1 = {}
        self._cq_qh = {}
        self._cq_tapv = 0

    def begin_phase(self):
        self._stack = contextlib.ExitStack()
        self._semmap = {}
        self._semidx = 0

    def end_phase(self):
        if self._stack is not None:
            self._stack.close()
            self._stack = None
        for s in self._pool:
            self.nc.gpsimd.sem_clear(s)
        self.bar()

    def sem(self, name):
        if name not in self._semmap:
            assert self._semidx < len(self._pool), "sem pool exhausted"
            self._semmap[name] = self._pool[self._semidx]
            self._semidx += 1
        return self._semmap[name]

    def phase_sems(self, *names):
        return tuple(self.sem(n) for n in names)

    def sbp(self, name, shape, dtype):
        return self.nc.alloc_sbuf_tensor(name, shape, dtype)

    def sb(self, name, shape, dtype):
        if self._stack is None:
            return self.nc.alloc_sbuf_tensor(name, shape, dtype)
        return self._stack.enter_context(self.nc.sbuf_tensor(name, shape, dtype))

    def ps(self, name, shape, dtype=F32):
        if self._stack is None:
            return self.nc.alloc_psum_tensor(name, shape, dtype)
        return self._stack.enter_context(self.nc.psum_tensor(name, shape, dtype))

    def bar(self):
        self.nc.all_engine_barrier()

    @contextmanager
    def fori(self, start, end, step=1, depth=0):
        nc = self.nc
        while len(self._loopregs) <= depth:
            j = len(self._loopregs)
            self._loopregs.append((
                nc.alloc_registers("kf_i%d" % j, engines=mybir.ALL_ENGINES),
                nc.alloc_registers("kf_s%d" % j, engines=mybir.ALL_ENGINES)))
        R, SN = self._loopregs[depth]
        lid = nc.next_id()
        loop_start, loop_end = "kf%d_loop" % lid, "kf%d_end" % lid
        nc.regs_mov(R, start)
        nc.br(loop_start, engines=mybir.ALL_ENGINES)
        with nc.body(loop_start, valid_engines=mybir.ALL_ENGINES):
            nc.regs_mov(SN, R)
            yield RuntimeValue(SN, min_val=start, max_val=end - step)
            nc.regs_alu(R, R, step, op=OP.add)
            nc.br_lt(R, end, on_true=loop_start, on_false=loop_end,
                     engines=mybir.ALL_ENGINES)
        nc.switch_bb(loop_end)

    def clears(self, *sems):
        self.bar()
        for s in sems:
            self.nc.gpsimd.sem_clear(s)
        self.bar()

    def declare_io(self):
        nc = self.nc
        dram = lambda n, s, d=F32, k="ExternalInput": nc.dram_tensor(n, s, d, kind=k).ap()
        self.x = dram("L0_attn_x", [S, D])
        self.W_in = dram("L0_attn_W_in", [D, 2 * E0])
        self.b_in = dram("L0_attn_b_in", [2 * E0])
        self.conv_w = dram("L0_attn_conv_w", [E0, 1, 4])
        self.conv_b = dram("L0_attn_conv_b", [E0])
        self.x_proj_w = dram("L0_attn_x_proj_w", [E0, DTR + 2 * N0])
        self.dt_proj_w = dram("L0_attn_dt_proj_w", [DTR, E0])
        self.dt_bias = dram("L0_attn_dt_bias", [E0])
        self.A_log = dram("L0_attn_A_log", [E0, N0])
        self.D_param = dram("L0_attn_D_param", [E0])
        self.W_out = dram("L0_attn_W_out", [E0, D])
        self.b_out = dram("L0_attn_b_out", [D])
        self.fW1 = dram("L0_ffn_W1", [FF, D])
        self.fb1 = dram("L0_ffn_b1", [FF])
        self.fW2 = dram("L0_ffn_W2", [D, FF])
        self.fb2 = dram("L0_ffn_b2", [D])
        self.Wq = dram("L1_attn_Wq", [D, KD])
        self.Wk = dram("L1_attn_Wk", [D, KD])
        self.Wv = dram("L1_attn_Wv", [D, VD])
        self.Wg = dram("L1_attn_Wg", [D, VD])
        self.Wo = dram("L1_attn_Wo", [VD, D])
        self.conv_q = dram("L1_attn_conv_q", [KD, 4])
        self.conv_k = dram("L1_attn_conv_k", [KD, 4])
        self.conv_v = dram("L1_attn_conv_v", [VD, 4])
        self.Wgk = dram("L1_attn_Wgk", [D, NH])
        self.Wb = dram("L1_attn_Wb", [D, NH])
        self.b_b = dram("L1_attn_b_b", [NH])
        self.A1_log = dram("L1_attn_A_log", [NH])
        self.dt1_bias = dram("L1_attn_dt_bias", [NH])
        self.gnw = dram("L1_attn_g_norm_weight", [DVH])
        self.Wr = dram("L1_ffn_Wr", [D, NEXP])
        self.br = dram("L1_ffn_br", [NEXP])
        self.mW1 = dram("L1_ffn_W1", [NEXP, D, FF])
        self.mb1 = dram("L1_ffn_b1", [NEXP, FF])
        self.mW2 = dram("L1_ffn_W2", [NEXP, FF, D])
        self.mb2 = dram("L1_ffn_b2", [NEXP, D])
        self.identf = dram("c_identf", [128, 128])
        self.identb = dram("c_identb", [128, 128], BF16)
        self.negtI = dram("c_negtI", [128, 128])
        self.negtS = dram("c_negtS", [128, 128])
        self.oneh16 = dram("c_oneh16", [16, 16 * 128])
        self.oneh12 = dram("c_oneh12", [12, 12 * 128])
        self.oneh8 = dram("c_oneh8", [8, 8 * 128])
        self.eye12 = dram("c_eye12", [12, 12])
        self.ones1 = dram("c_ones1", [1, 128])
        self.ones128c = dram("c_ones128c", [128, 1])
        self.sel2 = dram("c_sel2", [128, 2])
        self.bc2 = dram("c_bc2", [2, 128])
        self.out = dram("out", [S, D], F32, "ExternalOutput")

        scr = lambda n, s, d: nc.dram_tensor(n, s, d, kind="Internal").ap()
        self.xnz = scr("s_xnz", [S // 512, 128, 8 * 512], BF16)
        self.zz = scr("s_zz", [S // 512, 128, 32 * 512], F32)
        self.xc32 = scr("s_xc32", [16, 128, S], F32)
        self.delta32 = scr("s_delta32", [16, 128, S], F32)
        self.yT = scr("s_yT", [16, 128, S], BF16)
        self.s0T = scr("s_s0T", [8, 128, S], F32)
        self.g0z = scr("s_g0z", [S // 512, 128, 8 * 512], F32)
        self.x1s = scr("s_x1", [S, D], F32)
        self.xn1z = scr("s_xn1z", [S // 512, 128, 8 * 512], BF16)
        self.qkrz = scr("s_qkrz", [S // 512, 128, 12 * 512], BF16)
        self.vrz = scr("s_vrz", [S // 512, 128, 12 * 512], BF16)
        self.qkhz = scr("s_qkhz", [S // 512, 128, 12 * 512], BF16)
        self.vgz = scr("s_vgz", [S // 512, 128, 24 * 512], BF16)
        self.gl_d = scr("s_gl", [32, S], F32)
        self.BC = scr("s_BC", [16, 2 * S], F32)
        self.dtT = scr("s_dtT", [128, S], BF16)
        self.apack = scr("s_apack", [16, 128, 17], F32)
        self.mbboth_d = scr("s_mbboth", [128, NEXP, 40], F32)
        self.s1z = scr("s_s1z", [S // 512, 128, 8 * 512], F32)
        self.weT_d = scr("s_weT", [NEXP, S], F32)
        self.W_in_b = scr("w_in_b", [8, 128, 2 * E0], BF16)
        self.xproj_b = scr("w_xproj_b", [16, 128, DTR + 2 * N0], BF16)
        self.dtproj_b = scr("w_dtproj_b", [1, 128, E0], BF16)
        self.Wout_b = scr("w_wout_b", [16, 128, D], BF16)
        self.W1t_b = scr("w_w1t_b", [8, 128, FF], BF16)
        self.W2t_b = scr("w_w2t_b", [32, 128, D], BF16)
        self.Wq_b = scr("w_wq_b", [8, 128, KD], BF16)
        self.Wk_b = scr("w_wk_b", [8, 128, KD], BF16)
        self.Wv_b = scr("w_wv_b", [8, 128, VD], BF16)
        self.Wg_b = scr("w_wg_b", [8, 128, VD], BF16)
        self.Wo_b = scr("w_wo_b", [12, 128, D], BF16)
        self.Wgk_b = scr("w_wgk_b", [8, 128, NH], BF16)
        self.Wb_b = scr("w_wb_b", [8, 128, NH], BF16)
        self.Wr_b = scr("w_wr_b", [8, 128, NEXP], BF16)
        self.mW1_b = scr("w_mw1_b", [NEXP * 8, 128, FF], BF16)
        self.mW2_b = scr("w_mw2_b", [NEXP * 32, 128, D], BF16)

        sb = self.sbp
        self.c_identf = sb("k_identf", [128, 128], F32)
        self.c_identb = sb("k_identb", [128, 128], BF16)
        self.c_negtI = sb("k_negtI", [128, 128], F32)
        self.c_negtS = sb("k_negtS", [128, 128], F32)
        self.c_oneh16 = sb("k_oneh16", [16, 16 * 128], F32)
        self.c_oneh12 = sb("k_oneh12", [12, 12 * 128], F32)
        self.c_oneh8 = sb("k_oneh8", [8, 8 * 128], F32)
        self.c_eye12 = sb("k_eye12", [12, 12], F32)
        self.c_ones1 = sb("k_ones1", [1, 128], F32)
        self.c_ones128c = sb("k_ones128c", [128, 1], F32)
        self.c_sel2 = sb("k_sel2", [128, 2], F32)
        self.c_bc2 = sb("k_bc2", [2, 128], F32)
        self.bc_b_in = sb("bc_b_in", [128, 32], F32)
        self.bc_conv_b = sb("bc_conv_b", [128, 16], F32)
        self.bc_dt_bias = sb("bc_dt_bias", [128, 16], F32)
        self.bc_D = sb("bc_D", [128, 16], F32)
        self.bc_b_out = sb("bc_b_out", [128, 8], F32)
        self.bc_fb1 = sb("bc_fb1", [128, 32], F32)
        self.bc_fb2 = sb("bc_fb2", [128, 8], F32)
        self.bc_mb1 = sb("bc_mb1", [128, NEXP * 32], F32)
        self.bc_mb2 = sb("bc_mb2", [128, NEXP * 8], F32)
        self.bc_gnw = sb("bc_gnw", [128, 1], F32)
        self.bc_cw0 = sb("bc_cw0", [128, 16 * 4], F32)
        self.bc_cwq = sb("bc_cwq", [128, 6 * 4], F32)
        self.bc_cwk = sb("bc_cwk", [128, 6 * 4], F32)
        self.bc_cwv = sb("bc_cwv", [128, 12 * 4], F32)
        self.A0col = sb("A0col", [128, 16 * 16], F32)
        self.r_bb = sb("r_bb", [12, 1], F32)
        self.r_dtb = sb("r_dtb", [12, 1], F32)
        self.r_negA = sb("r_negA", [12, 1], F32)
        self.Sst = sb("Sst", [64, NH * DVH], F32)

    def _bias_cols(self, dst, src1d, ncols, col0=0):
        nc = self.nc
        st, pt = self._wc_stage, self._wc_pt
        sL, sPt, sPv = self.sem("wcL"), self.sem("wcPt"), self.sem("wcPv")
        self._wcl += 1
        nc.sync.wait_ge(sL, max(0, 16 * self._wcl - 16))
        nc.sync.dma_start(out=st[0:ncols, 0:128],
                          in_=src1d.rearrange("(a b) -> a b", b=128)).then_inc(sL, 16)
        nc.tensor.wait_ge(sL, 16 * self._wcl)
        nc.tensor.transpose(pt[0:128, 0:ncols], st[0:ncols, 0:128],
                            self.c_identf[0:ncols, 0:ncols]).then_inc(sPt, 1)
        self._wcpt += 1
        nc.vector.wait_ge(sPt, self._wcpt)
        nc.vector.tensor_copy(dst[:, col0:col0 + ncols], pt[0:128, 0:ncols]).then_inc(sPv, 1)
        self._wcpv += 1
        nc.tensor.wait_ge(sPv, self._wcpv)
        nc.sync.wait_ge(sPv, self._wcpv)

    def _cast_unroll(self, src2d, dst3d, ktiles, w):
        nc = self.nc
        sC = self.sem("wcC")
        for k in range(ktiles):
            c = self._wcc
            buf = self._wc_cin[c % 2]
            obuf = self._wc_cout[c % 2]
            if c >= 1:
                nc.sync.wait_ge(sC, (c - 1) * 48 + 32)
            nc.sync.dma_start(out=buf[:, 0:w], in_=src2d[k * 128:(k + 1) * 128, 0:w]).then_inc(sC, 16)
            nc.vector.wait_ge(sC, c * 48 + 16)
            nc.vector.tensor_copy(obuf[:, 0:w], buf[:, 0:w]).then_inc(sC, 16)
            nc.scalar.wait_ge(sC, c * 48 + 32)
            nc.scalar.dma_start(out=dst3d[k, :, 0:w], in_=obuf[:, 0:w]).then_inc(sC, 16)
            self._wcc += 1

    def ph_wcast(self):
        nc = self.nc
        sL = self.phase_sems("wcL", "wcPt", "wcPv", "wcPa", "wcC", "wcCV", "wcCS")[0]
        self._wcl = 0
        self._wcpt = 0
        self._wcpv = 0
        self._wcpa = 0
        self._wcc = 0
        for dst, src in [(self.c_identf, self.identf), (self.c_identb, self.identb),
                         (self.c_negtI, self.negtI), (self.c_negtS, self.negtS),
                         (self.c_oneh16, self.oneh16), (self.c_oneh12, self.oneh12),
                         (self.c_oneh8, self.oneh8), (self.c_eye12, self.eye12),
                         (self.c_ones1, self.ones1), (self.c_ones128c, self.ones128c),
                         (self.c_sel2, self.sel2), (self.c_bc2, self.bc2)]:
            nc.sync.wait_ge(sL, max(0, 16 * self._wcl - 16))
            nc.sync.wait_ge(sL, max(0, 16 * self._wcl))
            nc.sync.dma_start(out=dst[:], in_=src[:]).then_inc(sL, 16)
            self._wcl += 1
        self._wc_stage = self.sb("wc_stage", [128, 128], F32)
        self._wc_pt = self.ps("wc_pt", [128, 128], F32)
        nc.tensor.wait_ge(sL, 16 * self._wcl)
        self._bias_cols(self.bc_b_in, self.b_in, 32)
        self._bias_cols(self.bc_conv_b, self.conv_b, 16)
        self._bias_cols(self.bc_dt_bias, self.dt_bias, 16)
        self._bias_cols(self.bc_D, self.D_param, 16)
        self._bias_cols(self.bc_b_out, self.b_out, 8)
        self._bias_cols(self.bc_fb1, self.fb1, 32)
        self._bias_cols(self.bc_fb2, self.fb2, 8)
        for e in range(NEXP):
            self._bias_cols(self.bc_mb1, self.mb1[e, :], 32, e * 32)
            self._bias_cols(self.bc_mb2, self.mb2[e, :], 8, e * 8)
        sPv = self.sem("wcPv")
        for (dst, src, ntile) in [(self.bc_cw0, self.conv_w.rearrange("c o j -> (c o) j"), 16),
                                  (self.bc_cwq, self.conv_q, 6), (self.bc_cwk, self.conv_k, 6),
                                  (self.bc_cwv, self.conv_v, 12)]:
            for t in range(ntile):
                self._wcl += 1
                nc.sync.wait_ge(sL, max(0, 16 * self._wcl - 16))
                nc.sync.dma_start(out=self._wc_stage[0:128, 0:4],
                                  in_=src[t * 128:(t + 1) * 128, :]).then_inc(sL, 16)
                nc.vector.wait_ge(sL, 16 * self._wcl)
                nc.vector.tensor_copy(dst[:, t * 4:(t + 1) * 4],
                                      self._wc_stage[0:128, 0:4]).then_inc(sPv, 1)
                self._wcpv += 1
                nc.sync.wait_ge(sPv, self._wcpv)
        self._wcl += 1
        nc.sync.wait_ge(sL, max(0, 16 * self._wcl - 16))
        nc.sync.dma_start(out=self.bc_gnw[:, 0:1],
                          in_=self.gnw.rearrange("(a b) -> a b", b=1)).then_inc(sL, 16)
        a0s = self.sb("wc_a0", [128, 16 * 16], F32)
        for t in range(16):
            self._wcl += 1
            nc.sync.wait_ge(sL, max(0, 16 * self._wcl - 16))
            nc.sync.dma_start(out=a0s[:, t * 16:(t + 1) * 16],
                              in_=self.A_log[t * 128:(t + 1) * 128, :]).then_inc(sL, 16)
        sPa = self.sem("wcPa")
        nc.scalar.wait_ge(sL, 16 * self._wcl)
        nc.scalar.activation(self.A0col[:], a0s[:], AF.Exp, bias=0.0, scale=1.0).then_inc(sPa, 1)
        self._wcpa += 1
        nc.vector.wait_ge(sPa, self._wcpa)
        nc.vector.tensor_scalar_mul(self.A0col[:], self.A0col[:], -1.0).then_inc(sPv, 1)
        self._wcpv += 1
        nc.vector.wait_ge(sPv, self._wcpv)
        s12 = self.sb("wc_s12", [12, 3], F32)
        for j, src in enumerate([self.b_b, self.dt1_bias, self.A1_log]):
            self._wcl += 1
            nc.sync.wait_ge(sL, max(0, 16 * self._wcl - 16))
            nc.sync.dma_start(out=s12[0:12, j:j + 1],
                              in_=src.rearrange("(a b) -> a b", b=1)).then_inc(sL, 16)
        nc.vector.wait_ge(sL, 16 * self._wcl)
        nc.vector.tensor_copy(self.r_bb[:], s12[0:12, 0:1])
        nc.vector.tensor_copy(self.r_dtb[:], s12[0:12, 1:2]).then_inc(sPv, 1)
        self._wcpv += 1
        nc.scalar.wait_ge(sPv, self._wcpv)
        nc.scalar.activation(self.r_negA[:], s12[0:12, 2:3], AF.Exp, bias=0.0, scale=1.0).then_inc(sPa, 1)
        self._wcpa += 1
        nc.vector.wait_ge(sPa, self._wcpa)
        nc.vector.tensor_scalar_mul(self.r_negA[:], self.r_negA[:], -1.0).then_inc(sPv, 1)
        self._wcpv += 1
        nc.sync.wait_ge(sPv, self._wcpv)
        st17 = self.sb("wc_a17", [128, 17], F32)
        for e in range(16):
            if e > 0:
                nc.vector.wait_ge(sL, 16 * self._wcl)
            nc.vector.tensor_copy(st17[:, 0:16], self.A0col[:, e * 16:(e + 1) * 16])
            nc.vector.tensor_copy(st17[:, 16:17], self.bc_D[:, e:e + 1]).then_inc(sPv, 1)
            self._wcpv += 1
            nc.sync.wait_ge(sPv, self._wcpv)
            nc.sync.dma_start(out=self.apack[e, :, :], in_=st17[:]).then_inc(sL, 16)
            self._wcl += 1
        for e in range(NEXP):
            nc.sync.wait_ge(sL, 16 * self._wcl)
            nc.sync.dma_start(out=self.mbboth_d[:, e, 0:32],
                              in_=self.bc_mb1[:, e * 32:(e + 1) * 32]).then_inc(sL, 16)
            nc.sync.wait_ge(sL, 16 * self._wcl + 16)
            nc.sync.dma_start(out=self.mbboth_d[:, e, 32:40],
                              in_=self.bc_mb2[:, e * 8:(e + 1) * 8]).then_inc(sL, 16)
            self._wcl += 2
        nc.sync.wait_ge(sL, 16 * self._wcl)
        self.bar()
        self._wc_cin = [self.sb("wc_cin%d" % j, [128, 4096], F32) for j in range(2)]
        self._wc_cout = [self.sb("wc_cout%d" % j, [128, 4096], BF16) for j in range(2)]
        U = self._cast_unroll
        U(self.W_in, self.W_in_b, 8, 2 * E0)
        U(self.x_proj_w, self.xproj_b, 16, DTR + 2 * N0)
        U(self.dt_proj_w, self.dtproj_b, 1, E0)
        U(self.W_out, self.Wout_b, 16, D)
        U(self.Wq, self.Wq_b, 8, KD)
        U(self.Wk, self.Wk_b, 8, KD)
        U(self.Wv, self.Wv_b, 8, VD)
        U(self.Wg, self.Wg_b, 8, VD)
        U(self.Wo, self.Wo_b, 12, D)
        U(self.Wgk, self.Wgk_b, 8, NH)
        U(self.Wb, self.Wb_b, 8, NH)
        U(self.Wr, self.Wr_b, 8, NEXP)
        nc.vector.wait_ge(self.sem("wcC"), self._wcc * 48)
        self.bar()
        U(self.mW1.rearrange("e d f -> (e d) f"), self.mW1_b, NEXP * 8, FF)
        U(self.mW2.rearrange("e f d -> (e f) d"), self.mW2_b, NEXP * 32, D)
        nc.vector.wait_ge(self.sem("wcC"), self._wcc * 48)

    def _ln_tile(self, xt, xn, dump, stats, sV, sA, vb, ab):
        nc = self.nc
        nc.scalar.activation(dump[:, 0:D], xt[:], AF.Square, bias=0.0, scale=1.0,
                             accum_out=stats[:, 0:1]).then_inc(sA, 1)
        nc.scalar.wait_ge(sA, ab + 1)
        nc.scalar.activation(dump[:, D:2 * D], xt[:], AF.Identity, bias=0.0, scale=1.0,
                             accum_out=stats[:, 7:8]).then_inc(sA, 1)
        nc.vector.wait_ge(sA, ab + 2)
        nc.vector.tensor_scalar(stats[:, 1:2], stats[:, 7:8], stats[:, 7:8],
                                -1.0 / D, OP.mult, OP.mult).then_inc(sV, 1)
        nc.vector.wait_ge(sV, vb + 1)
        nc.vector.scalar_tensor_tensor(stats[:, 2:3], stats[:, 1:2], 1.0,
                                       stats[:, 0:1], OP.mult, OP.add).then_inc(sV, 1)
        nc.vector.wait_ge(sV, vb + 2)
        nc.vector.tensor_scalar(stats[:, 3:4], stats[:, 2:3], 1.0 / D, 1e-5,
                                OP.mult, OP.add).then_inc(sV, 1)
        nc.scalar.wait_ge(sV, vb + 3)
        nc.scalar.activation(stats[:, 4:5], stats[:, 3:4], AF.Ln, bias=0.0, scale=1.0).then_inc(sA, 1)
        nc.scalar.wait_ge(sA, ab + 3)
        nc.scalar.activation(stats[:, 5:6], stats[:, 4:5], AF.Exp, bias=0.0, scale=-0.5).then_inc(sA, 1)
        nc.vector.wait_ge(sA, ab + 4)
        nc.vector.tensor_scalar(stats[:, 6:7], stats[:, 7:8], stats[:, 5:6],
                                -1.0 / D, OP.mult, OP.mult).then_inc(sV, 1)
        nc.scalar.wait_ge(sV, vb + 4)
        nc.scalar.activation(xn[:], xt[:], AF.Identity, bias=stats[:, 6:7],
                             scale=stats[:, 5:6]).then_inc(sA, 1)

    def ph_ln0(self):
        nc = self.nc
        xt = self.sb("l0_xt", [128, D], F32)
        dump = self.sb("l0_dump", [128, 2 * D], F32)
        xn = self.sb("l0_xn", [128, D], BF16)
        stg = self.sb("l0_stg", [128, 8 * 128], BF16)
        stats = self.sb("l0_st", [128, 8], F32)
        pb = [self.ps("l0_pb%d" % j, [128, 128], BF16) for j in range(2)]
        sL, sV, sA, sT, sE, sO = self.phase_sems(*["l0_" + c for c in "LVATEO"])
        for i in range(S // 128):
            nc.sync.wait_ge(sA, i * 5)
            nc.sync.wait_ge(sL, i * 16)
            nc.sync.dma_start(out=xt[:], in_=self.x[i * 128:(i + 1) * 128, :]).then_inc(sL, 16)
            nc.vector.wait_ge(sL, i * 16 + 16)
            nc.scalar.wait_ge(sL, i * 16 + 16)
            self._ln_tile(xt, xn, dump, stats, sV, sA, i * 4, i * 5)
            nc.tensor.wait_ge(sA, i * 5 + 5)
            for d in range(8):
                j = i * 8 + d
                if j >= 2:
                    nc.tensor.wait_ge(sE, j - 1)
                nc.tensor.transpose(pb[d % 2][:], xn[:, d * 128:(d + 1) * 128],
                                    self.c_identb[:]).then_inc(sT, 1)
                nc.vector.wait_ge(sT, i * 8 + d + 1)
                if d == 0:
                    nc.vector.wait_ge(sO, i * 16)
                nc.vector.tensor_copy(stg[:, d * 128:(d + 1) * 128], pb[d % 2][:]).then_inc(sE, 1)
            nc.sync.wait_ge(sE, i * 8 + 8)
            nc.sync.wait_ge(sO, i * 16)
            nc.sync.dma_start(
                out=self.xnz[i // 4, :, :].rearrange("p (d s) -> p d s", d=8)[:, :, (i % 4) * 128:(i % 4 + 1) * 128],
                in_=stg[:].rearrange("p (d s) -> p d s", d=8)).then_inc(sO, 16)
        nc.sync.wait_ge(sO, (S // 128) * 16)

    def ph_zin(self):
        nc = self.nc
        wz = self.sb("z_w", [128, 8 * 4096], BF16)
        sW, = self.phase_sems("z_W")
        for k in range(8):
            nc.scalar.dma_start(out=wz[:, k * 4096:(k + 1) * 4096],
                                in_=self.W_in_b[k, :, :]).then_inc(sW, 16)
        nc.tensor.wait_ge(sW, 8 * 16)
        xch = self.sb("z_x", [128, 8 * 512], BF16)
        ostg = self.sb("z_o", [128, 32 * 512], F32)
        pz = [self.ps("z_p%d" % j, [128, 512], F32) for j in range(4)]
        sL, sT, sE = self.phase_sems("z_L", "z_T", "z_E")
        with self.fori(0, S // 512) as i:
            nc.sync.dma_start(out=xch[:], in_=self.xnz[ts(i, 1), :, :]).then_inc(sL, 16)
            nc.tensor.wait_ge(sL, 16)
            for m in range(32):
                b = m % 4
                if m >= 4:
                    nc.tensor.wait_ge(sE, m - 3)
                for k in range(8):
                    mm = nc.tensor.matmul(pz[b][:], wz[:, k * 4096 + m * 128:k * 4096 + (m + 1) * 128],
                                          xch[:, k * 512:(k + 1) * 512],
                                          start=(k == 0), stop=(k == 7))
                mm.then_inc(sT, 1)
                nc.scalar.wait_ge(sT, m + 1)
                fn = AF.Identity if m < 16 else AF.Silu
                nc.scalar.activation(ostg[:, m * 512:(m + 1) * 512], pz[b][:], fn,
                                     bias=self.bc_b_in[:, m:m + 1], scale=1.0).then_inc(sE, 1)
            nc.scalar.wait_ge(sE, 32)
            nc.scalar.dma_start(out=self.zz[ts(i, 1), :, :], in_=ostg[:]).then_inc(sZO := self.sem("z_O"), 16)
            nc.gpsimd.wait_ge(sL, 16)
            nc.gpsimd.wait_ge(sZO, 16)
            self.clears(sL, sT, sE, sZO)

    def ph_conv0(self):
        nc = self.nc
        wk = self.sb("c0_wk", [128, S + 3], F32)
        a1 = self.sb("c0_a1", [128, S], F32)
        a2 = self.sb("c0_a2", [128, S], F32)
        sL, sV, sA, sO = self.phase_sems("c0_L", "c0_V", "c0_A", "c0_O")
        for e in range(16):
            if e > 0:
                nc.sync.wait_ge(sV, e * 4)
            nc.sync.wait_ge(sL, e * 32)
            half = (S // 512) // 2
            for g in range(2):
                nc.sync.dma_start(
                    out=wk[:, 3:S + 3].rearrange("p (c s) -> p c s", c=S // 512)[:, g * half:(g + 1) * half, :],
                    in_=self.zz.rearrange("c p m -> p c m")[:, g * half:(g + 1) * half,
                                                           e * 512:(e + 1) * 512]).then_inc(sL, 16)
            if e == 0:
                nc.vector.memset(wk[:, 0:3], 0.0).then_inc(self.sem("c0_M"), 1)
                nc.vector.wait_ge(self.sem("c0_M"), 1)
            nc.vector.wait_ge(sL, e * 32 + 32)
            if e > 0:
                nc.vector.wait_ge(sA, e * 1)
                nc.vector.wait_ge(sO, e * 16)
            w = self.bc_cw0
            nc.vector.tensor_scalar(a1[:], wk[:, 0:S], w[:, 4 * e:4 * e + 1], None,
                                    OP.mult).then_inc(sV, 1)
            nc.vector.wait_ge(sV, e * 4 + 1)
            nc.vector.scalar_tensor_tensor(a2[:], wk[:, 1:S + 1], w[:, 4 * e + 1:4 * e + 2],
                                           a1[:], OP.mult, OP.add).then_inc(sV, 1)
            nc.vector.wait_ge(sV, e * 4 + 2)
            nc.vector.scalar_tensor_tensor(a1[:], wk[:, 2:S + 2], w[:, 4 * e + 2:4 * e + 3],
                                           a2[:], OP.mult, OP.add).then_inc(sV, 1)
            nc.vector.wait_ge(sV, e * 4 + 3)
            nc.vector.scalar_tensor_tensor(a2[:], wk[:, 3:S + 3], w[:, 4 * e + 3:4 * e + 4],
                                           a1[:], OP.mult, OP.add).then_inc(sV, 1)
            nc.scalar.wait_ge(sV, e * 4 + 4)
            if e > 0:
                nc.scalar.wait_ge(sO, e * 16)
            nc.scalar.activation(a1[:], a2[:], AF.Silu,
                                 bias=self.bc_conv_b[:, e:e + 1], scale=1.0).then_inc(sA, 1)
            nc.scalar.wait_ge(sA, e + 1)
            nc.scalar.wait_ge(sO, e * 16)
            nc.scalar.dma_start(out=self.xc32[e, :, :], in_=a1[:]).then_inc(sO, 16)
        nc.sync.wait_ge(sO, 16 * 16)

    def ph_xproj(self):
        nc = self.nc
        wxp = self.sb("xp_w", [128, 16 * 160], BF16)
        sW, = self.phase_sems("xp_W")
        nc.scalar.dma_start(out=wxp[:].rearrange("p (k m) -> p k m", k=16),
                            in_=self.xproj_b.rearrange("k p m -> p k m")).then_inc(sW, 16)
        nc.tensor.wait_ge(sW, 16)
        nc.vector.wait_ge(sW, 16)
        xcf = self.sb("xp_xf", [128, 16 * 512], F32)
        xcb = self.sb("xp_xb", [128, 16 * 512], BF16)
        dstg = self.sb("xp_d", [128, 512], BF16)
        bstg = self.sb("xp_bc", [16, 1024], F32)
        pd = self.ps("xp_pd", [128, 512], F32)
        pb = self.ps("xp_pb", [16, 512], F32)
        pc = self.ps("xp_pc", [16, 512], F32)
        sL, sT, sEv, sEa, sO = self.phase_sems("xp_L", "xp_T", "xp_Ev", "xp_Ea", "xp_O")
        for i in range(S // 512):
            if i > 0:
                nc.sync.wait_ge(sEv, i * 3 - 2)
            nc.sync.wait_ge(sL, i * 32)
            for g in range(2):
                nc.sync.dma_start(
                    out=xcf[:, g * 8 * 512:(g + 1) * 8 * 512].rearrange("p (e s) -> p e s", e=8),
                    in_=self.xc32.rearrange("e p s -> p e s")[:, g * 8:(g + 1) * 8,
                                                             i * 512:(i + 1) * 512]).then_inc(sL, 16)
            nc.vector.wait_ge(sL, i * 32 + 32)
            if i > 0:
                nc.vector.wait_ge(sT, i * 3)
            nc.vector.tensor_copy(xcb[:], xcf[:]).then_inc(sEv, 1)
            nc.tensor.wait_ge(sEv, i * 3 + 1)
            if i > 0:
                nc.tensor.wait_ge(sEv, i * 3)
                nc.tensor.wait_ge(sEa, i)
            for k in range(16):
                mm = nc.tensor.matmul(pd[:], wxp[:, k * 160:k * 160 + 128],
                                      xcb[:, k * 512:(k + 1) * 512],
                                      start=(k == 0), stop=(k == 15))
            mm.then_inc(sT, 1)
            for k in range(16):
                mm = nc.tensor.matmul(pb[:], wxp[:, k * 160 + 128:k * 160 + 144],
                                      xcb[:, k * 512:(k + 1) * 512],
                                      start=(k == 0), stop=(k == 15))
            mm.then_inc(sT, 1)
            for k in range(16):
                mm = nc.tensor.matmul(pc[:], wxp[:, k * 160 + 144:k * 160 + 160],
                                      xcb[:, k * 512:(k + 1) * 512],
                                      start=(k == 0), stop=(k == 15))
            mm.then_inc(sT, 1)
            nc.scalar.wait_ge(sT, i * 3 + 1)
            if i > 0:
                nc.scalar.wait_ge(sO, i * 48)
            nc.scalar.copy(dstg[:], pd[:]).then_inc(sEa, 1)
            nc.vector.wait_ge(sT, i * 3 + 2)
            if i > 0:
                nc.vector.wait_ge(sO, i * 48)
            nc.vector.tensor_copy(bstg[0:16, 0:512], pb[:]).then_inc(sEv, 1)
            nc.vector.wait_ge(sT, i * 3 + 3)
            nc.vector.wait_ge(sEv, i * 3 + 2)
            nc.vector.tensor_copy(bstg[0:16, 512:1024], pc[:]).then_inc(sEv, 1)
            nc.scalar.wait_ge(sEv, i * 3 + 3)
            nc.scalar.wait_ge(sEa, i + 1)
            nc.scalar.wait_ge(sO, i * 48)
            nc.scalar.dma_start(out=self.dtT[:, i * 512:(i + 1) * 512], in_=dstg[:]).then_inc(sO, 16)
            nc.scalar.dma_start(out=self.BC[:, i * 512:(i + 1) * 512],
                                in_=bstg[:, 0:512]).then_inc(sO, 16)
            nc.scalar.dma_start(out=self.BC[:, S + i * 512:S + (i + 1) * 512],
                                in_=bstg[:, 512:1024]).then_inc(sO, 16)
        nc.sync.wait_ge(sO, (S // 512) * 48)

    def ph_dtproj(self):
        nc = self.nc
        wdt = self.sb("dp_w", [128, E0], BF16)
        sW, = self.phase_sems("dp_W")
        nc.scalar.dma_start(out=wdt[:], in_=self.dtproj_b[0, :, :]).then_inc(sW, 16)
        nc.tensor.wait_ge(sW, 16)
        dch = self.sb("dp_x", [128, 512], BF16)
        ustg = self.sb("dp_u", [128, 16 * 512], F32)
        pz = [self.ps("dp_p%d" % j, [128, 512], F32) for j in range(4)]
        sL, sT, sV, sA, sO = self.phase_sems("dp_L", "dp_T", "dp_V", "dp_A", "dp_O")
        for i in range(S // 512):
            if i > 0:
                nc.sync.wait_ge(sT, i * 16)
            nc.sync.wait_ge(sL, i * 16)
            nc.sync.dma_start(out=dch[:], in_=self.dtT[:, i * 512:(i + 1) * 512]).then_inc(sL, 16)
            nc.tensor.wait_ge(sL, i * 16 + 16)
            for m in range(16):
                b = m % 4
                gm = i * 16 + m
                if gm >= 4:
                    nc.tensor.wait_ge(sV, gm - 3)
                nc.tensor.matmul(pz[b][:], wdt[:, m * 128:(m + 1) * 128], dch[:],
                                 start=True, stop=True).then_inc(sT, 1)
                nc.vector.wait_ge(sT, gm + 1)
                if i > 0 and m == 0:
                    nc.vector.wait_ge(sO, i * 32)
                nc.vector.tensor_scalar(ustg[:, m * 512:(m + 1) * 512], pz[b][:],
                                        self.bc_dt_bias[:, m:m + 1], 80.0,
                                        OP.add, OP.min).then_inc(sV, 1)
            nc.scalar.wait_ge(sV, i * 16 + 16)
            for m in range(16):
                sl = ustg[:, m * 512:(m + 1) * 512]
                nc.scalar.activation(sl, sl, AF.Exp, bias=0.0, scale=1.0).then_inc(sA, 1)
                nc.scalar.wait_ge(sA, i * 32 + 2 * m + 1)
                nc.scalar.activation(sl, sl, AF.Ln, bias=1.0, scale=1.0).then_inc(sA, 1)
                nc.scalar.wait_ge(sA, i * 32 + 2 * m + 2)
            nc.scalar.wait_ge(sO, i * 32)
            for g in range(2):
                nc.scalar.dma_start(
                    out=self.delta32.rearrange("e p s -> p e s")[:, g * 8:(g + 1) * 8,
                                                                 i * 512:(i + 1) * 512],
                    in_=ustg[:, g * 8 * 512:(g + 1) * 8 * 512].rearrange("p (e s) -> p e s", e=8)).then_inc(sO, 16)
        nc.sync.wait_ge(sO, (S // 512) * 32)

    def ph_scan(self):
        nc = self.nc
        drow = self.sb("sc_d", [128, S], F32)
        urow = self.sb("sc_u", [128, S], F32)
        yrow = self.sb("sc_y", [128, S], BF16)
        acur = self.sb("sc_a", [128, 17], F32)
        bct = [self.sb("sc_bc%d" % j, [16, 1024], F32) for j in range(2)]
        dab = [self.sb("sc_da%d" % j, [128, 512], F32) for j in range(2)]
        dub = self.sb("sc_du", [128, 512], F32)
        dbb = [self.sb("sc_db%d" % j, [128, 512], F32) for j in range(2)]
        hall = self.sb("sc_h", [128, 16 * 512], F32)
        pCc = [self.sb("sc_pc%d" % j, [128, 512], F32) for j in range(2)]
        ybufP = self.sb("sc_ybp", [128, 512], F32)
        carry = self.sb("sc_c", [128, 16], F32)
        pB = [self.ps("sc_pB%d" % j, [128, 512], F32) for j in range(2)]
        pC = [self.ps("sc_pC%d" % j, [128, 512], F32) for j in range(2)]
        sL, sT, sV, sA, sO = self.phase_sems("sc_L", "sc_T", "sc_V", "sc_A", "sc_O")
        sBC = self.sem("sc_BC")
        sP = self.sem("sc_P")
        NCHK = S // 512
        with self.fori(0, 16) as e:
            nc.sync.dma_start(out=drow[:], in_=self.delta32[ts(e, 1), :, :]).then_inc(sL, 16)
            nc.sync.dma_start(out=urow[:], in_=self.xc32[ts(e, 1), :, :]).then_inc(sL, 16)
            nc.scalar.dma_start(out=acur[:], in_=self.apack[ts(e, 1), :, :]).then_inc(sLA := self.sem("sc_LA"), 16)
            nc.vector.memset(carry[:], 0.0).then_inc(sV, 1)
            vv, pp, tt, aa = 1, 0, 0, 0
            dbuvD = dbuvP = None
            pvD = pvP = None
            scvD = scvP = None
            nc.vector.wait_ge(sL, 32)
            nc.scalar.wait_ge(sL, 32)
            nc.tensor.wait_ge(sL, 32)
            nc.gpsimd.wait_ge(sV, 1)
            nc.vector.wait_ge(sLA, 16)
            nc.scalar.wait_ge(sLA, 16)
            nc.tensor.wait_ge(sLA, 16)
            for c in range(NCHK):
                csl = slice(c * 512, (c + 1) * 512)
                if c >= 1:
                    nc.sync.wait_ge(sBC, c * 32)
                if c >= 2:
                    nc.sync.wait_ge(sT, (c - 1) * 32)
                nc.sync.dma_start(out=bct[c % 2][:, 0:512],
                                  in_=self.BC[:, c * 512:(c + 1) * 512]).then_inc(sBC, 16)
                nc.sync.dma_start(out=bct[c % 2][:, 512:1024],
                                  in_=self.BC[:, S + c * 512:S + (c + 1) * 512]).then_inc(sBC, 16)
                nc.vector.tensor_mul(dub[:], drow[:, csl], urow[:, csl]).then_inc(sV, 1)
                vv += 1
                duv = vv
                nc.tensor.wait_ge(sBC, (c + 1) * 32)
                dbuv = {}
                scanv = {}
                expv = {}
                evacv = {}
                ccv = {}
                hmv = {}
                addv = {}
                for n in range(16):
                    b = n % 2
                    hsl = slice(n * 512, (n + 1) * 512)
                    if n >= 2:
                        nc.tensor.wait_ge(sV, dbuv[n - 2])
                    elif c >= 1:
                        nc.tensor.wait_ge(sV, self._sc_dbuv_last[14 + b])
                    nc.tensor.matmul(pB[b][:], self.c_oneh16[:, n * 128:(n + 1) * 128],
                                     bct[c % 2][0:16, 0:512], start=True, stop=True).then_inc(sT, 1)
                    tt += 1
                    tB = tt
                    if n >= 2:
                        nc.scalar.wait_ge(sV, scanv[n - 2])
                    elif c >= 1:
                        nc.scalar.wait_ge(sV, self._sc_scanv_last[14 + b])
                    nc.scalar.activation(dab[b][:], drow[:, csl], AF.Exp, bias=0.0,
                                         scale=acur[:, n:n + 1]).then_inc(sA, 1)
                    aa += 1
                    expv[n] = aa
                    nc.vector.wait_ge(sT, tB)
                    nc.vector.wait_ge(sA, aa)
                    nc.vector.wait_ge(sV, duv)
                    nc.vector.tensor_mul(dbb[b][:], dub[:], pB[b][:]).then_inc(sV, 1)
                    vv += 1
                    dbuv[n] = vv
                    nc.vector.wait_ge(sV, vv)
                    if c >= 1:
                        nc.vector.wait_ge(sP, self._sc_ccv_last[n])
                        nc.vector.wait_ge(sP, self._sc_addv_last[n])
                    nc.vector.tensor_tensor_scan(hall[:, hsl], dab[b][:],
                                                 dbb[b][:], carry[:, n:n + 1],
                                                 OP.mult, OP.add).then_inc(sV, 1)
                    vv += 1
                    scanv[n] = vv
                    if n >= 2:
                        nc.tensor.wait_ge(sA, evacv[n - 2])
                    elif c >= 1:
                        nc.tensor.wait_ge(sA, self._sc_evacv_last[14 + b])
                    nc.tensor.matmul(pC[b][:], self.c_oneh16[:, n * 128:(n + 1) * 128],
                                     bct[c % 2][0:16, 512:1024], start=True, stop=True).then_inc(sT, 1)
                    tt += 1
                    nc.scalar.wait_ge(sT, tt)
                    if n >= 2:
                        nc.scalar.wait_ge(sP, hmv[n - 2])
                    elif c >= 1:
                        nc.scalar.wait_ge(sP, self._sc_hmv_last[14 + b])
                    nc.scalar.copy(pCc[b][:], pC[b][:]).then_inc(sA, 1)
                    aa += 1
                    evacv[n] = aa
                    nc.gpsimd.wait_ge(sV, scanv[n])
                    nc.gpsimd.tensor_copy(carry[:, n:n + 1],
                                          hall[:, n * 512 + 511:n * 512 + 512]).then_inc(sP, 1)
                    pp += 1
                    ccv[n] = pp
                    nc.gpsimd.wait_ge(sP, pp)
                    nc.gpsimd.wait_ge(sA, evacv[n])
                    nc.gpsimd.tensor_mul(hall[:, hsl], hall[:, hsl], pCc[b][:]).then_inc(sP, 1)
                    pp += 1
                    hmv[n] = pp
                    if n == 0:
                        nc.gpsimd.wait_ge(sP, pp)
                        if c >= 1:
                            nc.gpsimd.wait_ge(sV, self._sc_sttv)
                        nc.gpsimd.tensor_copy(ybufP[:], hall[:, 0:512]).then_inc(sP, 1)
                    else:
                        nc.gpsimd.wait_ge(sP, pp)
                        nc.gpsimd.tensor_add(ybufP[:], ybufP[:], hall[:, hsl]).then_inc(sP, 1)
                    pp += 1
                    addv[n] = pp
                self._sc_dbuv_last = dbuv
                self._sc_scanv_last = scanv
                self._sc_evacv_last = evacv
                self._sc_ccv_last = ccv
                self._sc_hmv_last = hmv
                self._sc_addv_last = addv
                nc.vector.wait_ge(sP, addv[15])
                nc.vector.scalar_tensor_tensor(yrow[:, csl], urow[:, csl], acur[:, 16:17],
                                               ybufP[:], OP.mult, OP.add).then_inc(sV, 1)
                vv += 1
                self._sc_sttv = vv
            nc.scalar.wait_ge(sV, vv)
            nc.scalar.dma_start(out=self.yT[ts(e, 1), :, :], in_=yrow[:]).then_inc(sO, 16)
            nc.gpsimd.wait_ge(sO, 16)
            self.clears(sL, sT, sV, sA, sO, sBC, sP, self.sem("sc_LA"))

    def ph_wout(self):
        nc = self.nc
        wwo = self.sb("wo_w", [128, 16 * D], BF16)
        sW, = self.phase_sems("wo_W")
        for g in range(2):
            nc.scalar.dma_start(out=wwo[:, g * 8 * D:(g + 1) * 8 * D].rearrange("p (k m) -> p k m", k=8),
                                in_=self.Wout_b.rearrange("k p m -> p k m")[:, g * 8:(g + 1) * 8, :]).then_inc(sW, 16)
        nc.tensor.wait_ge(sW, 32)
        ych = self.sb("wo_y", [128, 16 * 512], BF16)
        xrch = self.sb("wo_xr", [128, 16 * 512], F32)
        yx = self.sb("wo_yx", [128, 16 * 512], BF16)
        ostg = self.sb("wo_o", [128, 8 * 512], F32)
        pz = [self.ps("wo_p%d" % j, [128, 512], F32) for j in range(4)]
        sL, sT, sV, sE, sO = self.phase_sems("wo_L", "wo_T", "wo_V", "wo_E", "wo_O")
        for i in range(S // 512):
            if i > 0:
                nc.sync.wait_ge(sV, i)
            nc.sync.wait_ge(sL, i * 48)
            for g in range(2):
                nc.sync.dma_start(
                    out=ych[:, g * 8 * 512:(g + 1) * 8 * 512].rearrange("p (e s) -> p e s", e=8),
                    in_=self.yT.rearrange("e p s -> p e s")[:, g * 8:(g + 1) * 8,
                                                           i * 512:(i + 1) * 512]).then_inc(sL, 16)
            nc.sync.dma_start(out=xrch[:],
                              in_=self.zz[i, :, 16 * 512:32 * 512]).then_inc(sL, 16)
            nc.vector.wait_ge(sL, i * 48 + 48)
            if i > 0:
                nc.vector.wait_ge(sT, i * 8)
            nc.vector.tensor_mul(yx[:], ych[:], xrch[:]).then_inc(sV, 1)
            nc.tensor.wait_ge(sV, i + 1)
            for m in range(8):
                b = m % 4
                gm = i * 8 + m
                if gm >= 4:
                    nc.tensor.wait_ge(sE, gm - 3)
                for k in range(16):
                    mm = nc.tensor.matmul(pz[b][:], wwo[:, k * D + m * 128:k * D + (m + 1) * 128],
                                          yx[:, k * 512:(k + 1) * 512],
                                          start=(k == 0), stop=(k == 15))
                mm.then_inc(sT, 1)
                nc.scalar.wait_ge(sT, gm + 1)
                if i > 0 and m == 0:
                    nc.scalar.wait_ge(sO, i * 32)
                nc.scalar.activation(ostg[:, m * 512:(m + 1) * 512], pz[b][:], AF.Identity,
                                     bias=self.bc_b_out[:, m:m + 1], scale=1.0).then_inc(sE, 1)
            nc.scalar.wait_ge(sE, i * 8 + 8)
            nc.scalar.wait_ge(sO, i * 32)
            for g in range(2):
                nc.scalar.dma_start(
                    out=self.s0T.rearrange("e p s -> p e s")[:, g * 4:(g + 1) * 4,
                                                             i * 512:(i + 1) * 512],
                    in_=ostg[:, g * 4 * 512:(g + 1) * 4 * 512].rearrange("p (e s) -> p e s", e=4)).then_inc(sO, 16)
        nc.sync.wait_ge(sO, (S // 512) * 32)

    def ph_tern(self):
        nc = self.nc
        self._f_l = 0
        self._f_v = 0
        self._f_t = 0
        self._f_o = 0
        self._f_tr = 0
        tin = [self.sb("f_tin%d" % j, [128, 4096], F32) for j in range(2)]
        tstat = self.sb("f_ts", [128, 4], F32)
        absacc = self.sb("f_aa", [128, 1], F32)
        scb = self.sb("f_sc", [128, 1], F32)
        tern = self.sb("f_tb", [128, 4096], BF16)
        tstg = self.sb("f_tstg", [128, 32 * 128], BF16)
        ptr = [self.ps("f_pt%d" % j, [128, 128], BF16) for j in range(2)]
        psc = self.ps("f_psc", [128, 512], F32)
        sL, sV, sT, sE, sO = self.phase_sems("f_L", "f_V", "f_T", "f_E", "f_O")
        for wsel in range(2):
            ntile = 32 if wsel == 0 else 8
            wwid = 1024 if wsel == 0 else 4096
            srcw = self.fW1 if wsel == 0 else self.fW2
            dstw = self.W1t_b if wsel == 0 else self.W2t_b
            redv = {}
            ternv = {}
            nc.vector.wait_ge(sO, self._f_o * 16)
            nc.vector.memset(absacc[:], 0.0).then_inc(sV, 1)
            self._f_v += 1
            nc.vector.wait_ge(sV, self._f_v)
            for t in range(ntile):
                if t >= 2:
                    nc.sync.wait_ge(sV, redv[t - 2])
                else:
                    nc.sync.wait_ge(sV, self._f_v)
                nc.sync.wait_ge(sL, self._f_l * 16)
                nc.sync.dma_start(out=tin[t % 2][:, 0:wwid],
                                  in_=srcw[t * 128:(t + 1) * 128, :]).then_inc(sL, 16)
                self._f_l += 1
                nc.vector.wait_ge(sL, self._f_l * 16)
                nc.vector.wait_ge(sV, self._f_v)
                nc.vector.tensor_reduce(tstat[:, 0:1], tin[t % 2][:, 0:wwid], AX.X,
                                        OP.add, apply_absolute_value=True).then_inc(sV, 1)
                self._f_v += 1
                nc.vector.wait_ge(sV, self._f_v)
                nc.vector.tensor_add(absacc[:], absacc[:], tstat[:, 0:1]).then_inc(sV, 1)
                self._f_v += 1
                redv[t] = self._f_v
            nc.tensor.wait_ge(sV, self._f_v)
            nc.tensor.matmul(psc[0:1, 0:1], absacc[:], self.c_ones128c[:],
                             start=True, stop=True).then_inc(sT, 1)
            self._f_t += 1
            nc.vector.wait_ge(sT, self._f_t)
            nc.vector.tensor_scalar(tstat[0:1, 1:2], psc[0:1, 0:1],
                                    1.0 / (FF * D), 1e-8, OP.mult, OP.max).then_inc(sV, 1)
            self._f_v += 1
            nc.tensor.wait_ge(sV, self._f_v)
            nc.tensor.matmul(psc[0:128, 1:2], self.c_ones1[:], tstat[0:1, 1:2],
                             start=True, stop=True).then_inc(sT, 1)
            self._f_t += 1
            nc.vector.wait_ge(sT, self._f_t)
            nc.vector.tensor_copy(scb[:], psc[0:128, 1:2]).then_inc(sV, 1)
            self._f_v += 1
            nc.vector.wait_ge(sV, self._f_v)
            nc.vector.reciprocal(scb[:], scb[:]).then_inc(sV, 1)
            self._f_v += 1
            for t in range(ntile):
                if t >= 2:
                    nc.sync.wait_ge(sV, ternv[t - 2])
                else:
                    nc.sync.wait_ge(sV, self._f_v)
                nc.sync.wait_ge(sL, self._f_l * 16)
                nc.sync.dma_start(out=tin[t % 2][:, 0:wwid],
                                  in_=srcw[t * 128:(t + 1) * 128, :]).then_inc(sL, 16)
                self._f_l += 1
                nc.vector.wait_ge(sL, self._f_l * 16)
                nc.vector.wait_ge(sV, self._f_v)
                nc.vector.tensor_scalar(tin[t % 2][:, 0:wwid], tin[t % 2][:, 0:wwid],
                                        scb[:, 0:1], None, OP.mult).then_inc(sV, 1)
                self._f_v += 1
                nc.vector.wait_ge(sV, self._f_v)
                nc.vector.tensor_scalar(tin[t % 2][:, 0:wwid], tin[t % 2][:, 0:wwid],
                                        12582912.0, -12582912.0, OP.add, OP.add).then_inc(sV, 1)
                self._f_v += 1
                nc.vector.wait_ge(sV, self._f_v)
                nc.vector.wait_ge(sO, self._f_o * 16)
                nc.vector.tensor_scalar(tern[:, 0:wwid], tin[t % 2][:, 0:wwid],
                                        -1.0, 1.0, OP.max, OP.min).then_inc(sV, 1)
                self._f_v += 1
                ternv[t] = self._f_v
                ntr = wwid // 128
                nc.tensor.wait_ge(sV, self._f_v)
                base_tr = self._f_tr
                for d in range(ntr):
                    gidx = self._f_tr
                    if gidx >= 2:
                        nc.tensor.wait_ge(sE, gidx - 1)
                    nc.tensor.transpose(ptr[gidx % 2][:], tern[:, d * 128:(d + 1) * 128],
                                        self.c_identb[:]).then_inc(sT, 1)
                    self._f_t += 1
                    nc.vector.wait_ge(sT, self._f_t)
                    nc.vector.tensor_copy(tstg[:, d * 128:(d + 1) * 128],
                                          ptr[gidx % 2][:]).then_inc(sE, 1)
                    self._f_tr += 1
                nc.scalar.wait_ge(sE, self._f_tr)
                ng = max(1, ntr // 8)
                nc.scalar.wait_ge(sO, self._f_o * 16)
                for g in range(ng):
                    k1 = min((g + 1) * 8, ntr)
                    nc.scalar.dma_start(
                        out=dstw.rearrange("k p m -> p k m")[:, g * 8:k1, t * 128:(t + 1) * 128],
                        in_=tstg[:, g * 8 * 128:k1 * 128].rearrange("p (k m) -> p k m", m=128)).then_inc(sO, 16)
                    self._f_o += 1
        nc.vector.wait_ge(sO, self._f_o * 16)

    def ph_ffn0(self):
        nc = self.nc
        w1t = self.sb("f_w1", [128, 8 * FF], BF16)
        sW, = self.phase_sems("f_W")
        for g in range(2):
            nc.scalar.dma_start(out=w1t[:, g * 4 * FF:(g + 1) * 4 * FF].rearrange("p (k m) -> p k m", k=4),
                                in_=self.W1t_b.rearrange("k p m -> p k m")[:, g * 4:(g + 1) * 4, :]).then_inc(sW, 16)
        nc.tensor.wait_ge(sW, 32)
        xnb = self.sb("f_x", [128, 8 * 512], BF16)
        h1b = self.sb("f_h", [128, 32 * 512], BF16)
        w2g = [self.sb("f_w2%d" % j, [128, 8 * D], BF16) for j in range(2)]
        yac = self.sb("f_ya", [128, 8 * 512], F32)
        gstg = self.sb("f_g", [128, 8 * 512], F32)
        pz = [self.ps("f_p%d" % j, [128, 512], F32) for j in range(4)]
        p2 = [self.ps("f_q%d" % j, [128, 512], F32) for j in range(2)]
        sL2, sT2, sE2, sO2, sU = self.phase_sems("f_L2", "f_T2", "f_E2", "f_O2", "f_U")
        with self.fori(0, S // 512) as i:
            nc.sync.dma_start(out=xnb[:], in_=self.xnz[ts(i, 1), :, :]).then_inc(sL2, 16)
            nc.sync.dma_start(out=w2g[0][:].rearrange("p (k m) -> p k m", k=8),
                              in_=self.W2t_b.rearrange("k p m -> p k m")[:, 0:8, :]).then_inc(sU, 16)
            nc.tensor.wait_ge(sL2, 16)
            for m in range(32):
                b = m % 4
                if m >= 4:
                    nc.tensor.wait_ge(sE2, m - 3)
                for k in range(8):
                    mm = nc.tensor.matmul(pz[b][:], w1t[:, k * FF + m * 128:k * FF + (m + 1) * 128],
                                          xnb[:, k * 512:(k + 1) * 512],
                                          start=(k == 0), stop=(k == 7))
                mm.then_inc(sT2, 1)
                nc.scalar.wait_ge(sT2, m + 1)
                nc.scalar.activation(h1b[:, m * 512:(m + 1) * 512], pz[b][:], AF.Gelu,
                                     bias=self.bc_fb1[:, m:m + 1], scale=1.0).then_inc(sE2, 1)
            tcnt = 32
            sYs = self.sem("f_Ys")
            sYv = self.sem("f_Yv")
            sG = self.sem("f_G")
            evv = {}
            for kg in range(4):
                if kg < 3:
                    if kg >= 1:
                        nc.sync.wait_ge(sT2, 32 + kg * 8)
                    nc.sync.wait_ge(sU, (kg + 1) * 16)
                    nc.sync.dma_start(out=w2g[(kg + 1) % 2][:].rearrange("p (k m) -> p k m", k=8),
                                      in_=self.W2t_b.rearrange("k p m -> p k m")[:, (kg + 1) * 8:(kg + 2) * 8, :]).then_inc(sU, 16)
                nc.tensor.wait_ge(sU, kg * 16 + 16)
                nc.tensor.wait_ge(sE2, min(32, kg * 8 + 8))
                for m in range(8):
                    b = m % 2
                    gi = kg * 8 + m
                    if gi >= 2:
                        pk_, pv_ = evv[gi - 2]
                        if pk_ == 0:
                            nc.tensor.wait_ge(sYs, pv_)
                        else:
                            nc.tensor.wait_ge(sYv, pv_)
                    for k in range(8):
                        mm = nc.tensor.matmul(p2[b][:], w2g[kg % 2][:, k * D + m * 128:k * D + (m + 1) * 128],
                                              h1b[:, (kg * 8 + k) * 512:(kg * 8 + k + 1) * 512],
                                              start=(k == 0), stop=(k == 7))
                    mm.then_inc(sT2, 1)
                    tcnt += 1
                    if kg == 0:
                        nc.scalar.wait_ge(sT2, tcnt)
                        nc.scalar.copy(yac[:, m * 512:(m + 1) * 512], p2[b][:]).then_inc(sYs, 1)
                    else:
                        nc.vector.wait_ge(sT2, tcnt)
                        nc.vector.tensor_add(yac[:, m * 512:(m + 1) * 512],
                                             yac[:, m * 512:(m + 1) * 512], p2[b][:]).then_inc(sYv, 1)
                    evv[gi] = (0, m + 1) if kg == 0 else (1, (kg - 1) * 8 + m + 1)
            for m in range(8):
                nc.scalar.wait_ge(sYv, 16 + m + 1)
                nc.scalar.activation(gstg[:, m * 512:(m + 1) * 512], yac[:, m * 512:(m + 1) * 512],
                                     AF.Silu, bias=self.bc_fb2[:, m:m + 1], scale=1.0).then_inc(sG, 1)
            nc.scalar.wait_ge(sG, 8)
            nc.scalar.dma_start(out=self.g0z[ts(i, 1), :, :], in_=gstg[:]).then_inc(sO2, 16)
            nc.gpsimd.wait_ge(sO2, 16)
            self.clears(sL2, sT2, sE2, sO2, sU, sYs, sYv, sG)


    def ph_resid1(self):
        nc = self.nc
        s0c = self.sb("r_s", [128, 8 * 512], F32)
        g0c = self.sb("r_g", [128, 8 * 512], F32)
        psl = self.sb("r_p", [128, 8 * 512], F32)
        xtc = self.sb("r_x", [128, 4 * 1024], F32)
        ptok = self.sb("r_pt", [128, 4 * 1024], F32)
        x1c = self.sb("r_x1", [128, 4 * 1024], F32)
        xn1 = self.sb("r_xn", [128, 1024], BF16)
        xstg = self.sb("r_xs", [128, 8 * 512], BF16)
        dump = self.sb("r_du", [128, 2 * D], F32)
        stats = self.sb("r_st", [128, 8], F32)
        ptf = [self.ps("r_pf%d" % j, [128, 128], F32) for j in range(2)]
        ptb = [self.ps("r_pb%d" % j, [128, 128], BF16) for j in range(2)]
        sL, sV, sA, sTf, sEf, sTb, sEb, sO = self.phase_sems(
            "r_L", "r_V", "r_A", "r_Tf", "r_Ef", "r_Tb", "r_Eb", "r_O")
        ntrf = 0
        ntrb = 0
        for i in range(S // 512):
            if i > 0:
                nc.sync.wait_ge(sV, i * 21)
            nc.sync.wait_ge(sL, i * 48)
            nc.sync.dma_start(out=s0c[:].rearrange("p (e s) -> p e s", e=8),
                              in_=self.s0T.rearrange("e p s -> p e s")[:, :,
                                                                      i * 512:(i + 1) * 512]).then_inc(sL, 16)
            nc.sync.dma_start(out=g0c[:], in_=self.g0z[i, :, :]).then_inc(sL, 16)
            nc.sync.dma_start(out=xtc[:].rearrange("p (t d) -> p t d", t=4),
                              in_=self.x.rearrange("(n p) d -> p n d", p=128)[:, i * 4:(i + 1) * 4, :]).then_inc(sL, 16)
            nc.vector.wait_ge(sL, i * 48 + 48)
            nc.vector.tensor_mul(psl[:], s0c[:], g0c[:]).then_inc(sV, 1)
            vv = i * 21 + 1
            nc.tensor.wait_ge(sV, vv)
            for tt in range(4):
                for e in range(8):
                    if ntrf >= 2:
                        nc.tensor.wait_ge(sEf, ntrf - 1)
                    nc.tensor.transpose(ptf[ntrf % 2][:], psl[:, e * 512 + tt * 128:e * 512 + (tt + 1) * 128],
                                        self.c_identf[:]).then_inc(sTf, 1)
                    nc.scalar.wait_ge(sTf, ntrf + 1)
                    nc.scalar.copy(ptok[:, tt * 1024 + e * 128:tt * 1024 + (e + 1) * 128],
                                   ptf[ntrf % 2][:]).then_inc(sEf, 1)
                    ntrf += 1
            aa = i * 20
            for tt in range(4):
                nc.vector.wait_ge(sEf, i * 32 + (tt + 1) * 8)
                if i > 0 and tt == 0:
                    nc.vector.wait_ge(sO, i * 32)
                nc.vector.tensor_add(x1c[:, tt * 1024:(tt + 1) * 1024],
                                     xtc[:, tt * 1024:(tt + 1) * 1024],
                                     ptok[:, tt * 1024:(tt + 1) * 1024]).then_inc(sV, 1)
                vv += 1
                nc.scalar.wait_ge(sV, vv)
                if ntrb >= 8:
                    nc.scalar.wait_ge(sEb, ntrb)
                nc.vector.wait_ge(sV, vv)
                self._ln_tile(x1c[:, tt * 1024:(tt + 1) * 1024], xn1, dump, stats,
                              sV, sA, vv, aa)
                vv += 4
                aa += 5
                nc.tensor.wait_ge(sA, aa)
                for d in range(8):
                    if ntrb >= 2:
                        nc.tensor.wait_ge(sEb, ntrb - 1)
                    nc.tensor.transpose(ptb[ntrb % 2][:], xn1[:, d * 128:(d + 1) * 128],
                                        self.c_identb[:]).then_inc(sTb, 1)
                    nc.vector.wait_ge(sTb, ntrb + 1)
                    nc.vector.tensor_copy(xstg[:, d * 512 + tt * 128:d * 512 + (tt + 1) * 128],
                                          ptb[ntrb % 2][:]).then_inc(sEb, 1)
                    ntrb += 1
            nc.scalar.wait_ge(sEb, ntrb)
            nc.scalar.wait_ge(sO, i * 32)
            nc.scalar.dma_start(out=self.x1s.rearrange("(n p) d -> p n d", p=128)[:, i * 4:(i + 1) * 4, :],
                                in_=x1c[:].rearrange("p (t d) -> p t d", t=4)).then_inc(sO, 16)
            nc.scalar.dma_start(out=self.xn1z[i, :, :], in_=xstg[:]).then_inc(sO, 16)
        nc.sync.wait_ge(sO, (S // 512) * 32)

    def ph_qkvg(self):
        nc = self.nc
        wq = self.sb("q_wq", [128, 8 * KD], BF16)
        wk_ = self.sb("q_wk", [128, 8 * KD], BF16)
        wv = self.sb("q_wv", [128, 8 * VD], BF16)
        wg = self.sb("q_wg", [128, 8 * VD], BF16)
        wgk = self.sb("q_wgk", [128, 8 * NH], BF16)
        wbb = self.sb("q_wb", [128, 8 * NH], BF16)
        wr = self.sb("q_wr", [128, 8 * NEXP], BF16)
        sW, = self.phase_sems("q_W")
        nw = 0
        for (dst, srcw) in [(wq, self.Wq_b), (wk_, self.Wk_b), (wv, self.Wv_b),
                            (wg, self.Wg_b), (wgk, self.Wgk_b), (wbb, self.Wb_b),
                            (wr, self.Wr_b)]:
            kk = srcw.shape[2]
            nc.scalar.dma_start(out=dst[:].rearrange("p (k m) -> p k m", k=8),
                                in_=srcw.rearrange("k p m -> p k m")).then_inc(sW, 16)
            nw += 1
        nc.tensor.wait_ge(sW, nw * 16)
        xnb = self.sb("q_x", [128, 8 * 512], BF16)
        qkstg = self.sb("q_qk", [128, 12 * 512], BF16)
        vgstg = self.sb("q_vg", [128, 24 * 512], BF16)
        glstg = self.sb("q_gl", [32, 512], F32)
        bstg12 = self.sb("q_b12", [12, 512], F32)
        lstg8 = self.sb("q_l8", [8, 512], F32)
        pz = [self.ps("q_p%d" % j, [128, 512], F32) for j in range(2)]
        pgk = self.ps("q_pgk", [12, 512], F32)
        pbt = self.ps("q_pbt", [12, 512], F32)
        plg = self.ps("q_plg", [8, 512], F32)
        sL, sT, sE, sV, sA, sO = self.phase_sems("q_L", "q_T", "q_E", "q_V", "q_A", "q_O")
        with self.fori(0, S // 512) as i:
            nc.sync.dma_start(out=xnb[:], in_=self.xn1z[ts(i, 1), :, :]).then_inc(sL, 16)
            nc.tensor.wait_ge(sL, 16)
            grp = 0
            plan = ([("q", wq, mt, qkstg, mt, AF.Identity) for mt in range(6)] +
                    [("k", wk_, mt, qkstg, 6 + mt, AF.Identity) for mt in range(6)] +
                    [("v", wv, mt, vgstg, mt, AF.Identity) for mt in range(12)] +
                    [("g", wg, mt, vgstg, 12 + mt, AF.Silu) for mt in range(12)])
            for (_nm, wsrc, mt, dstg, dcol, fn) in plan:
                b = grp % 2
                if grp >= 2:
                    nc.tensor.wait_ge(sE, grp - 1)
                wid = wsrc.shape[1] // 8
                for k in range(8):
                    mm = nc.tensor.matmul(pz[b][:], wsrc[:, k * wid + mt * 128:k * wid + (mt + 1) * 128],
                                          xnb[:, k * 512:(k + 1) * 512],
                                          start=(k == 0), stop=(k == 7))
                mm.then_inc(sT, 1)
                grp += 1
                nc.scalar.wait_ge(sT, grp)
                nc.scalar.activation(dstg[:, dcol * 512:(dcol + 1) * 512], pz[b][:], fn,
                                     bias=0.0, scale=1.0).then_inc(sE, 1)
            for k in range(8):
                mm = nc.tensor.matmul(pgk[:], wgk[:, k * NH:(k + 1) * NH], xnb[:, k * 512:(k + 1) * 512],
                                      start=(k == 0), stop=(k == 7))
            mm.then_inc(sT, 1)
            for k in range(8):
                mm = nc.tensor.matmul(pbt[:], wbb[:, k * NH:(k + 1) * NH], xnb[:, k * 512:(k + 1) * 512],
                                      start=(k == 0), stop=(k == 7))
            mm.then_inc(sT, 1)
            for k in range(8):
                mm = nc.tensor.matmul(plg[:], wr[:, k * NEXP:(k + 1) * NEXP], xnb[:, k * 512:(k + 1) * 512],
                                      start=(k == 0), stop=(k == 7))
            mm.then_inc(sT, 1)
            nc.vector.wait_ge(sT, 37)
            nc.vector.tensor_scalar(glstg[0:12, :], pgk[:], self.r_dtb[:, 0:1], 80.0,
                                    OP.add, OP.min).then_inc(sV, 1)
            nc.scalar.wait_ge(sV, 1)
            nc.scalar.activation(glstg[0:12, :], glstg[0:12, :], AF.Exp, bias=0.0, scale=1.0).then_inc(sA, 1)
            nc.scalar.wait_ge(sA, 1)
            nc.scalar.activation(glstg[0:12, :], glstg[0:12, :], AF.Ln, bias=1.0, scale=1.0).then_inc(sA, 1)
            nc.vector.wait_ge(sA, 2)
            nc.vector.tensor_scalar(glstg[0:12, :], glstg[0:12, :], self.r_negA[:, 0:1], None,
                                    OP.mult).then_inc(sV, 1)
            nc.scalar.wait_ge(sT, 38)
            nc.scalar.activation(bstg12[:], pbt[:], AF.Sigmoid,
                                 bias=self.r_bb[:, 0:1], scale=1.0).then_inc(sA, 1)
            nc.scalar.wait_ge(sA, 3)
            nc.scalar.dma_start(out=glstg[12:24, :], in_=bstg12[:]).then_inc(sA, 16)
            nc.vector.wait_ge(sT, 39)
            nc.vector.tensor_copy(lstg8[:], plg[:]).then_inc(sV, 1)
            nc.scalar.wait_ge(sV, 3)
            nc.scalar.dma_start(out=glstg[24:32, :], in_=lstg8[:]).then_inc(sA, 16)
            nc.scalar.wait_ge(sE, 36)
            nc.scalar.wait_ge(sV, 3)
            nc.scalar.wait_ge(sA, 35)
            nc.scalar.dma_start(out=self.qkrz[ts(i, 1), :, :], in_=qkstg[:]).then_inc(sO, 16)
            nc.scalar.dma_start(out=self.vrz[ts(i, 1), :, :],
                                in_=vgstg[:, 0:12 * 512]).then_inc(sO, 16)
            nc.scalar.dma_start(out=self.vgz[ts(i, 1), :, 12 * 512:24 * 512],
                                in_=vgstg[:, 12 * 512:24 * 512]).then_inc(sO, 16)
            nc.scalar.dma_start(out=self.gl_d[:, ts(i, 512)], in_=glstg[:]).then_inc(sO, 16)
            nc.gpsimd.wait_ge(sO, 4 * 16)
            self.clears(sL, sT, sE, sV, sA, sO)

    def ph_convq(self):
        nc = self.nc
        wk2 = self.sb("cq_wk", [128, S + 3], BF16)
        a1 = self.sb("cq_a1", [128, S], F32)
        a2 = self.sb("cq_a2", [128, S], F32)
        q2 = self.sb("cq_q2", [128, S], F32)
        rbuf = self.sb("cq_r", [2, 512], F32)
        ostg = self.sb("cq_o", [128, S], BF16)
        psq = [self.ps("cq_ps%d" % j, [2, 512], F32) for j in range(2)]
        pbc = [self.ps("cq_pb%d" % j, [128, 512], F32) for j in range(2)]
        sL, sV, sA, sT, sO = self.phase_sems("cq_L", "cq_V", "cq_A", "cq_T", "cq_O")
        nc.vector.memset(wk2[:, 0:3], 0.0).then_inc(sV, 1)
        nc.vector.wait_ge(sV, 1)
        vv, aa, tt, oo, ll = 1, 0, 0, 0, 0
        for tile in range(24):
            is_v = tile >= 12
            if is_v:
                srcz, srccol = self.vrz, tile - 12
                cw, ci = self.bc_cwv, tile - 12
            elif tile < 6:
                srcz, srccol = self.qkrz, tile
                cw, ci = self.bc_cwq, tile
            else:
                srcz, srccol = self.qkrz, tile
                cw, ci = self.bc_cwk, tile - 6
            if tile > 0:
                nc.sync.wait_ge(sV, self._cq_tapv)
            nc.sync.wait_ge(sL, ll * 16)
            half = (S // 512) // 2
            for g in range(2):
                nc.sync.dma_start(
                    out=wk2[:, 3:S + 3].rearrange("p (c s) -> p c s", c=S // 512)[:, g * half:(g + 1) * half, :],
                    in_=srcz.rearrange("c p m -> p c m")[:, g * half:(g + 1) * half,
                                                         srccol * 512:(srccol + 1) * 512]).then_inc(sL, 16)
            ll += 2
            nc.vector.wait_ge(sL, ll * 16)
            nc.vector.wait_ge(sV, vv)
            if tile > 0:
                nc.vector.wait_ge(sA, aa)
            nc.vector.tensor_scalar(a1[:], wk2[:, 0:S], cw[:, 4 * ci:4 * ci + 1], None,
                                    OP.mult).then_inc(sV, 1)
            vv += 1
            nc.vector.wait_ge(sV, vv)
            nc.vector.scalar_tensor_tensor(a2[:], wk2[:, 1:S + 1], cw[:, 4 * ci + 1:4 * ci + 2],
                                           a1[:], OP.mult, OP.add).then_inc(sV, 1)
            vv += 1
            nc.vector.wait_ge(sV, vv)
            nc.vector.scalar_tensor_tensor(a1[:], wk2[:, 2:S + 2], cw[:, 4 * ci + 2:4 * ci + 3],
                                           a2[:], OP.mult, OP.add).then_inc(sV, 1)
            vv += 1
            nc.vector.wait_ge(sV, vv)
            nc.vector.scalar_tensor_tensor(a2[:], wk2[:, 3:S + 3], cw[:, 4 * ci + 3:4 * ci + 4],
                                           a1[:], OP.mult, OP.add).then_inc(sV, 1)
            vv += 1
            self._cq_tapv = vv
            nc.scalar.wait_ge(sV, vv)
            if tile > 0:
                nc.scalar.wait_ge(sO, oo * 16)
            if is_v:
                nc.scalar.activation(ostg[:], a2[:], AF.Silu, bias=0.0, scale=1.0).then_inc(sA, 1)
                aa += 1
                nc.scalar.wait_ge(sA, aa)
                nc.scalar.wait_ge(sO, oo * 16)
                half = (S // 512) // 2
                for g in range(2):
                    nc.scalar.dma_start(
                        out=self.vgz.rearrange("c p m -> p c m")[:, g * half:(g + 1) * half,
                                                                 (tile - 12) * 512:(tile - 11) * 512],
                        in_=ostg[:].rearrange("p (c s) -> p c s", c=S // 512)[:, g * half:(g + 1) * half, :]).then_inc(sO, 16)
                    oo += 1
            else:
                nc.scalar.activation(a1[:], a2[:], AF.Silu, bias=0.0, scale=1.0).then_inc(sA, 1)
                aa += 1
                nc.scalar.wait_ge(sA, aa)
                nc.scalar.activation(q2[:], a1[:], AF.Square, bias=0.0, scale=1.0).then_inc(sA, 1)
                aa += 1
                nc.tensor.wait_ge(sA, aa)
                nc.vector.wait_ge(sO, oo * 16)
                scl = 0.125 if tile < 6 else 1.0
                for sub in range(S // 512):
                    ssl = slice(sub * 512, (sub + 1) * 512)
                    bq = sub % 2
                    if sub >= 2:
                        nc.tensor.wait_ge(sV, self._cq_r1[sub - 2])
                    nc.tensor.matmul(psq[bq][:], self.c_sel2[:], q2[:, ssl],
                                     start=True, stop=True).then_inc(sT, 1)
                    tt += 1
                    nc.vector.wait_ge(sT, tt)
                    nc.vector.tensor_scalar(rbuf[:], psq[bq][:], 1.0, 1e-6,
                                            OP.mult, OP.add).then_inc(sV, 1)
                    vv += 1
                    self._cq_r1[sub] = vv
                    nc.scalar.wait_ge(sV, vv)
                    nc.scalar.sqrt(rbuf[:], rbuf[:]).then_inc(sA, 1)
                    aa += 1
                    nc.vector.wait_ge(sA, aa)
                    nc.vector.reciprocal(rbuf[:], rbuf[:]).then_inc(sV, 1)
                    vv += 1
                    nc.tensor.wait_ge(sV, vv)
                    if sub >= 2:
                        nc.tensor.wait_ge(sV, self._cq_qh[sub - 2])
                    nc.tensor.matmul(pbc[bq][:], self.c_bc2[:], rbuf[:],
                                     start=True, stop=True).then_inc(sT, 1)
                    tt += 1
                    nc.vector.wait_ge(sT, tt)
                    nc.vector.scalar_tensor_tensor(ostg[:, ssl], a1[:, ssl], scl,
                                                   pbc[bq][:], OP.mult, OP.mult).then_inc(sV, 1)
                    vv += 1
                    self._cq_qh[sub] = vv
                nc.scalar.wait_ge(sV, vv)
                nc.scalar.wait_ge(sO, oo * 16)
                half = (S // 512) // 2
                for g in range(2):
                    nc.scalar.dma_start(
                        out=self.qkhz.rearrange("c p m -> p c m")[:, g * half:(g + 1) * half,
                                                                  tile * 512:(tile + 1) * 512],
                        in_=ostg[:].rearrange("p (c s) -> p c s", c=S // 512)[:, g * half:(g + 1) * half, :]).then_inc(sO, 16)
                    oo += 1
        nc.sync.wait_ge(sO, oo * 16)

    def ph_delta(self):
        nc = self.nc
        woB = self.sb("d_wo", [128, 12 * D], BF16)
        onesg = self.sb("d_one", [12, TC], F32)
        sW, = self.phase_sems("d_W")
        for g in range(2):
            nc.scalar.dma_start(out=woB[:, g * 6 * D:(g + 1) * 6 * D].rearrange("p (k m) -> p k m", k=6),
                                in_=self.Wo_b.rearrange("k p m -> p k m")[:, g * 6:(g + 1) * 6, :]).then_inc(sW, 16)
        nc.vector.memset(onesg[:], 1.0)
        nc.vector.memset(self.Sst[:], 0.0).then_inc(sW, 1)
        nc.tensor.wait_ge(sW, 33)
        nc.gpsimd.wait_ge(sW, 33)
        self.bar()
        gkb = self.sb("d_gkb", [24, 512], F32)
        gkbb = self.sb("d_gkbb", [12, 512], F32)
        qkc = self.sb("d_qk", [128, 12 * 512], BF16)
        vgc = self.sb("d_vg", [128, 24 * 512], BF16)
        gam = self.sb("d_gam", [12, TC], F32)
        ngam = self.sb("d_ng", [12, TC], F32)
        dec = self.sb("d_dec", [12, TC], F32)
        expdec = self.sb("d_ed", [12, TC], F32)
        lnb = self.sb("d_lnb", [12, TC], F32)
        expg = self.sb("d_eg", [12, TC], F32)
        gTb = self.sb("d_gT", [128, 12], F32)
        bTb = self.sb("d_bT", [128, 12], F32)
        e1b = self.sb("d_e1", [128, TC], F32)
        e2b = self.sb("d_e2", [128, TC], F32)
        Wb = self.sb("d_Wb", [128, TC], BF16)
        NTq = [self.sb("d_nt%d" % j, [128, TC], F32) for j in range(2)]
        Tq = self.sb("d_tq", [128, TC], F32)
        Uj = [self.sb("d_u%d" % j, [128, DVH], F32) for j in range(2)]
        Ub = self.sb("d_ub", [128, DVH], BF16)
        Kb = self.sb("d_kb", [64, TC], BF16)
        Qb = self.sb("d_qb", [64, TC], BF16)
        S0b = self.sb("d_s0", [64, DVH], BF16)
        Vt = self.sb("d_vt", [128, DVH], BF16)
        dcs = self.sb("d_dc", [128, 1], F32)
        eGb = self.sb("d_egb", [128, 1], F32)
        KtT = self.sb("d_kt", [128, 64], BF16)
        o2b = self.sb("d_o2", [128, TC], F32)
        osb = self.sb("d_os", [128, TC], BF16)
        onb = self.sb("d_on", [128, TC], F32)
        rb2 = self.sb("d_rb", [1, TC], F32)
        ostage = self.sb("d_ost", [128, 12 * TC], BF16)
        s1stg = self.sb("d_s1", [128, 8 * 512], F32)
        pk = [self.ps("d_p%d" % j, [128, 512], F32) for j in range(6)]
        pX = self.ps("d_px", [128, 512], BF16)
        pY = self.ps("d_py", [128, 512], F32)
        sL, sT, sV, sA, sO = self.phase_sems("d_L", "d_T", "d_V", "d_A", "d_O")
        PA, PB, PC, PD, PE, PF = range(6)

        with self.fori(0, S // 512) as c:
            bk = {}

            def guard(bank):
                if bank in bk:
                    sem, val = bk[bank]
                    nc.tensor.wait_ge(sem, val)

            nc.sync.dma_start(out=gkb[:], in_=self.gl_d[0:24, ts(c, 512)]).then_inc(sL, 16)
            nc.sync.dma_start(out=qkc[:], in_=self.qkhz[ts(c, 1), :, :]).then_inc(sL, 16)
            nc.scalar.dma_start(out=vgc[:], in_=self.vgz[ts(c, 1), :, :]).then_inc(sLA2 := self.sem("d_LA"), 16)
            nc.scalar.wait_ge(sL, 32)
            nc.scalar.dma_start(out=gkbb[0:12, :], in_=gkb[12:24, :]).then_inc(sLA2, 16)
            vv = tt = aa = 0
            nc.vector.wait_ge(sL, 32)
            nc.scalar.wait_ge(sL, 32)
            nc.tensor.wait_ge(sL, 32)
            nc.vector.wait_ge(sLA2, 32)
            nc.scalar.wait_ge(sLA2, 32)
            nc.tensor.wait_ge(sLA2, 32)
            for cb in range(4):
              csl = slice(cb * 128, (cb + 1) * 128)
              if True:
                nc.vector.wait_ge(sT, tt)
                nc.vector.tensor_tensor_scan(gam[:], onesg[:], gkb[0:12, csl], 0.0,
                                             OP.mult, OP.add).then_inc(sV, 1)
              vv += 1
              nc.vector.wait_ge(sV, vv)
              nc.vector.tensor_scalar_mul(ngam[:], gam[:], -1.0).then_inc(sV, 1)
              vv += 1
              nc.vector.wait_ge(sV, vv)
              nc.vector.tensor_scalar(dec[:], gam[:], gam[:, TC - 1:TC], None,
                                      OP.subtract).then_inc(sV, 1)
              vv += 1
              nc.scalar.wait_ge(sV, vv)
              nc.scalar.activation(expdec[:], dec[:], AF.Exp, bias=0.0, scale=-1.0).then_inc(sA, 1)
              nc.scalar.activation(lnb[:], gkbb[0:12, csl], AF.Ln, bias=0.0, scale=1.0).then_inc(sA, 1)
              nc.scalar.activation(expg[:], gam[:], AF.Exp, bias=0.0, scale=1.0).then_inc(sA, 1)
              aa += 3
              nc.tensor.wait_ge(sV, vv)
              nc.tensor.transpose(pY[0:128, 0:12], gam[:], self.c_identf[0:12, 0:12]).then_inc(sT, 1)
              tt += 1
              nc.scalar.wait_ge(sT, tt)
              nc.scalar.copy(gTb[:], pY[0:128, 0:12]).then_inc(sA, 1)
              aa += 1
              nc.tensor.wait_ge(sA, aa)
              nc.tensor.transpose(pY[0:128, 0:12], gkbb[0:12, csl], self.c_identf[0:12, 0:12]).then_inc(sT, 1)
              tt += 1
              nc.scalar.wait_ge(sT, tt)
              nc.scalar.copy(bTb[:], pY[0:128, 0:12]).then_inc(sA, 1)
              aa += 1
              bk_pY = ("sA", aa)
              for h in range(12):
                  oh = self.c_oneh12[:, h * 128:(h + 1) * 128]
                  rq = (h % 2) * 64
                  qt = (h // 2)
                  qh = qkc[rq:rq + 64, qt * 512 + cb * 128:qt * 512 + (cb + 1) * 128]
                  kh = qkc[rq:rq + 64, (qt + 6) * 512 + cb * 128:(qt + 6) * 512 + (cb + 1) * 128]
                  vh = vgc[:, h * 512 + cb * 128:h * 512 + (cb + 1) * 128]
                  gh = vgc[:, (12 + h) * 512 + cb * 128:(12 + h) * 512 + (cb + 1) * 128]
                  guard(PA)
                  nc.tensor.matmul(pk[PA][:, 0:TC], oh, gam[:], start=True, stop=False)
                  nc.tensor.matmul(pk[PA][:, 0:TC], ngam[:], oh, start=False, stop=False)
                  nc.tensor.matmul(pk[PA][:, 0:TC], self.c_negtI[:], self.c_identf[:],
                                   start=False, stop=True).then_inc(sT, 1)
                  tt += 1
                  nc.scalar.wait_ge(sT, tt)
                  nc.scalar.activation(e1b[:], pk[PA][:, 0:TC], AF.Exp, bias=0.0, scale=1.0).then_inc(sA, 1)
                  aa += 1
                  bk[PA] = (sA, aa)
                  guard(PB)
                  nc.tensor.matmul(pk[PB][:, 0:TC], oh, gam[:], start=True, stop=False)
                  nc.tensor.matmul(pk[PB][:, 0:TC], ngam[:], oh, start=False, stop=False)
                  nc.tensor.matmul(pk[PB][:, 0:TC], oh, lnb[:], start=False, stop=False)
                  nc.tensor.matmul(pk[PB][:, 0:TC], self.c_negtS[:], self.c_identf[:],
                                   start=False, stop=True).then_inc(sT, 1)
                  tt += 1
                  nc.scalar.wait_ge(sT, tt)
                  nc.scalar.activation(e2b[:], pk[PB][:, 0:TC], AF.Exp, bias=0.0, scale=1.0).then_inc(sA, 1)
                  aa += 1
                  bk[PB] = (sA, aa)
                  guard(PC)
                  nc.tensor.matmul(pk[PC][:, 0:TC], kh, qh, start=True, stop=True).then_inc(sT, 1)
                  tt += 1
                  guard(PD)
                  nc.tensor.matmul(pk[PD][:, 0:TC], kh, kh, start=True, stop=True).then_inc(sT, 1)
                  tt += 1
                  nc.vector.wait_ge(sT, tt)
                  nc.vector.wait_ge(sA, aa)
                  nc.vector.tensor_mul(Wb[:], pk[PC][:, 0:TC], e1b[:]).then_inc(sV, 1)
                  vv += 1
                  bk[PC] = (sV, vv)
                  nc.vector.wait_ge(sV, vv)
                  nc.vector.scalar_tensor_tensor(NTq[0][:], pk[PD][:, 0:TC], -1.0, e2b[:],
                                                 OP.mult, OP.mult).then_inc(sV, 1)
                  vv += 1
                  bk[PD] = (sV, vv)
                  guard(PE)
                  nc.tensor.matmul(pk[PE][:, 0:TC], oh, expg[:], start=True, stop=True).then_inc(sT, 1)
                  tt += 1
                  nc.vector.wait_ge(sT, tt)
                  nc.vector.tensor_mul(Kb[0:64, :], kh, pk[PE][rq:rq + 64, 0:TC]).then_inc(sV, 1)
                  vv += 1
                  nc.vector.wait_ge(sV, vv)
                  nc.vector.tensor_mul(Qb[0:64, :], qh, pk[PE][rq:rq + 64, 0:TC]).then_inc(sV, 1)
                  vv += 1
                  bk[PE] = (sV, vv)
                  nc.scalar.wait_ge(sV, vv)
                  nc.scalar.copy(S0b[:], self.Sst[:, h * DVH:(h + 1) * DVH]).then_inc(sA, 1)
                  aa += 1
                  guard(PF)
                  nc.tensor.wait_ge(sV, vv)
                  nc.tensor.wait_ge(sA, aa)
                  nc.tensor.matmul(pk[PF][:, 0:DVH], Kb[0:64, :], S0b[:], start=True, stop=True).then_inc(sT, 1)
                  tt += 1
                  nc.tensor.transpose(pX[:, 0:TC], vh, self.c_identb[:]).then_inc(sT, 1)
                  tt += 1
                  nc.scalar.wait_ge(sT, tt)
                  nc.scalar.copy(Vt[:], pX[:, 0:TC]).then_inc(sA, 1)
                  aa += 1
                  nc.vector.wait_ge(sT, tt)
                  nc.vector.wait_ge(sA, aa)
                  nc.vector.tensor_sub(Uj[0][:], Vt[:], pk[PF][:, 0:DVH]).then_inc(sV, 1)
                  vv += 1
                  bk[PF] = (sV, vv)
                  nc.vector.wait_ge(sV, vv)
                  nc.vector.tensor_scalar(Uj[0][:], Uj[0][:], bTb[:, h:h + 1], None,
                                          OP.mult).then_inc(sV, 1)
                  vv += 1
                  upar = 0
                  qpar = 0
                  for j in range(7):
                      nc.tensor.wait_ge(sV, vv)
                      nc.tensor.wait_ge(sA, aa)
                      nc.tensor.matmul(pY[:, 0:DVH], NTq[qpar][:], Uj[upar][:],
                                       start=True, stop=False)
                      nc.tensor.matmul(pY[:, 0:DVH], self.c_identf[:], Uj[upar][:],
                                       start=False, stop=True).then_inc(sT, 1)
                      tt += 1
                      nc.scalar.wait_ge(sT, tt)
                      nc.scalar.copy(Uj[1 - upar][:], pY[:, 0:DVH]).then_inc(sA, 1)
                      aa += 1
                      upar = 1 - upar
                      if j < 6:
                          nc.tensor.wait_ge(sA, aa)
                          nc.tensor.transpose(pY[:, 0:TC], NTq[qpar][:], self.c_identf[:]).then_inc(sT, 1)
                          tt += 1
                          nc.scalar.wait_ge(sT, tt)
                          nc.scalar.copy(Tq[:], pY[:, 0:TC]).then_inc(sA, 1)
                          aa += 1
                          nc.tensor.wait_ge(sA, aa)
                          nc.tensor.matmul(pY[:, 0:TC], Tq[:], NTq[qpar][:],
                                           start=True, stop=True).then_inc(sT, 1)
                          tt += 1
                          nc.scalar.wait_ge(sT, tt)
                          nc.scalar.copy(NTq[1 - qpar][:], pY[:, 0:TC]).then_inc(sA, 1)
                          aa += 1
                          qpar = 1 - qpar
                  nc.vector.wait_ge(sA, aa)
                  nc.vector.tensor_copy(Ub[:], Uj[upar][:]).then_inc(sV, 1)
                  vv += 1
                  guard(PA)
                  nc.tensor.wait_ge(sV, vv)
                  nc.tensor.matmul(pk[PA][:, 0:TC], Ub[:], Wb[:], start=True, stop=False)
                  nc.tensor.matmul(pk[PA][:, 0:TC], S0b[:], Qb[0:64, :],
                                   start=False, stop=True).then_inc(sT, 1)
                  tt += 1
                  nc.scalar.wait_ge(sT, tt)
                  nc.scalar.activation(o2b[:], pk[PA][:, 0:TC], AF.Square, bias=0.0, scale=1.0).then_inc(sA, 1)
                  aa += 1
                  nc.scalar.wait_ge(sA, aa)
                  nc.scalar.copy(osb[:], pk[PA][:, 0:TC]).then_inc(sA, 1)
                  aa += 1
                  bk[PA] = (sA, aa)
                  guard(PB)
                  nc.tensor.wait_ge(sA, aa)
                  nc.tensor.matmul(pk[PB][0:1, 0:TC], self.c_ones128c[:], o2b[:],
                                   start=True, stop=True).then_inc(sT, 1)
                  tt += 1
                  nc.vector.wait_ge(sT, tt)
                  nc.vector.tensor_scalar(rb2[:], pk[PB][0:1, 0:TC], 1.0 / DVH, 1e-5,
                                          OP.mult, OP.add).then_inc(sV, 1)
                  vv += 1
                  bk[PB] = (sV, vv)
                  nc.scalar.wait_ge(sV, vv)
                  nc.scalar.sqrt(rb2[:], rb2[:]).then_inc(sA, 1)
                  aa += 1
                  nc.vector.wait_ge(sA, aa)
                  nc.vector.reciprocal(rb2[:], rb2[:]).then_inc(sV, 1)
                  vv += 1
                  guard(PC)
                  nc.tensor.wait_ge(sV, vv)
                  nc.tensor.matmul(pk[PC][:, 0:TC], self.c_ones1[:], rb2[:],
                                   start=True, stop=True).then_inc(sT, 1)
                  tt += 1
                  nc.vector.wait_ge(sT, tt)
                  nc.vector.scalar_tensor_tensor(onb[:], osb[:], self.bc_gnw[:, 0:1],
                                                 pk[PC][:, 0:TC], OP.mult, OP.mult).then_inc(sV, 1)
                  vv += 1
                  bk[PC] = (sV, vv)
                  nc.vector.wait_ge(sV, vv)
                  nc.vector.tensor_mul(ostage[:, h * TC:(h + 1) * TC], onb[:], gh).then_inc(sV, 1)
                  vv += 1
                  guard(PD)
                  nc.tensor.matmul(pk[PD][:, 0:1], expdec[:], self.c_eye12[:, h:h + 1],
                                   start=True, stop=True).then_inc(sT, 1)
                  tt += 1
                  nc.scalar.wait_ge(sT, tt)
                  nc.scalar.copy(dcs[:], pk[PD][:, 0:1]).then_inc(sA, 1)
                  aa += 1
                  bk[PD] = (sA, aa)
                  nc.tensor.wait_ge(sA, aa)
                  nc.tensor.transpose(pX[:, 0:64], kh, self.c_identb[rq:rq + 64, rq:rq + 64]).then_inc(sT, 1)
                  tt += 1
                  nc.vector.wait_ge(sT, tt)
                  nc.vector.tensor_scalar(KtT[:], pX[:, 0:64], dcs[:, 0:1], None,
                                          OP.mult).then_inc(sV, 1)
                  vv += 1
                  guard(PF)
                  nc.tensor.wait_ge(sV, vv)
                  nc.tensor.matmul(pk[PF][0:64, 0:DVH], KtT[:], Ub[:], start=True, stop=True).then_inc(sT, 1)
                  tt += 1
                  guard(PE)
                  nc.tensor.matmul(pk[PE][:, 0:1], oh, expg[:, TC - 1:TC], start=True, stop=True).then_inc(sT, 1)
                  tt += 1
                  nc.scalar.wait_ge(sT, tt)
                  nc.scalar.copy(eGb[:], pk[PE][:, 0:1]).then_inc(sA, 1)
                  aa += 1
                  bk[PE] = (sA, aa)
                  nc.vector.wait_ge(sT, tt)
                  nc.vector.wait_ge(sA, aa)
                  nc.vector.scalar_tensor_tensor(self.Sst[:, h * DVH:(h + 1) * DVH],
                                                 self.Sst[:, h * DVH:(h + 1) * DVH],
                                                 eGb[0:64, 0:1], pk[PF][0:64, 0:DVH],
                                                 OP.mult, OP.add).then_inc(sV, 1)
                  vv += 1
                  bk[PF] = (sV, vv)
              nc.vector.wait_ge(sV, vv)
              wop = 0
              for m in range(8):
                  b = PA if m % 2 == 0 else PB
                  guard(b)
                  for k in range(12):
                      mm = nc.tensor.matmul(pk[b][:, 0:TC], woB[:, k * D + m * 128:k * D + (m + 1) * 128],
                                            ostage[:, k * TC:(k + 1) * TC],
                                            start=(k == 0), stop=(k == 11))
                  mm.then_inc(sT, 1)
                  tt += 1
                  nc.scalar.wait_ge(sT, tt)
                  nc.scalar.copy(s1stg[:, m * 512 + cb * 128:m * 512 + (cb + 1) * 128], pk[b][:, 0:TC]).then_inc(sA, 1)
                  aa += 1
                  bk[b] = (sA, aa)
            nc.scalar.wait_ge(sA, aa)
            nc.scalar.dma_start(out=self.s1z[ts(c, 1), :, :], in_=s1stg[:]).then_inc(sO, 16)
            nc.gpsimd.wait_ge(sO, 16)
            self.clears(sL, self.sem("d_LA"), sT, sV, sA, sO)

    def ph_route(self):
        nc = self.nc
        lg = self.sb("rt_lg", [8, 128], F32)
        pr = self.sb("rt_pr", [128, 8], F32)
        ex = self.sb("rt_ex", [128, 8], F32)
        t8 = self.sb("rt_t8", [128, 8], F32)
        msk = self.sb("rt_m", [128, 8 * 4], F32)
        dmp = self.sb("rt_d", [128, 8], F32)
        st_ = self.sb("rt_s", [128, 8], F32)
        ones8 = self.sb("rt_o8", [128, 8], F32)
        wes = self.sb("rt_we", [128, 8], F32)
        wstg = self.sb("rt_ws", [8, 128], F32)
        pR = self.ps("rt_p", [128, 512], F32)
        sL, sV, sA, sT, sO = self.phase_sems("rt_L", "rt_V", "rt_A", "rt_T", "rt_O")
        nc.vector.memset(ones8[:], 1.0).then_inc(sV, 1)
        NV = 24
        for i in range(S // 128):
            vv = 1 + i * NV
            aa = i * 2
            tb = i * 2
            if i > 0:
                nc.sync.wait_ge(sA, aa)
            nc.sync.wait_ge(sL, i * 16)
            nc.sync.dma_start(out=lg[:], in_=self.gl_d[24:32, i * 128:(i + 1) * 128]).then_inc(sL, 16)
            nc.tensor.wait_ge(sL, i * 16 + 16)
            if i > 0:
                nc.tensor.wait_ge(sA, aa - 1)
                nc.tensor.wait_ge(sV, vv)
            nc.tensor.transpose(pR[:, 0:8], lg[:], self.c_identf[0:8, 0:8]).then_inc(sT, 1)
            nc.scalar.wait_ge(sT, tb + 1)
            nc.scalar.copy(pr[:], pR[:, 0:8]).then_inc(sA, 1)
            aa += 1
            nc.vector.wait_ge(sA, aa)
            nc.vector.reduce_max(st_[:, 0:1], pr[:], AX.X, negate=True).then_inc(sV, 1)
            vv += 1
            nc.scalar.wait_ge(sV, vv)
            nc.scalar.activation(ex[:], pr[:], AF.Exp, bias=st_[:, 0:1], scale=1.0,
                                 accum_out=st_[:, 1:2]).then_inc(sA, 1)
            aa += 1
            nc.vector.wait_ge(sA, aa)
            nc.vector.reciprocal(st_[:, 2:3], st_[:, 1:2]).then_inc(sV, 1)
            vv += 1
            nc.vector.wait_ge(sV, vv)
            nc.vector.tensor_scalar(pr[:], ex[:], st_[:, 2:3], None, OP.mult).then_inc(sV, 1)
            vv += 1
            nc.vector.wait_ge(sV, vv)
            nc.vector.max(t8[:], pr[:]).then_inc(sV, 1)
            vv += 1
            nc.vector.wait_ge(sV, vv)
            nc.vector.tensor_scalar(msk[:, 0:8], pr[:], t8[:, 0:1], None, OP.is_equal).then_inc(sV, 1)
            vv += 1
            nc.vector.wait_ge(sV, vv)
            nc.vector.tensor_tensor_scan(dmp[:], ones8[:], msk[:, 0:8], 0.0,
                                         OP.mult, OP.add).then_inc(sV, 1)
            vv += 1
            nc.vector.wait_ge(sV, vv)
            nc.vector.tensor_scalar(dmp[:], dmp[:], 1.0, None, OP.is_equal).then_inc(sV, 1)
            vv += 1
            nc.vector.wait_ge(sV, vv)
            nc.vector.tensor_mul(msk[:, 8:16], msk[:, 0:8], dmp[:]).then_inc(sV, 1)
            vv += 1
            nc.vector.wait_ge(sV, vv)
            nc.vector.tensor_scalar(msk[:, 16:24], pr[:], t8[:, 1:2], None, OP.is_equal).then_inc(sV, 1)
            vv += 1
            nc.vector.wait_ge(sV, vv)
            nc.vector.scalar_tensor_tensor(msk[:, 16:24], msk[:, 8:16], -1.0, msk[:, 16:24],
                                           OP.mult, OP.add).then_inc(sV, 1)
            vv += 1
            nc.vector.wait_ge(sV, vv)
            nc.vector.tensor_tensor_scan(dmp[:], ones8[:], msk[:, 16:24], 0.0,
                                         OP.mult, OP.add).then_inc(sV, 1)
            vv += 1
            nc.vector.wait_ge(sV, vv)
            nc.vector.tensor_scalar(dmp[:], dmp[:], 1.0, None, OP.is_equal).then_inc(sV, 1)
            vv += 1
            nc.vector.wait_ge(sV, vv)
            nc.vector.tensor_mul(msk[:, 24:32], msk[:, 16:24], dmp[:]).then_inc(sV, 1)
            vv += 1
            nc.vector.wait_ge(sV, vv)
            nc.vector.tensor_add(st_[:, 3:4], t8[:, 0:1], t8[:, 1:2]).then_inc(sV, 1)
            vv += 1
            nc.vector.wait_ge(sV, vv)
            nc.vector.tensor_scalar(st_[:, 4:5], st_[:, 3:4], 1.0, 1e-9, OP.mult, OP.add).then_inc(sV, 1)
            vv += 1
            nc.vector.wait_ge(sV, vv)
            nc.vector.reciprocal(st_[:, 4:5], st_[:, 4:5]).then_inc(sV, 1)
            vv += 1
            nc.vector.wait_ge(sV, vv)
            nc.vector.tensor_scalar(st_[:, 5:6], st_[:, 3:4], st_[:, 4:5], None, OP.mult).then_inc(sV, 1)
            vv += 1
            nc.vector.wait_ge(sV, vv)
            nc.vector.tensor_scalar(st_[:, 6:7], st_[:, 5:6], 1.0, 1e-9, OP.mult, OP.add).then_inc(sV, 1)
            vv += 1
            nc.vector.wait_ge(sV, vv)
            nc.vector.reciprocal(st_[:, 6:7], st_[:, 6:7]).then_inc(sV, 1)
            vv += 1
            nc.vector.wait_ge(sV, vv)
            nc.vector.tensor_scalar(st_[:, 7:8], st_[:, 4:5], st_[:, 6:7], None, OP.mult).then_inc(sV, 1)
            vv += 1
            nc.vector.wait_ge(sV, vv)
            nc.vector.tensor_scalar(wes[:], msk[:, 8:16], t8[:, 0:1], None, OP.mult).then_inc(sV, 1)
            vv += 1
            nc.vector.wait_ge(sV, vv)
            nc.vector.scalar_tensor_tensor(wes[:], msk[:, 24:32], t8[:, 1:2], wes[:],
                                           OP.mult, OP.add).then_inc(sV, 1)
            vv += 1
            nc.vector.wait_ge(sV, vv)
            nc.vector.tensor_scalar(wes[:], wes[:], st_[:, 7:8], None, OP.mult).then_inc(sV, 1)
            vv += 1
            nc.tensor.wait_ge(sV, vv)
            nc.tensor.transpose(pR[0:8, 8:136], wes[:], self.c_identf[:]).then_inc(sT, 1)
            nc.vector.wait_ge(sT, tb + 2)
            if i > 0:
                nc.vector.wait_ge(sO, i * 16)
            nc.vector.tensor_copy(wstg[:], pR[0:8, 8:136]).then_inc(sV, 1)
            vv += 1
            assert vv == 1 + (i + 1) * NV, (vv, i)
            nc.scalar.wait_ge(sV, vv)
            nc.scalar.wait_ge(sO, i * 16)
            nc.scalar.dma_start(out=self.weT_d[:, i * 128:(i + 1) * 128], in_=wstg[:]).then_inc(sO, 16)
        nc.sync.wait_ge(sO, (S // 128) * 16)

    def ph_moe(self):
        nc = self.nc
        xnb = self.sb("m_x", [128, 8 * 512], BF16)
        s1c = self.sb("m_s1", [128, 8 * 512], F32)
        x1c = self.sb("m_x1", [128, 4 * 1024], F32)
        acc = self.sb("m_ac", [128, 8 * 512], F32)
        w1b = self.sb("m_w1", [128, 8 * 2048], BF16)
        w2b = self.sb("m_w2", [128, 32 * 512], BF16)
        h1b = self.sb("m_h1", [128, 32 * 512], BF16)
        webr = self.sb("m_wr", [1, 512], F32)
        weB = self.sb("m_wb", [128, 512], F32)
        mbc = self.sb("m_mb", [128, 40], F32)
        ptok = self.sb("m_pt", [128, 4 * 1024], F32)
        pz = [self.ps("m_p%d" % j, [128, 512], F32) for j in range(4)]
        p2 = [self.ps("m_q%d" % j, [128, 512], F32) for j in range(4)]
        oL, oV, oO = self.phase_sems("m_oL", "m_oV", "m_oO")
        iL, iT, iE, iV, iA, iW = self.phase_sems("m_iL", "m_iT", "m_iE", "m_iV", "m_iA", "m_iW")
        oTf, oEf, oA = self.phase_sems("m_oTf", "m_oEf", "m_oA")
        with self.fori(0, S // 512) as c:
            nc.sync.dma_start(out=xnb[:], in_=self.xn1z[ts(c, 1), :, :]).then_inc(oL, 16)
            nc.sync.dma_start(out=s1c[:], in_=self.s1z[ts(c, 1), :, :]).then_inc(oL, 16)
            nc.scalar.dma_start(out=x1c[:].rearrange("p (t d) -> p t d", t=4),
                                in_=self.x1s.rearrange("(n p) d -> p n d", p=128)[:, ts(c, 4), :]).then_inc(oLA := self.sem("m_oLA"), 16)
            nc.vector.wait_ge(oL, 32)
            nc.vector.memset(acc[:], 0.0).then_inc(oV, 1)
            with self.fori(0, NEXP, depth=1) as e:
                nc.sync.dma_start(out=webr[:], in_=self.weT_d[ts(e, 1), ts(c, 512)]).then_inc(iL, 16)
                nc.scalar.dma_start(out=mbc[:], in_=self.mbboth_d[:, ts(e, 1), :].rearrange("p e n -> p (e n)")).then_inc(iLA := self.sem("m_iLA"), 16)
                for mh in range(2):
                    if mh == 1:
                        nc.sync.wait_ge(iW, 16)
                    nc.sync.dma_start(out=w1b[:].rearrange("p (k m) -> p k m", k=8),
                                      in_=self.mW1_b[ts(e, 8), :, mh * 2048:(mh + 1) * 2048].rearrange("k p m -> p k m")).then_inc(iW, 16)
                    nc.tensor.wait_ge(iW, (mh + 1) * 16)
                    nc.tensor.wait_ge(oV, 1)
                    for mm_ in range(16):
                        m = mh * 16 + mm_
                        b = mm_ % 4
                        gi = m
                        if m >= 4:
                            nc.tensor.wait_ge(iE, m - 3)
                        for k in range(8):
                            mmi = nc.tensor.matmul(pz[b][:], w1b[:, k * 2048 + mm_ * 128:k * 2048 + (mm_ + 1) * 128],
                                                   xnb[:, k * 512:(k + 1) * 512],
                                                   start=(k == 0), stop=(k == 7))
                        mmi.then_inc(iT, 1)
                        nc.scalar.wait_ge(iT, mh * 16 + mm_ + 1)
                        nc.scalar.wait_ge(iLA, 16)
                        nc.scalar.activation(h1b[:, m * 512:(m + 1) * 512], pz[b][:], AF.Gelu,
                                             bias=mbc[:, m:m + 1], scale=1.0).then_inc(iE, 1)
                    if mh == 0:
                        nc.sync.wait_ge(iT, 16)
                nc.tensor.wait_ge(iL, 16)
                nc.tensor.matmul(p2[3][:], self.c_ones1[:], webr[:], start=True, stop=True).then_inc(iT, 1)
                nc.scalar.wait_ge(iT, 33)
                nc.scalar.copy(weB[:], p2[3][:]).then_inc(iA, 1)
                for m2h in range(2):
                    if m2h == 1:
                        nc.sync.wait_ge(iT, 37)
                    nc.sync.wait_ge(iW, 32 + m2h * 16)
                    nc.sync.dma_start(out=w2b[:].rearrange("p (k m) -> p k m", k=32),
                                      in_=self.mW2_b[ts(e, 32), :, m2h * 512:(m2h + 1) * 512].rearrange("k p m -> p k m")).then_inc(iW, 16)
                    nc.tensor.wait_ge(iW, 32 + (m2h + 1) * 16)
                    nc.tensor.wait_ge(iE, 32)
                    for m2 in range(4):
                        d = m2h * 4 + m2
                        b = m2 % 4
                        if d == 3:
                            nc.tensor.wait_ge(iA, 1)
                        if d >= 4:
                            nc.tensor.wait_ge(iV, 2 * d - 7)
                        for k in range(32):
                            mmi = nc.tensor.matmul(p2[b][:], w2b[:, k * 512 + m2 * 128:k * 512 + (m2 + 1) * 128],
                                                   h1b[:, k * 512:(k + 1) * 512],
                                                   start=(k == 0), stop=(k == 31))
                        mmi.then_inc(iT, 1)
                        nc.vector.wait_ge(iT, 33 + m2h * 4 + m2 + 1)
                        nc.vector.wait_ge(iA, 1)
                        if d >= 1:
                            nc.vector.wait_ge(iV, 2 * d)
                        nc.vector.scalar_tensor_tensor(ptok[:, 0:512], p2[b][:], mbc[:, 32 + d:33 + d],
                                                       weB[:], OP.add, OP.mult).then_inc(iV, 1)
                        nc.vector.wait_ge(iV, m2h * 8 + m2 * 2 + 1)
                        nc.vector.tensor_add(acc[:, d * 512:(d + 1) * 512],
                                             acc[:, d * 512:(d + 1) * 512], ptok[:, 0:512]).then_inc(iV, 1)
                nc.gpsimd.wait_ge(iV, 16)
                self.clears(iL, self.sem("m_iLA"), iT, iE, iV, iA, iW)
            nc.scalar.wait_ge(oV, 1)
            nc.scalar.activation(acc[:], acc[:], AF.Silu, bias=0.0, scale=1.0).then_inc(oA, 1)
            nc.vector.wait_ge(oA, 1)
            nc.vector.tensor_mul(s1c[:], acc[:], s1c[:]).then_inc(oV, 1)
            nc.tensor.wait_ge(oV, 2)
            ntr = 0
            for tt2 in range(4):
                for e2 in range(8):
                    b = ntr % 2
                    if ntr >= 2:
                        nc.tensor.wait_ge(oEf, ntr - 1)
                    nc.tensor.transpose(pz[b][:, 0:128],
                                        s1c[:, e2 * 512 + tt2 * 128:e2 * 512 + (tt2 + 1) * 128],
                                        self.c_identf[:]).then_inc(oTf, 1)
                    nc.scalar.wait_ge(oTf, ntr + 1)
                    nc.scalar.copy(ptok[:, tt2 * 1024 + e2 * 128:tt2 * 1024 + (e2 + 1) * 128],
                                   pz[b][:, 0:128]).then_inc(oEf, 1)
                    ntr += 1
            nc.vector.wait_ge(oEf, 32)
            nc.vector.wait_ge(self.sem("m_oLA"), 16)
            nc.vector.tensor_add(x1c[:], x1c[:], ptok[:]).then_inc(oV, 1)
            nc.scalar.wait_ge(oV, 3)
            nc.scalar.dma_start(out=self.out.rearrange("(n p) d -> p n d", p=128)[:, ts(c, 4), :],
                                in_=x1c[:].rearrange("p (t d) -> p t d", t=4)).then_inc(oO, 16)
            nc.gpsimd.wait_ge(oO, 16)
            self.clears(oL, self.sem("m_oLA"), oV, oO, oTf, oEf, oA)

    def finish(self, dbg=()):
        nc = self.nc
        self._semmap = {}
        self._semidx = 0
        sF = self.sem("finS")
        n = 0
        for name in dbg:
            t = getattr(self, name)
            o = nc.dram_tensor("dbg_" + name, list(t.shape), t.dtype, kind="ExternalOutput").ap()
            nc.sync.dma_start(out=o[:], in_=t[:]).then_inc(sF, 16)
            n += 1
        if n:
            nc.sync.wait_ge(sF, 16 * n)


def build(segment="all", dbg=()):
    nc = bass.Bass("TRN2", target_bir_lowering=False)
    b = Builder(nc)
    b.declare_io()
    phases = Builder.PHASES if segment == "all" else segment
    for s in b._pool:
        nc.gpsimd.sem_clear(s)
    b.bar()
    for p in phases:
        b.begin_phase()
        with nc.named_scope(p):
            getattr(b, "ph_" + p)()
            b.bar()
        b.end_phase()
    b.finish(dbg)
    return nc


def _constants():
    import ml_dtypes
    c = {}
    c["c_identf"] = np.eye(128, dtype=np.float32)
    c["c_identb"] = np.eye(128, dtype=ml_dtypes.bfloat16)
    p = np.arange(128)[:, None]
    f = np.arange(128)[None, :]
    negI = np.where(p <= f, 0.0, -1e30).astype(np.float32)
    negS = np.where(p < f, 0.0, -1e30).astype(np.float32)
    c["c_negtI"] = negI.T.copy()
    c["c_negtS"] = negS.T.copy()
    for nm, n in [("c_oneh16", 16), ("c_oneh12", 12), ("c_oneh8", 8)]:
        oh = np.zeros((n, n * 128), np.float32)
        for h in range(n):
            oh[h, h * 128:(h + 1) * 128] = 1.0
        c[nm] = oh
    c["c_eye12"] = np.eye(12, dtype=np.float32)
    c["c_ones1"] = np.ones((1, 128), np.float32)
    c["c_ones128c"] = np.ones((128, 1), np.float32)
    s2 = np.zeros((128, 2), np.float32)
    s2[:64, 0] = 1.0
    s2[64:, 1] = 1.0
    c["c_sel2"] = s2
    b2 = np.zeros((2, 128), np.float32)
    b2[0, :64] = 1.0
    b2[1, 64:] = 1.0
    c["c_bc2"] = b2
    return c


def make_in_maps(inputs):
    consts = _constants()
    in_maps = []
    for b in range(8):
        m = {}
        for k, v in inputs.items():
            if k == "L0_attn_x":
                m[k] = np.ascontiguousarray(v[b])
            else:
                m[k] = np.ascontiguousarray(v)
        m.update(consts)
        in_maps.append(m)
    return in_maps


def kernel(**inputs):
    nc = build("all")
    res = run_bass_kernel_spmd(nc, make_in_maps(inputs), core_ids=list(range(8)))
    return np.stack([res.results[b]["out"] for b in range(8)], axis=0)


# version 04 (final)
# speedup vs baseline: 1.0419x; 1.0356x over previous
import numpy as np
import contextlib
from contextlib import contextmanager
import concourse.bass as bass
import concourse.mybir as mybir
from concourse.bass_utils import run_bass_kernel_spmd
from concourse.expressions_rust import RuntimeValue

F32 = mybir.dt.float32
BF16 = mybir.dt.bfloat16
AF = mybir.ActivationFunctionType
OP = mybir.AluOpType
AX = mybir.AxisListType
ts = bass.ts

S = 8192
D = 1024
E0 = 2048
N0 = 16
DTR = 128
NH, DKH, DVH = 12, 64, 128
KD, VD = NH * DKH, NH * DVH
FF = 4096
NEXP = 8
TC = 128
NCH = S // TC


def set_seq(s):
    global S, NCH
    S = s
    NCH = S // TC


class Builder:
    PHASES = ["wcast", "ln0", "zin", "conv0", "xproj", "dtproj", "scan", "wout",
              "tern", "ffn0", "resid1", "qkvg", "convq", "delta", "route", "moe"]

    def __init__(self, nc):
        self.nc = nc
        self._semn = {}
        self._pool = [nc.alloc_semaphore('gp%d' % j) for j in range(20)]
        self._newsems = []
        self._stack = None
        self._loopregs = []
        self._cq_r1 = {}
        self._cq_qh = {}
        self._cq_tapv = 0

    def begin_phase(self):
        self._stack = contextlib.ExitStack()
        self._semmap = {}
        self._semidx = 0

    def end_phase(self):
        if self._stack is not None:
            self._stack.close()
            self._stack = None
        for s in self._pool:
            self.nc.gpsimd.sem_clear(s)
        self.bar()

    def sem(self, name):
        if name not in self._semmap:
            assert self._semidx < len(self._pool), "sem pool exhausted"
            self._semmap[name] = self._pool[self._semidx]
            self._semidx += 1
        return self._semmap[name]

    def phase_sems(self, *names):
        return tuple(self.sem(n) for n in names)

    def sbp(self, name, shape, dtype):
        return self.nc.alloc_sbuf_tensor(name, shape, dtype)

    def sb(self, name, shape, dtype):
        if self._stack is None:
            return self.nc.alloc_sbuf_tensor(name, shape, dtype)
        return self._stack.enter_context(self.nc.sbuf_tensor(name, shape, dtype))

    def ps(self, name, shape, dtype=F32):
        if self._stack is None:
            return self.nc.alloc_psum_tensor(name, shape, dtype)
        return self._stack.enter_context(self.nc.psum_tensor(name, shape, dtype))

    def bar(self):
        self.nc.all_engine_barrier()

    @contextmanager
    def fori(self, start, end, step=1, depth=0):
        nc = self.nc
        while len(self._loopregs) <= depth:
            j = len(self._loopregs)
            self._loopregs.append((
                nc.alloc_registers("kf_i%d" % j, engines=mybir.ALL_ENGINES),
                nc.alloc_registers("kf_s%d" % j, engines=mybir.ALL_ENGINES)))
        R, SN = self._loopregs[depth]
        lid = nc.next_id()
        loop_start, loop_end = "kf%d_loop" % lid, "kf%d_end" % lid
        nc.regs_mov(R, start)
        nc.br(loop_start, engines=mybir.ALL_ENGINES)
        with nc.body(loop_start, valid_engines=mybir.ALL_ENGINES):
            nc.regs_mov(SN, R)
            yield RuntimeValue(SN, min_val=start, max_val=end - step)
            nc.regs_alu(R, R, step, op=OP.add)
            nc.br_lt(R, end, on_true=loop_start, on_false=loop_end,
                     engines=mybir.ALL_ENGINES)
        nc.switch_bb(loop_end)

    def clears(self, *sems):
        self.bar()
        for s in sems:
            self.nc.gpsimd.sem_clear(s)
        self.bar()

    def declare_io(self):
        nc = self.nc
        dram = lambda n, s, d=F32, k="ExternalInput": nc.dram_tensor(n, s, d, kind=k).ap()
        self.x = dram("L0_attn_x", [S, D])
        self.W_in = dram("L0_attn_W_in", [D, 2 * E0])
        self.b_in = dram("L0_attn_b_in", [2 * E0])
        self.conv_w = dram("L0_attn_conv_w", [E0, 1, 4])
        self.conv_b = dram("L0_attn_conv_b", [E0])
        self.x_proj_w = dram("L0_attn_x_proj_w", [E0, DTR + 2 * N0])
        self.dt_proj_w = dram("L0_attn_dt_proj_w", [DTR, E0])
        self.dt_bias = dram("L0_attn_dt_bias", [E0])
        self.A_log = dram("L0_attn_A_log", [E0, N0])
        self.D_param = dram("L0_attn_D_param", [E0])
        self.W_out = dram("L0_attn_W_out", [E0, D])
        self.b_out = dram("L0_attn_b_out", [D])
        self.fW1 = dram("L0_ffn_W1", [FF, D])
        self.fb1 = dram("L0_ffn_b1", [FF])
        self.fW2 = dram("L0_ffn_W2", [D, FF])
        self.fb2 = dram("L0_ffn_b2", [D])
        self.Wq = dram("L1_attn_Wq", [D, KD])
        self.Wk = dram("L1_attn_Wk", [D, KD])
        self.Wv = dram("L1_attn_Wv", [D, VD])
        self.Wg = dram("L1_attn_Wg", [D, VD])
        self.Wo = dram("L1_attn_Wo", [VD, D])
        self.conv_q = dram("L1_attn_conv_q", [KD, 4])
        self.conv_k = dram("L1_attn_conv_k", [KD, 4])
        self.conv_v = dram("L1_attn_conv_v", [VD, 4])
        self.Wgk = dram("L1_attn_Wgk", [D, NH])
        self.Wb = dram("L1_attn_Wb", [D, NH])
        self.b_b = dram("L1_attn_b_b", [NH])
        self.A1_log = dram("L1_attn_A_log", [NH])
        self.dt1_bias = dram("L1_attn_dt_bias", [NH])
        self.gnw = dram("L1_attn_g_norm_weight", [DVH])
        self.Wr = dram("L1_ffn_Wr", [D, NEXP])
        self.br = dram("L1_ffn_br", [NEXP])
        self.mW1 = dram("L1_ffn_W1", [NEXP, D, FF])
        self.mb1 = dram("L1_ffn_b1", [NEXP, FF])
        self.mW2 = dram("L1_ffn_W2", [NEXP, FF, D])
        self.mb2 = dram("L1_ffn_b2", [NEXP, D])
        self.identf = dram("c_identf", [128, 128])
        self.identb = dram("c_identb", [128, 128], BF16)
        self.negtI = dram("c_negtI", [128, 128])
        self.negtS = dram("c_negtS", [128, 128])
        self.negIr = dram("c_negIr", [128, 128])
        self.negSr = dram("c_negSr", [128, 128])
        self.oneh16 = dram("c_oneh16", [16, 16 * 128])
        self.oneh16b = dram("c_oneh16b", [16, 16 * 128], BF16)
        self.oneh12 = dram("c_oneh12", [12, 12 * 128])
        self.oneh8 = dram("c_oneh8", [8, 8 * 128])
        self.eye12 = dram("c_eye12", [12, 12])
        self.ones1 = dram("c_ones1", [1, 128])
        self.ones128c = dram("c_ones128c", [128, 1])
        self.sel2 = dram("c_sel2", [128, 2])
        self.bc2 = dram("c_bc2", [2, 128])
        self.out = dram("out", [S, D], F32, "ExternalOutput")

        scr = lambda n, s, d: nc.dram_tensor(n, s, d, kind="Internal").ap()
        self.xnz = scr("s_xnz", [S // 512, 128, 8 * 512], BF16)
        self.zz = scr("s_zz", [S // 512, 128, 32 * 512], F32)
        self.xc32 = scr("s_xc32", [16, 128, S], F32)
        self.delta32 = scr("s_delta32", [16, 128, S], F32)
        self.yT = scr("s_yT", [16, 128, S], BF16)
        self.s0T = scr("s_s0T", [8, 128, S], F32)
        self.g0z = scr("s_g0z", [S // 512, 128, 8 * 512], F32)
        self.x1s = scr("s_x1", [S, D], F32)
        self.xn1z = scr("s_xn1z", [S // 512, 128, 8 * 512], BF16)
        self.qkrz = scr("s_qkrz", [S // 512, 128, 12 * 512], BF16)
        self.vrz = scr("s_vrz", [S // 512, 128, 12 * 512], BF16)
        self.qkhz = scr("s_qkhz", [S // 512, 128, 12 * 512], BF16)
        self.vgz = scr("s_vgz", [S // 512, 128, 24 * 512], BF16)
        self.gl_d = scr("s_gl", [32, S], F32)
        self.BC = scr("s_BC", [16, 2 * S], F32)
        self.dtT = scr("s_dtT", [128, S], BF16)
        self.apack = scr("s_apack", [16, 128, 17], F32)
        self.mbboth_d = scr("s_mbboth", [128, NEXP, 40], F32)
        self.s1z = scr("s_s1z", [S // 512, 128, 8 * 512], F32)
        self.weT_d = scr("s_weT", [NEXP, S], F32)
        self.W_in_b = scr("w_in_b", [8, 128, 2 * E0], BF16)
        self.xproj_b = scr("w_xproj_b", [16, 128, DTR + 2 * N0], BF16)
        self.dtproj_b = scr("w_dtproj_b", [1, 128, E0], BF16)
        self.Wout_b = scr("w_wout_b", [16, 128, D], BF16)
        self.W1t_b = scr("w_w1t_b", [8, 128, FF], BF16)
        self.W2t_b = scr("w_w2t_b", [32, 128, D], BF16)
        self.Wq_b = scr("w_wq_b", [8, 128, KD], BF16)
        self.Wk_b = scr("w_wk_b", [8, 128, KD], BF16)
        self.Wv_b = scr("w_wv_b", [8, 128, VD], BF16)
        self.Wg_b = scr("w_wg_b", [8, 128, VD], BF16)
        self.Wo_b = scr("w_wo_b", [12, 128, D], BF16)
        self.Wgk_b = scr("w_wgk_b", [8, 128, NH], BF16)
        self.Wb_b = scr("w_wb_b", [8, 128, NH], BF16)
        self.Wr_b = scr("w_wr_b", [8, 128, NEXP], BF16)
        self.mW1_b = scr("w_mw1_b", [NEXP * 8, 128, FF], BF16)
        self.mW2_b = scr("w_mw2_b", [NEXP * 32, 128, D], BF16)

        sb = self.sbp
        self.c_identf = sb("k_identf", [128, 128], F32)
        self.c_identb = sb("k_identb", [128, 128], BF16)
        self.c_negtI = sb("k_negtI", [128, 128], F32)
        self.c_negtS = sb("k_negtS", [128, 128], F32)
        self.c_negIr = sb("k_negIr", [128, 128], F32)
        self.c_negSr = sb("k_negSr", [128, 128], F32)
        self.c_oneh16b = sb("k_oneh16b", [16, 16 * 128], BF16)
        self.c_oneh12 = sb("k_oneh12", [12, 12 * 128], F32)
        self.c_oneh8 = sb("k_oneh8", [8, 8 * 128], F32)
        self.c_eye12 = sb("k_eye12", [12, 12], F32)
        self.c_ones1 = sb("k_ones1", [1, 128], F32)
        self.c_ones128c = sb("k_ones128c", [128, 1], F32)
        self.c_sel2 = sb("k_sel2", [128, 2], F32)
        self.c_bc2 = sb("k_bc2", [2, 128], F32)
        self.bc_b_in = sb("bc_b_in", [128, 32], F32)
        self.bc_conv_b = sb("bc_conv_b", [128, 16], F32)
        self.bc_dt_bias = sb("bc_dt_bias", [128, 16], F32)
        self.bc_D = sb("bc_D", [128, 16], F32)
        self.bc_b_out = sb("bc_b_out", [128, 8], F32)
        self.bc_fb1 = sb("bc_fb1", [128, 32], F32)
        self.bc_fb2 = sb("bc_fb2", [128, 8], F32)
        self.bc_mb1 = sb("bc_mb1", [128, NEXP * 32], F32)
        self.bc_mb2 = sb("bc_mb2", [128, NEXP * 8], F32)
        self.bc_gnw = sb("bc_gnw", [128, 1], F32)
        self.bc_cw0 = sb("bc_cw0", [128, 16 * 4], F32)
        self.bc_cwq = sb("bc_cwq", [128, 6 * 4], F32)
        self.bc_cwk = sb("bc_cwk", [128, 6 * 4], F32)
        self.bc_cwv = sb("bc_cwv", [128, 12 * 4], F32)
        self.A0col = sb("A0col", [128, 16 * 16], F32)
        self.r_bb = sb("r_bb", [12, 1], F32)
        self.r_dtb = sb("r_dtb", [12, 1], F32)
        self.r_negA = sb("r_negA", [12, 1], F32)
        self.Sst = sb("Sst", [64, NH * DVH], F32)

    def _bias_cols(self, dst, src1d, ncols, col0=0):
        nc = self.nc
        st, pt = self._wc_stage, self._wc_pt
        sL, sPt, sPv = self.sem("wcL"), self.sem("wcPt"), self.sem("wcPv")
        self._wcl += 1
        nc.sync.wait_ge(sL, max(0, 16 * self._wcl - 16))
        nc.sync.dma_start(out=st[0:ncols, 0:128],
                          in_=src1d.rearrange("(a b) -> a b", b=128)).then_inc(sL, 16)
        nc.tensor.wait_ge(sL, 16 * self._wcl)
        nc.tensor.transpose(pt[0:128, 0:ncols], st[0:ncols, 0:128],
                            self.c_identf[0:ncols, 0:ncols]).then_inc(sPt, 1)
        self._wcpt += 1
        nc.vector.wait_ge(sPt, self._wcpt)
        nc.vector.tensor_copy(dst[:, col0:col0 + ncols], pt[0:128, 0:ncols]).then_inc(sPv, 1)
        self._wcpv += 1
        nc.tensor.wait_ge(sPv, self._wcpv)
        nc.sync.wait_ge(sPv, self._wcpv)

    def _cast_unroll(self, src2d, dst3d, ktiles, w):
        nc = self.nc
        sC = self.sem("wcC")
        for k in range(ktiles):
            c = self._wcc
            buf = self._wc_cin[c % 2]
            obuf = self._wc_cout[c % 2]
            if c >= 1:
                nc.sync.wait_ge(sC, (c - 1) * 48 + 32)
            nc.sync.dma_start(out=buf[:, 0:w], in_=src2d[k * 128:(k + 1) * 128, 0:w]).then_inc(sC, 16)
            nc.vector.wait_ge(sC, c * 48 + 16)
            nc.vector.tensor_copy(obuf[:, 0:w], buf[:, 0:w]).then_inc(sC, 16)
            nc.scalar.wait_ge(sC, c * 48 + 32)
            nc.scalar.dma_start(out=dst3d[k, :, 0:w], in_=obuf[:, 0:w]).then_inc(sC, 16)
            self._wcc += 1

    def ph_wcast(self):
        nc = self.nc
        sL = self.phase_sems("wcL", "wcPt", "wcPv", "wcPa", "wcC", "wcCV", "wcCS")[0]
        self._wcl = 0
        self._wcpt = 0
        self._wcpv = 0
        self._wcpa = 0
        self._wcc = 0
        for dst, src in [(self.c_identf, self.identf), (self.c_identb, self.identb),
                         (self.c_negtI, self.negtI), (self.c_negtS, self.negtS),
                         (self.c_negIr, self.negIr), (self.c_negSr, self.negSr),
                         (self.c_oneh16b, self.oneh16b), (self.c_oneh12, self.oneh12),
                         (self.c_oneh8, self.oneh8), (self.c_eye12, self.eye12),
                         (self.c_ones1, self.ones1), (self.c_ones128c, self.ones128c),
                         (self.c_sel2, self.sel2), (self.c_bc2, self.bc2)]:
            nc.sync.wait_ge(sL, max(0, 16 * self._wcl - 16))
            nc.sync.wait_ge(sL, max(0, 16 * self._wcl))
            nc.sync.dma_start(out=dst[:], in_=src[:]).then_inc(sL, 16)
            self._wcl += 1
        self._wc_stage = self.sb("wc_stage", [128, 128], F32)
        self._wc_pt = self.ps("wc_pt", [128, 128], F32)
        nc.tensor.wait_ge(sL, 16 * self._wcl)
        self._bias_cols(self.bc_b_in, self.b_in, 32)
        self._bias_cols(self.bc_conv_b, self.conv_b, 16)
        self._bias_cols(self.bc_dt_bias, self.dt_bias, 16)
        self._bias_cols(self.bc_D, self.D_param, 16)
        self._bias_cols(self.bc_b_out, self.b_out, 8)
        self._bias_cols(self.bc_fb1, self.fb1, 32)
        self._bias_cols(self.bc_fb2, self.fb2, 8)
        for e in range(NEXP):
            self._bias_cols(self.bc_mb1, self.mb1[e, :], 32, e * 32)
            self._bias_cols(self.bc_mb2, self.mb2[e, :], 8, e * 8)
        sPv = self.sem("wcPv")
        for (dst, src, ntile) in [(self.bc_cw0, self.conv_w.rearrange("c o j -> (c o) j"), 16),
                                  (self.bc_cwq, self.conv_q, 6), (self.bc_cwk, self.conv_k, 6),
                                  (self.bc_cwv, self.conv_v, 12)]:
            for t in range(ntile):
                self._wcl += 1
                nc.sync.wait_ge(sL, max(0, 16 * self._wcl - 16))
                nc.sync.dma_start(out=self._wc_stage[0:128, 0:4],
                                  in_=src[t * 128:(t + 1) * 128, :]).then_inc(sL, 16)
                nc.vector.wait_ge(sL, 16 * self._wcl)
                nc.vector.tensor_copy(dst[:, t * 4:(t + 1) * 4],
                                      self._wc_stage[0:128, 0:4]).then_inc(sPv, 1)
                self._wcpv += 1
                nc.sync.wait_ge(sPv, self._wcpv)
        self._wcl += 1
        nc.sync.wait_ge(sL, max(0, 16 * self._wcl - 16))
        nc.sync.dma_start(out=self.bc_gnw[:, 0:1],
                          in_=self.gnw.rearrange("(a b) -> a b", b=1)).then_inc(sL, 16)
        a0s = self.sb("wc_a0", [128, 16 * 16], F32)
        for t in range(16):
            self._wcl += 1
            nc.sync.wait_ge(sL, max(0, 16 * self._wcl - 16))
            nc.sync.dma_start(out=a0s[:, t * 16:(t + 1) * 16],
                              in_=self.A_log[t * 128:(t + 1) * 128, :]).then_inc(sL, 16)
        sPa = self.sem("wcPa")
        nc.scalar.wait_ge(sL, 16 * self._wcl)
        nc.scalar.activation(self.A0col[:], a0s[:], AF.Exp, bias=0.0, scale=1.0).then_inc(sPa, 1)
        self._wcpa += 1
        nc.vector.wait_ge(sPa, self._wcpa)
        nc.vector.tensor_scalar_mul(self.A0col[:], self.A0col[:], -1.0).then_inc(sPv, 1)
        self._wcpv += 1
        nc.vector.wait_ge(sPv, self._wcpv)
        s12 = self.sb("wc_s12", [12, 3], F32)
        for j, src in enumerate([self.b_b, self.dt1_bias, self.A1_log]):
            self._wcl += 1
            nc.sync.wait_ge(sL, max(0, 16 * self._wcl - 16))
            nc.sync.dma_start(out=s12[0:12, j:j + 1],
                              in_=src.rearrange("(a b) -> a b", b=1)).then_inc(sL, 16)
        nc.vector.wait_ge(sL, 16 * self._wcl)
        nc.vector.tensor_copy(self.r_bb[:], s12[0:12, 0:1])
        nc.vector.tensor_copy(self.r_dtb[:], s12[0:12, 1:2]).then_inc(sPv, 1)
        self._wcpv += 1
        nc.scalar.wait_ge(sPv, self._wcpv)
        nc.scalar.activation(self.r_negA[:], s12[0:12, 2:3], AF.Exp, bias=0.0, scale=1.0).then_inc(sPa, 1)
        self._wcpa += 1
        nc.vector.wait_ge(sPa, self._wcpa)
        nc.vector.tensor_scalar_mul(self.r_negA[:], self.r_negA[:], -1.0).then_inc(sPv, 1)
        self._wcpv += 1
        nc.sync.wait_ge(sPv, self._wcpv)
        st17 = self.sb("wc_a17", [128, 17], F32)
        for e in range(16):
            if e > 0:
                nc.vector.wait_ge(sL, 16 * self._wcl)
            nc.vector.tensor_copy(st17[:, 0:16], self.A0col[:, e * 16:(e + 1) * 16])
            nc.vector.tensor_copy(st17[:, 16:17], self.bc_D[:, e:e + 1]).then_inc(sPv, 1)
            self._wcpv += 1
            nc.sync.wait_ge(sPv, self._wcpv)
            nc.sync.dma_start(out=self.apack[e, :, :], in_=st17[:]).then_inc(sL, 16)
            self._wcl += 1
        for e in range(NEXP):
            nc.sync.wait_ge(sL, 16 * self._wcl)
            nc.sync.dma_start(out=self.mbboth_d[:, e, 0:32],
                              in_=self.bc_mb1[:, e * 32:(e + 1) * 32]).then_inc(sL, 16)
            nc.sync.wait_ge(sL, 16 * self._wcl + 16)
            nc.sync.dma_start(out=self.mbboth_d[:, e, 32:40],
                              in_=self.bc_mb2[:, e * 8:(e + 1) * 8]).then_inc(sL, 16)
            self._wcl += 2
        nc.sync.wait_ge(sL, 16 * self._wcl)
        self.bar()
        self._wc_cin = [self.sb("wc_cin%d" % j, [128, 4096], F32) for j in range(2)]
        self._wc_cout = [self.sb("wc_cout%d" % j, [128, 4096], BF16) for j in range(2)]
        U = self._cast_unroll
        U(self.W_in, self.W_in_b, 8, 2 * E0)
        U(self.x_proj_w, self.xproj_b, 16, DTR + 2 * N0)
        U(self.dt_proj_w, self.dtproj_b, 1, E0)
        U(self.W_out, self.Wout_b, 16, D)
        U(self.Wq, self.Wq_b, 8, KD)
        U(self.Wk, self.Wk_b, 8, KD)
        U(self.Wv, self.Wv_b, 8, VD)
        U(self.Wg, self.Wg_b, 8, VD)
        U(self.Wo, self.Wo_b, 12, D)
        U(self.Wgk, self.Wgk_b, 8, NH)
        U(self.Wb, self.Wb_b, 8, NH)
        U(self.Wr, self.Wr_b, 8, NEXP)
        nc.vector.wait_ge(self.sem("wcC"), self._wcc * 48)
        self.bar()
        U(self.mW1.rearrange("e d f -> (e d) f"), self.mW1_b, NEXP * 8, FF)
        U(self.mW2.rearrange("e f d -> (e f) d"), self.mW2_b, NEXP * 32, D)
        nc.vector.wait_ge(self.sem("wcC"), self._wcc * 48)

    def _ln_tile(self, xt, xn, dump, stats, sV, sA, vb, ab):
        nc = self.nc
        nc.scalar.activation(dump[:, 0:D], xt[:], AF.Square, bias=0.0, scale=1.0,
                             accum_out=stats[:, 0:1]).then_inc(sA, 1)
        nc.scalar.wait_ge(sA, ab + 1)
        nc.scalar.activation(dump[:, D:2 * D], xt[:], AF.Identity, bias=0.0, scale=1.0,
                             accum_out=stats[:, 7:8]).then_inc(sA, 1)
        nc.vector.wait_ge(sA, ab + 2)
        nc.vector.tensor_scalar(stats[:, 1:2], stats[:, 7:8], stats[:, 7:8],
                                -1.0 / D, OP.mult, OP.mult).then_inc(sV, 1)
        nc.vector.wait_ge(sV, vb + 1)
        nc.vector.scalar_tensor_tensor(stats[:, 2:3], stats[:, 1:2], 1.0,
                                       stats[:, 0:1], OP.mult, OP.add).then_inc(sV, 1)
        nc.vector.wait_ge(sV, vb + 2)
        nc.vector.tensor_scalar(stats[:, 3:4], stats[:, 2:3], 1.0 / D, 1e-5,
                                OP.mult, OP.add).then_inc(sV, 1)
        nc.scalar.wait_ge(sV, vb + 3)
        nc.scalar.activation(stats[:, 4:5], stats[:, 3:4], AF.Ln, bias=0.0, scale=1.0).then_inc(sA, 1)
        nc.scalar.wait_ge(sA, ab + 3)
        nc.scalar.activation(stats[:, 5:6], stats[:, 4:5], AF.Exp, bias=0.0, scale=-0.5).then_inc(sA, 1)
        nc.vector.wait_ge(sA, ab + 4)
        nc.vector.tensor_scalar(stats[:, 6:7], stats[:, 7:8], stats[:, 5:6],
                                -1.0 / D, OP.mult, OP.mult).then_inc(sV, 1)
        nc.scalar.wait_ge(sV, vb + 4)
        nc.scalar.activation(xn[:], xt[:], AF.Identity, bias=stats[:, 6:7],
                             scale=stats[:, 5:6]).then_inc(sA, 1)

    def ph_ln0(self):
        nc = self.nc
        xt = self.sb("l0_xt", [128, D], F32)
        dump = self.sb("l0_dump", [128, 2 * D], F32)
        xn = self.sb("l0_xn", [128, D], BF16)
        stg = self.sb("l0_stg", [128, 8 * 128], BF16)
        stats = self.sb("l0_st", [128, 8], F32)
        pb = [self.ps("l0_pb%d" % j, [128, 128], BF16) for j in range(2)]
        sL, sV, sA, sT, sE, sO = self.phase_sems(*["l0_" + c for c in "LVATEO"])
        for i in range(S // 128):
            nc.sync.wait_ge(sA, i * 5)
            nc.sync.wait_ge(sL, i * 16)
            nc.sync.dma_start(out=xt[:], in_=self.x[i * 128:(i + 1) * 128, :]).then_inc(sL, 16)
            nc.vector.wait_ge(sL, i * 16 + 16)
            nc.scalar.wait_ge(sL, i * 16 + 16)
            self._ln_tile(xt, xn, dump, stats, sV, sA, i * 4, i * 5)
            nc.tensor.wait_ge(sA, i * 5 + 5)
            for d in range(8):
                j = i * 8 + d
                if j >= 2:
                    nc.tensor.wait_ge(sE, j - 1)
                nc.tensor.transpose(pb[d % 2][:], xn[:, d * 128:(d + 1) * 128],
                                    self.c_identb[:]).then_inc(sT, 1)
                nc.vector.wait_ge(sT, i * 8 + d + 1)
                if d == 0:
                    nc.vector.wait_ge(sO, i * 16)
                nc.vector.tensor_copy(stg[:, d * 128:(d + 1) * 128], pb[d % 2][:]).then_inc(sE, 1)
            nc.sync.wait_ge(sE, i * 8 + 8)
            nc.sync.wait_ge(sO, i * 16)
            nc.sync.dma_start(
                out=self.xnz[i // 4, :, :].rearrange("p (d s) -> p d s", d=8)[:, :, (i % 4) * 128:(i % 4 + 1) * 128],
                in_=stg[:].rearrange("p (d s) -> p d s", d=8)).then_inc(sO, 16)
        nc.sync.wait_ge(sO, (S // 128) * 16)

    def ph_zin(self):
        nc = self.nc
        wz = self.sb("z_w", [128, 8 * 4096], BF16)
        sW, = self.phase_sems("z_W")
        for k in range(8):
            nc.scalar.dma_start(out=wz[:, k * 4096:(k + 1) * 4096],
                                in_=self.W_in_b[k, :, :]).then_inc(sW, 16)
        nc.tensor.wait_ge(sW, 8 * 16)
        xch = self.sb("z_x", [128, 8 * 512], BF16)
        ostg = self.sb("z_o", [128, 32 * 512], F32)
        pz = [self.ps("z_p%d" % j, [128, 512], F32) for j in range(4)]
        sL, sT, sE = self.phase_sems("z_L", "z_T", "z_E")
        with self.fori(0, S // 512) as i:
            nc.sync.dma_start(out=xch[:], in_=self.xnz[ts(i, 1), :, :]).then_inc(sL, 16)
            nc.tensor.wait_ge(sL, 16)
            for m in range(32):
                b = m % 4
                if m >= 4:
                    nc.tensor.wait_ge(sE, m - 3)
                for k in range(8):
                    mm = nc.tensor.matmul(pz[b][:], wz[:, k * 4096 + m * 128:k * 4096 + (m + 1) * 128],
                                          xch[:, k * 512:(k + 1) * 512],
                                          start=(k == 0), stop=(k == 7))
                mm.then_inc(sT, 1)
                nc.scalar.wait_ge(sT, m + 1)
                fn = AF.Identity if m < 16 else AF.Silu
                nc.scalar.activation(ostg[:, m * 512:(m + 1) * 512], pz[b][:], fn,
                                     bias=self.bc_b_in[:, m:m + 1], scale=1.0).then_inc(sE, 1)
            nc.scalar.wait_ge(sE, 32)
            nc.scalar.dma_start(out=self.zz[ts(i, 1), :, :], in_=ostg[:]).then_inc(sZO := self.sem("z_O"), 16)
            nc.gpsimd.wait_ge(sL, 16)
            nc.gpsimd.wait_ge(sZO, 16)
            self.clears(sL, sT, sE, sZO)

    def ph_conv0(self):
        nc = self.nc
        wk = self.sb("c0_wk", [128, S + 3], F32)
        a1 = self.sb("c0_a1", [128, S], F32)
        a2 = self.sb("c0_a2", [128, S], F32)
        sL, sV, sA, sO = self.phase_sems("c0_L", "c0_V", "c0_A", "c0_O")
        for e in range(16):
            if e > 0:
                nc.sync.wait_ge(sV, e * 4)
            nc.sync.wait_ge(sL, e * 32)
            half = (S // 512) // 2
            for g in range(2):
                nc.sync.dma_start(
                    out=wk[:, 3:S + 3].rearrange("p (c s) -> p c s", c=S // 512)[:, g * half:(g + 1) * half, :],
                    in_=self.zz.rearrange("c p m -> p c m")[:, g * half:(g + 1) * half,
                                                           e * 512:(e + 1) * 512]).then_inc(sL, 16)
            if e == 0:
                nc.vector.memset(wk[:, 0:3], 0.0).then_inc(self.sem("c0_M"), 1)
                nc.vector.wait_ge(self.sem("c0_M"), 1)
            nc.vector.wait_ge(sL, e * 32 + 32)
            if e > 0:
                nc.vector.wait_ge(sA, e * 1)
                nc.vector.wait_ge(sO, e * 16)
            w = self.bc_cw0
            nc.vector.tensor_scalar(a1[:], wk[:, 0:S], w[:, 4 * e:4 * e + 1], None,
                                    OP.mult).then_inc(sV, 1)
            nc.vector.wait_ge(sV, e * 4 + 1)
            nc.vector.scalar_tensor_tensor(a2[:], wk[:, 1:S + 1], w[:, 4 * e + 1:4 * e + 2],
                                           a1[:], OP.mult, OP.add).then_inc(sV, 1)
            nc.vector.wait_ge(sV, e * 4 + 2)
            nc.vector.scalar_tensor_tensor(a1[:], wk[:, 2:S + 2], w[:, 4 * e + 2:4 * e + 3],
                                           a2[:], OP.mult, OP.add).then_inc(sV, 1)
            nc.vector.wait_ge(sV, e * 4 + 3)
            nc.vector.scalar_tensor_tensor(a2[:], wk[:, 3:S + 3], w[:, 4 * e + 3:4 * e + 4],
                                           a1[:], OP.mult, OP.add).then_inc(sV, 1)
            nc.scalar.wait_ge(sV, e * 4 + 4)
            if e > 0:
                nc.scalar.wait_ge(sO, e * 16)
            nc.scalar.activation(a1[:], a2[:], AF.Silu,
                                 bias=self.bc_conv_b[:, e:e + 1], scale=1.0).then_inc(sA, 1)
            nc.scalar.wait_ge(sA, e + 1)
            nc.scalar.wait_ge(sO, e * 16)
            nc.scalar.dma_start(out=self.xc32[e, :, :], in_=a1[:]).then_inc(sO, 16)
        nc.sync.wait_ge(sO, 16 * 16)

    def ph_xproj(self):
        nc = self.nc
        wxp = self.sb("xp_w", [128, 16 * 160], BF16)
        sW, = self.phase_sems("xp_W")
        nc.scalar.dma_start(out=wxp[:].rearrange("p (k m) -> p k m", k=16),
                            in_=self.xproj_b.rearrange("k p m -> p k m")).then_inc(sW, 16)
        nc.tensor.wait_ge(sW, 16)
        nc.vector.wait_ge(sW, 16)
        xcf = self.sb("xp_xf", [128, 16 * 512], F32)
        xcb = self.sb("xp_xb", [128, 16 * 512], BF16)
        dstg = self.sb("xp_d", [128, 512], BF16)
        bstg = self.sb("xp_bc", [16, 1024], F32)
        pd = self.ps("xp_pd", [128, 512], F32)
        pb = self.ps("xp_pb", [16, 512], F32)
        pc = self.ps("xp_pc", [16, 512], F32)
        sL, sT, sEv, sEa, sO = self.phase_sems("xp_L", "xp_T", "xp_Ev", "xp_Ea", "xp_O")
        for i in range(S // 512):
            if i > 0:
                nc.sync.wait_ge(sEv, i * 3 - 2)
            nc.sync.wait_ge(sL, i * 32)
            for g in range(2):
                nc.sync.dma_start(
                    out=xcf[:, g * 8 * 512:(g + 1) * 8 * 512].rearrange("p (e s) -> p e s", e=8),
                    in_=self.xc32.rearrange("e p s -> p e s")[:, g * 8:(g + 1) * 8,
                                                             i * 512:(i + 1) * 512]).then_inc(sL, 16)
            nc.vector.wait_ge(sL, i * 32 + 32)
            if i > 0:
                nc.vector.wait_ge(sT, i * 3)
            nc.vector.tensor_copy(xcb[:], xcf[:]).then_inc(sEv, 1)
            nc.tensor.wait_ge(sEv, i * 3 + 1)
            if i > 0:
                nc.tensor.wait_ge(sEv, i * 3)
                nc.tensor.wait_ge(sEa, i)
            for k in range(16):
                mm = nc.tensor.matmul(pd[:], wxp[:, k * 160:k * 160 + 128],
                                      xcb[:, k * 512:(k + 1) * 512],
                                      start=(k == 0), stop=(k == 15))
            mm.then_inc(sT, 1)
            for k in range(16):
                mm = nc.tensor.matmul(pb[:], wxp[:, k * 160 + 128:k * 160 + 144],
                                      xcb[:, k * 512:(k + 1) * 512],
                                      start=(k == 0), stop=(k == 15))
            mm.then_inc(sT, 1)
            for k in range(16):
                mm = nc.tensor.matmul(pc[:], wxp[:, k * 160 + 144:k * 160 + 160],
                                      xcb[:, k * 512:(k + 1) * 512],
                                      start=(k == 0), stop=(k == 15))
            mm.then_inc(sT, 1)
            nc.scalar.wait_ge(sT, i * 3 + 1)
            if i > 0:
                nc.scalar.wait_ge(sO, i * 48)
            nc.scalar.copy(dstg[:], pd[:]).then_inc(sEa, 1)
            nc.vector.wait_ge(sT, i * 3 + 2)
            if i > 0:
                nc.vector.wait_ge(sO, i * 48)
            nc.vector.tensor_copy(bstg[0:16, 0:512], pb[:]).then_inc(sEv, 1)
            nc.vector.wait_ge(sT, i * 3 + 3)
            nc.vector.wait_ge(sEv, i * 3 + 2)
            nc.vector.tensor_copy(bstg[0:16, 512:1024], pc[:]).then_inc(sEv, 1)
            nc.scalar.wait_ge(sEv, i * 3 + 3)
            nc.scalar.wait_ge(sEa, i + 1)
            nc.scalar.wait_ge(sO, i * 48)
            nc.scalar.dma_start(out=self.dtT[:, i * 512:(i + 1) * 512], in_=dstg[:]).then_inc(sO, 16)
            nc.scalar.dma_start(out=self.BC[:, i * 512:(i + 1) * 512],
                                in_=bstg[:, 0:512]).then_inc(sO, 16)
            nc.scalar.dma_start(out=self.BC[:, S + i * 512:S + (i + 1) * 512],
                                in_=bstg[:, 512:1024]).then_inc(sO, 16)
        nc.sync.wait_ge(sO, (S // 512) * 48)

    def ph_dtproj(self):
        nc = self.nc
        wdt = self.sb("dp_w", [128, E0], BF16)
        sW, = self.phase_sems("dp_W")
        nc.scalar.dma_start(out=wdt[:], in_=self.dtproj_b[0, :, :]).then_inc(sW, 16)
        nc.tensor.wait_ge(sW, 16)
        dch = self.sb("dp_x", [128, 512], BF16)
        ustg = self.sb("dp_u", [128, 16 * 512], F32)
        pz = [self.ps("dp_p%d" % j, [128, 512], F32) for j in range(4)]
        sL, sT, sV, sA, sO = self.phase_sems("dp_L", "dp_T", "dp_V", "dp_A", "dp_O")
        for i in range(S // 512):
            if i > 0:
                nc.sync.wait_ge(sT, i * 16)
            nc.sync.wait_ge(sL, i * 16)
            nc.sync.dma_start(out=dch[:], in_=self.dtT[:, i * 512:(i + 1) * 512]).then_inc(sL, 16)
            nc.tensor.wait_ge(sL, i * 16 + 16)
            for m in range(16):
                b = m % 4
                gm = i * 16 + m
                if gm >= 4:
                    nc.tensor.wait_ge(sV, gm - 3)
                nc.tensor.matmul(pz[b][:], wdt[:, m * 128:(m + 1) * 128], dch[:],
                                 start=True, stop=True).then_inc(sT, 1)
                nc.vector.wait_ge(sT, gm + 1)
                if i > 0 and m == 0:
                    nc.vector.wait_ge(sO, i * 32)
                nc.vector.tensor_scalar(ustg[:, m * 512:(m + 1) * 512], pz[b][:],
                                        self.bc_dt_bias[:, m:m + 1], 80.0,
                                        OP.add, OP.min).then_inc(sV, 1)
            nc.scalar.wait_ge(sV, i * 16 + 16)
            for m in range(16):
                sl = ustg[:, m * 512:(m + 1) * 512]
                nc.scalar.activation(sl, sl, AF.Exp, bias=0.0, scale=1.0).then_inc(sA, 1)
                nc.scalar.wait_ge(sA, i * 32 + 2 * m + 1)
                nc.scalar.activation(sl, sl, AF.Ln, bias=1.0, scale=1.0).then_inc(sA, 1)
                nc.scalar.wait_ge(sA, i * 32 + 2 * m + 2)
            nc.scalar.wait_ge(sO, i * 32)
            for g in range(2):
                nc.scalar.dma_start(
                    out=self.delta32.rearrange("e p s -> p e s")[:, g * 8:(g + 1) * 8,
                                                                 i * 512:(i + 1) * 512],
                    in_=ustg[:, g * 8 * 512:(g + 1) * 8 * 512].rearrange("p (e s) -> p e s", e=8)).then_inc(sO, 16)
        nc.sync.wait_ge(sO, (S // 512) * 32)

    def ph_scan(self):
        nc = self.nc
        drow = self.sb("sc_d", [128, S], F32)
        urow = self.sb("sc_u", [128, S], F32)
        yrow = self.sb("sc_y", [128, S], BF16)
        acur = self.sb("sc_a", [128, 17], F32)
        bct = [self.sb("sc_bc%d" % j, [16, 1024], F32) for j in range(2)]
        bhi = [self.sb("sc_bh%d" % j, [16, 1024], BF16) for j in range(2)]
        blo = [self.sb("sc_bl%d" % j, [16, 1024], BF16) for j in range(2)]
        btmps = self.sb("sc_bts", [16, 1024], F32)
        dab = [self.sb("sc_da%d" % j, [128, 512], F32) for j in range(4)]
        dub = self.sb("sc_du", [128, 512], F32)
        dbb = [self.sb("sc_db%d" % j, [128, 512], F32) for j in range(2)]
        hall = self.sb("sc_h", [128, 16 * 512], F32)
        hout = self.sb("sc_ho", [128, 16 * 512], F32)
        ybufP = self.sb("sc_ybp", [128, 512], F32)
        carry = self.sb("sc_c", [128, 16], F32)
        pB = [self.ps("sc_pB%d" % j, [128, 512], F32) for j in range(4)]
        pC = [self.ps("sc_pC%d" % j, [128, 512], F32) for j in range(4)]
        sL, sT, sV, sA, sO = self.phase_sems("sc_L", "sc_T", "sc_V", "sc_A", "sc_O")
        sBC = self.sem("sc_BC")
        sP = self.sem("sc_P")
        NCHK = S // 512
        with self.fori(0, 16) as e:
            nc.sync.dma_start(out=drow[:], in_=self.delta32[ts(e, 1), :, :]).then_inc(sL, 16)
            nc.sync.dma_start(out=urow[:], in_=self.xc32[ts(e, 1), :, :]).then_inc(sL, 16)
            nc.scalar.dma_start(out=acur[:], in_=self.apack[ts(e, 1), :, :]).then_inc(sLA := self.sem("sc_LA"), 16)
            nc.vector.memset(carry[:], 0.0).then_inc(sV, 1)
            vv, pp, tt, aa = 1, 0, 0, 0
            dbuvD = dbuvP = None
            pvD = pvP = None
            scvD = scvP = None
            nc.vector.wait_ge(sL, 32)
            nc.scalar.wait_ge(sL, 32)
            nc.tensor.wait_ge(sL, 32)
            nc.gpsimd.wait_ge(sV, 1)
            nc.vector.wait_ge(sLA, 16)
            nc.scalar.wait_ge(sLA, 16)
            nc.tensor.wait_ge(sLA, 16)
            for c in range(NCHK):
                csl = slice(c * 512, (c + 1) * 512)
                if c >= 1:
                    nc.sync.wait_ge(sBC, c * 32)
                if c >= 2:
                    nc.sync.wait_ge(sP, self._sc_prepd[c - 2])
                nc.sync.dma_start(out=bct[c % 2][:, 0:512],
                                  in_=self.BC[:, c * 512:(c + 1) * 512]).then_inc(sBC, 16)
                nc.sync.dma_start(out=bct[c % 2][:, 512:1024],
                                  in_=self.BC[:, S + c * 512:S + (c + 1) * 512]).then_inc(sBC, 16)
                nc.vector.tensor_mul(dub[:], drow[:, csl], urow[:, csl]).then_inc(sV, 1)
                vv += 1
                duv = vv
                nc.gpsimd.wait_ge(sBC, (c + 1) * 32)
                if c >= 2:
                    nc.gpsimd.wait_ge(sT, (c - 1) * 32)
                nc.gpsimd.tensor_copy(bhi[c % 2][:], bct[c % 2][:]).then_inc(sP, 1)
                pp += 1
                nc.gpsimd.wait_ge(sP, pp)
                nc.gpsimd.tensor_sub(btmps[:], bct[c % 2][:], bhi[c % 2][:]).then_inc(sP, 1)
                pp += 1
                nc.gpsimd.wait_ge(sP, pp)
                nc.gpsimd.tensor_copy(blo[c % 2][:], btmps[:]).then_inc(sP, 1)
                pp += 1
                self._sc_prepd = getattr(self, "_sc_prepd", {})
                self._sc_prepd[c] = pp
                nc.tensor.wait_ge(sP, pp)
                dbuv = {}
                scanv = {}
                expv = {}
                hm = {}
                for n in range(16):
                    b = n % 4
                    hsl = slice(n * 512, (n + 1) * 512)
                    ohb = self.c_oneh16b[:, n * 128:(n + 1) * 128]
                    if n >= 4:
                        nc.tensor.wait_ge(sV, dbuv[n - 4])
                    elif c >= 1:
                        nc.tensor.wait_ge(sV, self._sc_dbuv_last[12 + b])
                    nc.tensor.matmul(pB[b][:], ohb, bhi[c % 2][0:16, 0:512],
                                     start=True, stop=False)
                    nc.tensor.matmul(pB[b][:], ohb, blo[c % 2][0:16, 0:512],
                                     start=False, stop=True).then_inc(sT, 1)
                    tt += 1
                    tB = tt
                    if n >= 4:
                        nc.scalar.wait_ge(sV, scanv[n - 4])
                    elif c >= 1:
                        nc.scalar.wait_ge(sV, self._sc_scanv_last[12 + b])
                    nc.scalar.activation(dab[b][:], drow[:, csl], AF.Exp, bias=0.0,
                                         scale=acur[:, n:n + 1]).then_inc(sA, 1)
                    aa += 1
                    expv[n] = aa
                    nc.vector.wait_ge(sT, tB)
                    nc.vector.wait_ge(sA, aa)
                    nc.vector.wait_ge(sV, duv)
                    nc.vector.tensor_mul(dbb[n % 2][:], dub[:], pB[b][:]).then_inc(sV, 1)
                    vv += 1
                    dbuv[n] = vv
                    nc.vector.wait_ge(sV, vv)
                    if c >= 1:
                        nc.vector.wait_ge(sP, self._sc_ccv_last)
                    nc.vector.tensor_tensor_scan(hall[:, hsl], dab[b][:],
                                                 dbb[n % 2][:], carry[:, n:n + 1],
                                                 OP.mult, OP.add).then_inc(sV, 1)
                    vv += 1
                    scanv[n] = vv
                    if n >= 4:
                        nc.tensor.wait_ge(sV, hm[n - 4])
                    elif c >= 1:
                        nc.tensor.wait_ge(sV, self._sc_hm_last[12 + b])
                    nc.tensor.matmul(pC[b][:], ohb, bhi[c % 2][0:16, 512:1024],
                                     start=True, stop=False)
                    nc.tensor.matmul(pC[b][:], ohb, blo[c % 2][0:16, 512:1024],
                                     start=False, stop=True).then_inc(sT, 1)
                    tt += 1
                    nc.vector.wait_ge(sT, tt)
                    nc.vector.wait_ge(sV, vv)
                    if c >= 1:
                        nc.vector.wait_ge(sP, self._sc_addv_last[n])
                    nc.vector.tensor_mul(hout[:, hsl], hall[:, hsl], pC[b][:]).then_inc(sV, 1)
                    vv += 1
                    hm[n] = vv
                self._sc_dbuv_last = dbuv
                self._sc_scanv_last = scanv
                self._sc_hm_last = hm
                addv = {}
                for n in range(16):
                    nc.gpsimd.wait_ge(sV, hm[n])
                    if n == 0:
                        if c >= 1:
                            nc.gpsimd.wait_ge(sV, self._sc_sttv)
                        nc.gpsimd.tensor_copy(ybufP[:], hout[:, 0:512]).then_inc(sP, 1)
                    else:
                        nc.gpsimd.wait_ge(sP, pp)
                        nc.gpsimd.tensor_add(ybufP[:], ybufP[:],
                                             hout[:, n * 512:(n + 1) * 512]).then_inc(sP, 1)
                    pp += 1
                    addv[n] = pp
                self._sc_addv_last = addv
                nc.gpsimd.wait_ge(sV, scanv[15])
                nc.gpsimd.wait_ge(sP, pp)
                nc.gpsimd.tensor_copy(carry[:],
                                      hall[:].rearrange("p (n t) -> p n t", n=16)[:, :, 511]).then_inc(sP, 1)
                pp += 1
                self._sc_ccv_last = pp
                nc.vector.wait_ge(sP, addv[15])
                nc.vector.scalar_tensor_tensor(yrow[:, csl], urow[:, csl], acur[:, 16:17],
                                               ybufP[:], OP.mult, OP.add).then_inc(sV, 1)
                vv += 1
                self._sc_sttv = vv
            nc.scalar.wait_ge(sV, vv)
            nc.scalar.dma_start(out=self.yT[ts(e, 1), :, :], in_=yrow[:]).then_inc(sO, 16)
            nc.gpsimd.wait_ge(sO, 16)
            self.clears(sL, sT, sV, sA, sO, sBC, sP, self.sem("sc_LA"))

    def ph_wout(self):
        nc = self.nc
        wwo = self.sb("wo_w", [128, 16 * D], BF16)
        sW, = self.phase_sems("wo_W")
        for g in range(2):
            nc.scalar.dma_start(out=wwo[:, g * 8 * D:(g + 1) * 8 * D].rearrange("p (k m) -> p k m", k=8),
                                in_=self.Wout_b.rearrange("k p m -> p k m")[:, g * 8:(g + 1) * 8, :]).then_inc(sW, 16)
        nc.tensor.wait_ge(sW, 32)
        ych = self.sb("wo_y", [128, 16 * 512], BF16)
        xrch = self.sb("wo_xr", [128, 16 * 512], F32)
        yx = self.sb("wo_yx", [128, 16 * 512], BF16)
        ostg = self.sb("wo_o", [128, 8 * 512], F32)
        pz = [self.ps("wo_p%d" % j, [128, 512], F32) for j in range(4)]
        sL, sT, sV, sE, sO = self.phase_sems("wo_L", "wo_T", "wo_V", "wo_E", "wo_O")
        for i in range(S // 512):
            if i > 0:
                nc.sync.wait_ge(sV, i)
            nc.sync.wait_ge(sL, i * 48)
            for g in range(2):
                nc.sync.dma_start(
                    out=ych[:, g * 8 * 512:(g + 1) * 8 * 512].rearrange("p (e s) -> p e s", e=8),
                    in_=self.yT.rearrange("e p s -> p e s")[:, g * 8:(g + 1) * 8,
                                                           i * 512:(i + 1) * 512]).then_inc(sL, 16)
            nc.sync.dma_start(out=xrch[:],
                              in_=self.zz[i, :, 16 * 512:32 * 512]).then_inc(sL, 16)
            nc.vector.wait_ge(sL, i * 48 + 48)
            if i > 0:
                nc.vector.wait_ge(sT, i * 8)
            nc.vector.tensor_mul(yx[:], ych[:], xrch[:]).then_inc(sV, 1)
            nc.tensor.wait_ge(sV, i + 1)
            for m in range(8):
                b = m % 4
                gm = i * 8 + m
                if gm >= 4:
                    nc.tensor.wait_ge(sE, gm - 3)
                for k in range(16):
                    mm = nc.tensor.matmul(pz[b][:], wwo[:, k * D + m * 128:k * D + (m + 1) * 128],
                                          yx[:, k * 512:(k + 1) * 512],
                                          start=(k == 0), stop=(k == 15))
                mm.then_inc(sT, 1)
                nc.scalar.wait_ge(sT, gm + 1)
                if i > 0 and m == 0:
                    nc.scalar.wait_ge(sO, i * 32)
                nc.scalar.activation(ostg[:, m * 512:(m + 1) * 512], pz[b][:], AF.Identity,
                                     bias=self.bc_b_out[:, m:m + 1], scale=1.0).then_inc(sE, 1)
            nc.scalar.wait_ge(sE, i * 8 + 8)
            nc.scalar.wait_ge(sO, i * 32)
            for g in range(2):
                nc.scalar.dma_start(
                    out=self.s0T.rearrange("e p s -> p e s")[:, g * 4:(g + 1) * 4,
                                                             i * 512:(i + 1) * 512],
                    in_=ostg[:, g * 4 * 512:(g + 1) * 4 * 512].rearrange("p (e s) -> p e s", e=4)).then_inc(sO, 16)
        nc.sync.wait_ge(sO, (S // 512) * 32)

    def ph_tern(self):
        nc = self.nc
        self._f_l = 0
        self._f_v = 0
        self._f_t = 0
        self._f_o = 0
        self._f_tr = 0
        tin = [self.sb("f_tin%d" % j, [128, 4096], F32) for j in range(2)]
        tstat = self.sb("f_ts", [128, 4], F32)
        absacc = self.sb("f_aa", [128, 1], F32)
        scb = self.sb("f_sc", [128, 1], F32)
        tern = self.sb("f_tb", [128, 4096], BF16)
        tstg = self.sb("f_tstg", [128, 32 * 128], BF16)
        ptr = [self.ps("f_pt%d" % j, [128, 128], BF16) for j in range(2)]
        psc = self.ps("f_psc", [128, 512], F32)
        sL, sV, sT, sE, sO = self.phase_sems("f_L", "f_V", "f_T", "f_E", "f_O")
        for wsel in range(2):
            ntile = 32 if wsel == 0 else 8
            wwid = 1024 if wsel == 0 else 4096
            srcw = self.fW1 if wsel == 0 else self.fW2
            dstw = self.W1t_b if wsel == 0 else self.W2t_b
            redv = {}
            ternv = {}
            nc.vector.wait_ge(sO, self._f_o * 16)
            nc.vector.memset(absacc[:], 0.0).then_inc(sV, 1)
            self._f_v += 1
            nc.vector.wait_ge(sV, self._f_v)
            for t in range(ntile):
                if t >= 2:
                    nc.sync.wait_ge(sV, redv[t - 2])
                else:
                    nc.sync.wait_ge(sV, self._f_v)
                nc.sync.wait_ge(sL, self._f_l * 16)
                nc.sync.dma_start(out=tin[t % 2][:, 0:wwid],
                                  in_=srcw[t * 128:(t + 1) * 128, :]).then_inc(sL, 16)
                self._f_l += 1
                nc.vector.wait_ge(sL, self._f_l * 16)
                nc.vector.wait_ge(sV, self._f_v)
                nc.vector.tensor_reduce(tstat[:, 0:1], tin[t % 2][:, 0:wwid], AX.X,
                                        OP.add, apply_absolute_value=True).then_inc(sV, 1)
                self._f_v += 1
                nc.vector.wait_ge(sV, self._f_v)
                nc.vector.tensor_add(absacc[:], absacc[:], tstat[:, 0:1]).then_inc(sV, 1)
                self._f_v += 1
                redv[t] = self._f_v
            nc.tensor.wait_ge(sV, self._f_v)
            nc.tensor.matmul(psc[0:1, 0:1], absacc[:], self.c_ones128c[:],
                             start=True, stop=True).then_inc(sT, 1)
            self._f_t += 1
            nc.vector.wait_ge(sT, self._f_t)
            nc.vector.tensor_scalar(tstat[0:1, 1:2], psc[0:1, 0:1],
                                    1.0 / (FF * D), 1e-8, OP.mult, OP.max).then_inc(sV, 1)
            self._f_v += 1
            nc.tensor.wait_ge(sV, self._f_v)
            nc.tensor.matmul(psc[0:128, 1:2], self.c_ones1[:], tstat[0:1, 1:2],
                             start=True, stop=True).then_inc(sT, 1)
            self._f_t += 1
            nc.vector.wait_ge(sT, self._f_t)
            nc.vector.tensor_copy(scb[:], psc[0:128, 1:2]).then_inc(sV, 1)
            self._f_v += 1
            nc.vector.wait_ge(sV, self._f_v)
            nc.vector.reciprocal(scb[:], scb[:]).then_inc(sV, 1)
            self._f_v += 1
            for t in range(ntile):
                if t >= 2:
                    nc.sync.wait_ge(sV, ternv[t - 2])
                else:
                    nc.sync.wait_ge(sV, self._f_v)
                nc.sync.wait_ge(sL, self._f_l * 16)
                nc.sync.dma_start(out=tin[t % 2][:, 0:wwid],
                                  in_=srcw[t * 128:(t + 1) * 128, :]).then_inc(sL, 16)
                self._f_l += 1
                nc.vector.wait_ge(sL, self._f_l * 16)
                nc.vector.wait_ge(sV, self._f_v)
                nc.vector.tensor_scalar(tin[t % 2][:, 0:wwid], tin[t % 2][:, 0:wwid],
                                        scb[:, 0:1], None, OP.mult).then_inc(sV, 1)
                self._f_v += 1
                nc.vector.wait_ge(sV, self._f_v)
                nc.vector.tensor_scalar(tin[t % 2][:, 0:wwid], tin[t % 2][:, 0:wwid],
                                        12582912.0, -12582912.0, OP.add, OP.add).then_inc(sV, 1)
                self._f_v += 1
                nc.vector.wait_ge(sV, self._f_v)
                nc.vector.wait_ge(sO, self._f_o * 16)
                nc.vector.tensor_scalar(tern[:, 0:wwid], tin[t % 2][:, 0:wwid],
                                        -1.0, 1.0, OP.max, OP.min).then_inc(sV, 1)
                self._f_v += 1
                ternv[t] = self._f_v
                ntr = wwid // 128
                nc.tensor.wait_ge(sV, self._f_v)
                base_tr = self._f_tr
                for d in range(ntr):
                    gidx = self._f_tr
                    if gidx >= 2:
                        nc.tensor.wait_ge(sE, gidx - 1)
                    nc.tensor.transpose(ptr[gidx % 2][:], tern[:, d * 128:(d + 1) * 128],
                                        self.c_identb[:]).then_inc(sT, 1)
                    self._f_t += 1
                    nc.vector.wait_ge(sT, self._f_t)
                    nc.vector.tensor_copy(tstg[:, d * 128:(d + 1) * 128],
                                          ptr[gidx % 2][:]).then_inc(sE, 1)
                    self._f_tr += 1
                nc.scalar.wait_ge(sE, self._f_tr)
                ng = max(1, ntr // 8)
                nc.scalar.wait_ge(sO, self._f_o * 16)
                for g in range(ng):
                    k1 = min((g + 1) * 8, ntr)
                    nc.scalar.dma_start(
                        out=dstw.rearrange("k p m -> p k m")[:, g * 8:k1, t * 128:(t + 1) * 128],
                        in_=tstg[:, g * 8 * 128:k1 * 128].rearrange("p (k m) -> p k m", m=128)).then_inc(sO, 16)
                    self._f_o += 1
        nc.vector.wait_ge(sO, self._f_o * 16)

    def ph_ffn0(self):
        nc = self.nc
        w1t = self.sb("f_w1", [128, 8 * FF], BF16)
        sW, = self.phase_sems("f_W")
        for g in range(2):
            nc.scalar.dma_start(out=w1t[:, g * 4 * FF:(g + 1) * 4 * FF].rearrange("p (k m) -> p k m", k=4),
                                in_=self.W1t_b.rearrange("k p m -> p k m")[:, g * 4:(g + 1) * 4, :]).then_inc(sW, 16)
        nc.tensor.wait_ge(sW, 32)
        xnb = self.sb("f_x", [128, 8 * 512], BF16)
        h1b = self.sb("f_h", [128, 32 * 512], BF16)
        w2g = [self.sb("f_w2%d" % j, [128, 8 * D], BF16) for j in range(2)]
        yac = self.sb("f_ya", [128, 8 * 512], F32)
        gstg = self.sb("f_g", [128, 8 * 512], F32)
        pz = [self.ps("f_p%d" % j, [128, 512], F32) for j in range(4)]
        p2 = [self.ps("f_q%d" % j, [128, 512], F32) for j in range(2)]
        sL2, sT2, sE2, sO2, sU = self.phase_sems("f_L2", "f_T2", "f_E2", "f_O2", "f_U")
        with self.fori(0, S // 512) as i:
            nc.sync.dma_start(out=xnb[:], in_=self.xnz[ts(i, 1), :, :]).then_inc(sL2, 16)
            nc.sync.dma_start(out=w2g[0][:].rearrange("p (k m) -> p k m", k=8),
                              in_=self.W2t_b.rearrange("k p m -> p k m")[:, 0:8, :]).then_inc(sU, 16)
            nc.tensor.wait_ge(sL2, 16)
            for m in range(32):
                b = m % 4
                if m >= 4:
                    nc.tensor.wait_ge(sE2, m - 3)
                for k in range(8):
                    mm = nc.tensor.matmul(pz[b][:], w1t[:, k * FF + m * 128:k * FF + (m + 1) * 128],
                                          xnb[:, k * 512:(k + 1) * 512],
                                          start=(k == 0), stop=(k == 7))
                mm.then_inc(sT2, 1)
                nc.scalar.wait_ge(sT2, m + 1)
                nc.scalar.activation(h1b[:, m * 512:(m + 1) * 512], pz[b][:], AF.Gelu,
                                     bias=self.bc_fb1[:, m:m + 1], scale=1.0).then_inc(sE2, 1)
            tcnt = 32
            sYs = self.sem("f_Ys")
            sYv = self.sem("f_Yv")
            sG = self.sem("f_G")
            evv = {}
            for kg in range(4):
                if kg < 3:
                    if kg >= 1:
                        nc.sync.wait_ge(sT2, 32 + kg * 8)
                    nc.sync.wait_ge(sU, (kg + 1) * 16)
                    nc.sync.dma_start(out=w2g[(kg + 1) % 2][:].rearrange("p (k m) -> p k m", k=8),
                                      in_=self.W2t_b.rearrange("k p m -> p k m")[:, (kg + 1) * 8:(kg + 2) * 8, :]).then_inc(sU, 16)
                nc.tensor.wait_ge(sU, kg * 16 + 16)
                nc.tensor.wait_ge(sE2, min(32, kg * 8 + 8))
                for m in range(8):
                    b = m % 2
                    gi = kg * 8 + m
                    if gi >= 2:
                        pk_, pv_ = evv[gi - 2]
                        if pk_ == 0:
                            nc.tensor.wait_ge(sYs, pv_)
                        else:
                            nc.tensor.wait_ge(sYv, pv_)
                    for k in range(8):
                        mm = nc.tensor.matmul(p2[b][:], w2g[kg % 2][:, k * D + m * 128:k * D + (m + 1) * 128],
                                              h1b[:, (kg * 8 + k) * 512:(kg * 8 + k + 1) * 512],
                                              start=(k == 0), stop=(k == 7))
                    mm.then_inc(sT2, 1)
                    tcnt += 1
                    if kg == 0:
                        nc.scalar.wait_ge(sT2, tcnt)
                        nc.scalar.copy(yac[:, m * 512:(m + 1) * 512], p2[b][:]).then_inc(sYs, 1)
                    else:
                        nc.vector.wait_ge(sT2, tcnt)
                        nc.vector.tensor_add(yac[:, m * 512:(m + 1) * 512],
                                             yac[:, m * 512:(m + 1) * 512], p2[b][:]).then_inc(sYv, 1)
                    evv[gi] = (0, m + 1) if kg == 0 else (1, (kg - 1) * 8 + m + 1)
            for m in range(8):
                nc.scalar.wait_ge(sYv, 16 + m + 1)
                nc.scalar.activation(gstg[:, m * 512:(m + 1) * 512], yac[:, m * 512:(m + 1) * 512],
                                     AF.Silu, bias=self.bc_fb2[:, m:m + 1], scale=1.0).then_inc(sG, 1)
            nc.scalar.wait_ge(sG, 8)
            nc.scalar.dma_start(out=self.g0z[ts(i, 1), :, :], in_=gstg[:]).then_inc(sO2, 16)
            nc.gpsimd.wait_ge(sO2, 16)
            self.clears(sL2, sT2, sE2, sO2, sU, sYs, sYv, sG)


    def ph_resid1(self):
        nc = self.nc
        s0c = self.sb("r_s", [128, 8 * 512], F32)
        g0c = self.sb("r_g", [128, 8 * 512], F32)
        psl = self.sb("r_p", [128, 8 * 512], F32)
        xtc = self.sb("r_x", [128, 4 * 1024], F32)
        ptok = self.sb("r_pt", [128, 4 * 1024], F32)
        x1c = self.sb("r_x1", [128, 4 * 1024], F32)
        xn1 = self.sb("r_xn", [128, 1024], BF16)
        xstg = self.sb("r_xs", [128, 8 * 512], BF16)
        dump = self.sb("r_du", [128, 2 * D], F32)
        stats = self.sb("r_st", [128, 8], F32)
        ptf = [self.ps("r_pf%d" % j, [128, 128], F32) for j in range(2)]
        ptb = [self.ps("r_pb%d" % j, [128, 128], BF16) for j in range(2)]
        sL, sV, sA, sTf, sEf, sTb, sEb, sO = self.phase_sems(
            "r_L", "r_V", "r_A", "r_Tf", "r_Ef", "r_Tb", "r_Eb", "r_O")
        ntrf = 0
        ntrb = 0
        for i in range(S // 512):
            if i > 0:
                nc.sync.wait_ge(sV, i * 21)
            nc.sync.wait_ge(sL, i * 48)
            nc.sync.dma_start(out=s0c[:].rearrange("p (e s) -> p e s", e=8),
                              in_=self.s0T.rearrange("e p s -> p e s")[:, :,
                                                                      i * 512:(i + 1) * 512]).then_inc(sL, 16)
            nc.sync.dma_start(out=g0c[:], in_=self.g0z[i, :, :]).then_inc(sL, 16)
            nc.sync.dma_start(out=xtc[:].rearrange("p (t d) -> p t d", t=4),
                              in_=self.x.rearrange("(n p) d -> p n d", p=128)[:, i * 4:(i + 1) * 4, :]).then_inc(sL, 16)
            nc.vector.wait_ge(sL, i * 48 + 48)
            nc.vector.tensor_mul(psl[:], s0c[:], g0c[:]).then_inc(sV, 1)
            vv = i * 21 + 1
            nc.tensor.wait_ge(sV, vv)
            for tt in range(4):
                for e in range(8):
                    if ntrf >= 2:
                        nc.tensor.wait_ge(sEf, ntrf - 1)
                    nc.tensor.transpose(ptf[ntrf % 2][:], psl[:, e * 512 + tt * 128:e * 512 + (tt + 1) * 128],
                                        self.c_identf[:]).then_inc(sTf, 1)
                    nc.scalar.wait_ge(sTf, ntrf + 1)
                    nc.scalar.copy(ptok[:, tt * 1024 + e * 128:tt * 1024 + (e + 1) * 128],
                                   ptf[ntrf % 2][:]).then_inc(sEf, 1)
                    ntrf += 1
            aa = i * 20
            for tt in range(4):
                nc.vector.wait_ge(sEf, i * 32 + (tt + 1) * 8)
                if i > 0 and tt == 0:
                    nc.vector.wait_ge(sO, i * 32)
                nc.vector.tensor_add(x1c[:, tt * 1024:(tt + 1) * 1024],
                                     xtc[:, tt * 1024:(tt + 1) * 1024],
                                     ptok[:, tt * 1024:(tt + 1) * 1024]).then_inc(sV, 1)
                vv += 1
                nc.scalar.wait_ge(sV, vv)
                if ntrb >= 8:
                    nc.scalar.wait_ge(sEb, ntrb)
                nc.vector.wait_ge(sV, vv)
                self._ln_tile(x1c[:, tt * 1024:(tt + 1) * 1024], xn1, dump, stats,
                              sV, sA, vv, aa)
                vv += 4
                aa += 5
                nc.tensor.wait_ge(sA, aa)
                for d in range(8):
                    if ntrb >= 2:
                        nc.tensor.wait_ge(sEb, ntrb - 1)
                    nc.tensor.transpose(ptb[ntrb % 2][:], xn1[:, d * 128:(d + 1) * 128],
                                        self.c_identb[:]).then_inc(sTb, 1)
                    nc.vector.wait_ge(sTb, ntrb + 1)
                    nc.vector.tensor_copy(xstg[:, d * 512 + tt * 128:d * 512 + (tt + 1) * 128],
                                          ptb[ntrb % 2][:]).then_inc(sEb, 1)
                    ntrb += 1
            nc.scalar.wait_ge(sEb, ntrb)
            nc.scalar.wait_ge(sO, i * 32)
            nc.scalar.dma_start(out=self.x1s.rearrange("(n p) d -> p n d", p=128)[:, i * 4:(i + 1) * 4, :],
                                in_=x1c[:].rearrange("p (t d) -> p t d", t=4)).then_inc(sO, 16)
            nc.scalar.dma_start(out=self.xn1z[i, :, :], in_=xstg[:]).then_inc(sO, 16)
        nc.sync.wait_ge(sO, (S // 512) * 32)

    def ph_qkvg(self):
        nc = self.nc
        wq = self.sb("q_wq", [128, 8 * KD], BF16)
        wk_ = self.sb("q_wk", [128, 8 * KD], BF16)
        wv = self.sb("q_wv", [128, 8 * VD], BF16)
        wg = self.sb("q_wg", [128, 8 * VD], BF16)
        wgk = self.sb("q_wgk", [128, 8 * NH], BF16)
        wbb = self.sb("q_wb", [128, 8 * NH], BF16)
        wr = self.sb("q_wr", [128, 8 * NEXP], BF16)
        sW, = self.phase_sems("q_W")
        nw = 0
        for (dst, srcw) in [(wq, self.Wq_b), (wk_, self.Wk_b), (wv, self.Wv_b),
                            (wg, self.Wg_b), (wgk, self.Wgk_b), (wbb, self.Wb_b),
                            (wr, self.Wr_b)]:
            kk = srcw.shape[2]
            nc.scalar.dma_start(out=dst[:].rearrange("p (k m) -> p k m", k=8),
                                in_=srcw.rearrange("k p m -> p k m")).then_inc(sW, 16)
            nw += 1
        nc.tensor.wait_ge(sW, nw * 16)
        xnb = self.sb("q_x", [128, 8 * 512], BF16)
        qkstg = self.sb("q_qk", [128, 12 * 512], BF16)
        vgstg = self.sb("q_vg", [128, 24 * 512], BF16)
        glstg = self.sb("q_gl", [32, 512], F32)
        bstg12 = self.sb("q_b12", [12, 512], F32)
        lstg8 = self.sb("q_l8", [8, 512], F32)
        pz = [self.ps("q_p%d" % j, [128, 512], F32) for j in range(2)]
        pgk = self.ps("q_pgk", [12, 512], F32)
        pbt = self.ps("q_pbt", [12, 512], F32)
        plg = self.ps("q_plg", [8, 512], F32)
        sL, sT, sE, sV, sA, sO = self.phase_sems("q_L", "q_T", "q_E", "q_V", "q_A", "q_O")
        with self.fori(0, S // 512) as i:
            nc.sync.dma_start(out=xnb[:], in_=self.xn1z[ts(i, 1), :, :]).then_inc(sL, 16)
            nc.tensor.wait_ge(sL, 16)
            grp = 0
            plan = ([("q", wq, mt, qkstg, mt, AF.Identity) for mt in range(6)] +
                    [("k", wk_, mt, qkstg, 6 + mt, AF.Identity) for mt in range(6)] +
                    [("v", wv, mt, vgstg, mt, AF.Identity) for mt in range(12)] +
                    [("g", wg, mt, vgstg, 12 + mt, AF.Silu) for mt in range(12)])
            for (_nm, wsrc, mt, dstg, dcol, fn) in plan:
                b = grp % 2
                if grp >= 2:
                    nc.tensor.wait_ge(sE, grp - 1)
                wid = wsrc.shape[1] // 8
                for k in range(8):
                    mm = nc.tensor.matmul(pz[b][:], wsrc[:, k * wid + mt * 128:k * wid + (mt + 1) * 128],
                                          xnb[:, k * 512:(k + 1) * 512],
                                          start=(k == 0), stop=(k == 7))
                mm.then_inc(sT, 1)
                grp += 1
                nc.scalar.wait_ge(sT, grp)
                nc.scalar.activation(dstg[:, dcol * 512:(dcol + 1) * 512], pz[b][:], fn,
                                     bias=0.0, scale=1.0).then_inc(sE, 1)
            for k in range(8):
                mm = nc.tensor.matmul(pgk[:], wgk[:, k * NH:(k + 1) * NH], xnb[:, k * 512:(k + 1) * 512],
                                      start=(k == 0), stop=(k == 7))
            mm.then_inc(sT, 1)
            for k in range(8):
                mm = nc.tensor.matmul(pbt[:], wbb[:, k * NH:(k + 1) * NH], xnb[:, k * 512:(k + 1) * 512],
                                      start=(k == 0), stop=(k == 7))
            mm.then_inc(sT, 1)
            for k in range(8):
                mm = nc.tensor.matmul(plg[:], wr[:, k * NEXP:(k + 1) * NEXP], xnb[:, k * 512:(k + 1) * 512],
                                      start=(k == 0), stop=(k == 7))
            mm.then_inc(sT, 1)
            nc.vector.wait_ge(sT, 37)
            nc.vector.tensor_scalar(glstg[0:12, :], pgk[:], self.r_dtb[:, 0:1], 80.0,
                                    OP.add, OP.min).then_inc(sV, 1)
            nc.scalar.wait_ge(sV, 1)
            nc.scalar.activation(glstg[0:12, :], glstg[0:12, :], AF.Exp, bias=0.0, scale=1.0).then_inc(sA, 1)
            nc.scalar.wait_ge(sA, 1)
            nc.scalar.activation(glstg[0:12, :], glstg[0:12, :], AF.Ln, bias=1.0, scale=1.0).then_inc(sA, 1)
            nc.vector.wait_ge(sA, 2)
            nc.vector.tensor_scalar(glstg[0:12, :], glstg[0:12, :], self.r_negA[:, 0:1], None,
                                    OP.mult).then_inc(sV, 1)
            nc.scalar.wait_ge(sT, 38)
            nc.scalar.activation(bstg12[:], pbt[:], AF.Sigmoid,
                                 bias=self.r_bb[:, 0:1], scale=1.0).then_inc(sA, 1)
            nc.scalar.wait_ge(sA, 3)
            nc.scalar.dma_start(out=glstg[12:24, :], in_=bstg12[:]).then_inc(sA, 16)
            nc.vector.wait_ge(sT, 39)
            nc.vector.tensor_copy(lstg8[:], plg[:]).then_inc(sV, 1)
            nc.scalar.wait_ge(sV, 3)
            nc.scalar.dma_start(out=glstg[24:32, :], in_=lstg8[:]).then_inc(sA, 16)
            nc.scalar.wait_ge(sE, 36)
            nc.scalar.wait_ge(sV, 3)
            nc.scalar.wait_ge(sA, 35)
            nc.scalar.dma_start(out=self.qkrz[ts(i, 1), :, :], in_=qkstg[:]).then_inc(sO, 16)
            nc.scalar.dma_start(out=self.vrz[ts(i, 1), :, :],
                                in_=vgstg[:, 0:12 * 512]).then_inc(sO, 16)
            nc.scalar.dma_start(out=self.vgz[ts(i, 1), :, 12 * 512:24 * 512],
                                in_=vgstg[:, 12 * 512:24 * 512]).then_inc(sO, 16)
            nc.scalar.dma_start(out=self.gl_d[:, ts(i, 512)], in_=glstg[:]).then_inc(sO, 16)
            nc.gpsimd.wait_ge(sO, 4 * 16)
            self.clears(sL, sT, sE, sV, sA, sO)

    def ph_convq(self):
        nc = self.nc
        wk2 = self.sb("cq_wk", [128, S + 3], BF16)
        a1 = self.sb("cq_a1", [128, S], F32)
        a2 = self.sb("cq_a2", [128, S], F32)
        q2 = self.sb("cq_q2", [128, S], F32)
        rbuf = self.sb("cq_r", [2, 512], F32)
        ostg = self.sb("cq_o", [128, S], BF16)
        psq = [self.ps("cq_ps%d" % j, [2, 512], F32) for j in range(2)]
        pbc = [self.ps("cq_pb%d" % j, [128, 512], F32) for j in range(2)]
        sL, sV, sA, sT, sO = self.phase_sems("cq_L", "cq_V", "cq_A", "cq_T", "cq_O")
        nc.vector.memset(wk2[:, 0:3], 0.0).then_inc(sV, 1)
        nc.vector.wait_ge(sV, 1)
        vv, aa, tt, oo, ll = 1, 0, 0, 0, 0
        for tile in range(24):
            is_v = tile >= 12
            if is_v:
                srcz, srccol = self.vrz, tile - 12
                cw, ci = self.bc_cwv, tile - 12
            elif tile < 6:
                srcz, srccol = self.qkrz, tile
                cw, ci = self.bc_cwq, tile
            else:
                srcz, srccol = self.qkrz, tile
                cw, ci = self.bc_cwk, tile - 6
            if tile > 0:
                nc.sync.wait_ge(sV, self._cq_tapv)
            nc.sync.wait_ge(sL, ll * 16)
            half = (S // 512) // 2
            for g in range(2):
                nc.sync.dma_start(
                    out=wk2[:, 3:S + 3].rearrange("p (c s) -> p c s", c=S // 512)[:, g * half:(g + 1) * half, :],
                    in_=srcz.rearrange("c p m -> p c m")[:, g * half:(g + 1) * half,
                                                         srccol * 512:(srccol + 1) * 512]).then_inc(sL, 16)
            ll += 2
            nc.vector.wait_ge(sL, ll * 16)
            nc.vector.wait_ge(sV, vv)
            if tile > 0:
                nc.vector.wait_ge(sA, aa)
            nc.vector.tensor_scalar(a1[:], wk2[:, 0:S], cw[:, 4 * ci:4 * ci + 1], None,
                                    OP.mult).then_inc(sV, 1)
            vv += 1
            nc.vector.wait_ge(sV, vv)
            nc.vector.scalar_tensor_tensor(a2[:], wk2[:, 1:S + 1], cw[:, 4 * ci + 1:4 * ci + 2],
                                           a1[:], OP.mult, OP.add).then_inc(sV, 1)
            vv += 1
            nc.vector.wait_ge(sV, vv)
            nc.vector.scalar_tensor_tensor(a1[:], wk2[:, 2:S + 2], cw[:, 4 * ci + 2:4 * ci + 3],
                                           a2[:], OP.mult, OP.add).then_inc(sV, 1)
            vv += 1
            nc.vector.wait_ge(sV, vv)
            nc.vector.scalar_tensor_tensor(a2[:], wk2[:, 3:S + 3], cw[:, 4 * ci + 3:4 * ci + 4],
                                           a1[:], OP.mult, OP.add).then_inc(sV, 1)
            vv += 1
            self._cq_tapv = vv
            nc.scalar.wait_ge(sV, vv)
            if tile > 0:
                nc.scalar.wait_ge(sO, oo * 16)
            if is_v:
                nc.scalar.activation(ostg[:], a2[:], AF.Silu, bias=0.0, scale=1.0).then_inc(sA, 1)
                aa += 1
                nc.scalar.wait_ge(sA, aa)
                nc.scalar.wait_ge(sO, oo * 16)
                half = (S // 512) // 2
                for g in range(2):
                    nc.scalar.dma_start(
                        out=self.vgz.rearrange("c p m -> p c m")[:, g * half:(g + 1) * half,
                                                                 (tile - 12) * 512:(tile - 11) * 512],
                        in_=ostg[:].rearrange("p (c s) -> p c s", c=S // 512)[:, g * half:(g + 1) * half, :]).then_inc(sO, 16)
                    oo += 1
            else:
                nc.scalar.activation(a1[:], a2[:], AF.Silu, bias=0.0, scale=1.0).then_inc(sA, 1)
                aa += 1
                nc.scalar.wait_ge(sA, aa)
                nc.scalar.activation(q2[:], a1[:], AF.Square, bias=0.0, scale=1.0).then_inc(sA, 1)
                aa += 1
                nc.tensor.wait_ge(sA, aa)
                nc.vector.wait_ge(sO, oo * 16)
                scl = 0.125 if tile < 6 else 1.0
                for sub in range(S // 512):
                    ssl = slice(sub * 512, (sub + 1) * 512)
                    bq = sub % 2
                    if sub >= 2:
                        nc.tensor.wait_ge(sV, self._cq_r1[sub - 2])
                    nc.tensor.matmul(psq[bq][:], self.c_sel2[:], q2[:, ssl],
                                     start=True, stop=True).then_inc(sT, 1)
                    tt += 1
                    nc.vector.wait_ge(sT, tt)
                    nc.vector.tensor_scalar(rbuf[:], psq[bq][:], 1.0, 1e-6,
                                            OP.mult, OP.add).then_inc(sV, 1)
                    vv += 1
                    self._cq_r1[sub] = vv
                    nc.scalar.wait_ge(sV, vv)
                    nc.scalar.sqrt(rbuf[:], rbuf[:]).then_inc(sA, 1)
                    aa += 1
                    nc.vector.wait_ge(sA, aa)
                    nc.vector.reciprocal(rbuf[:], rbuf[:]).then_inc(sV, 1)
                    vv += 1
                    nc.tensor.wait_ge(sV, vv)
                    if sub >= 2:
                        nc.tensor.wait_ge(sV, self._cq_qh[sub - 2])
                    nc.tensor.matmul(pbc[bq][:], self.c_bc2[:], rbuf[:],
                                     start=True, stop=True).then_inc(sT, 1)
                    tt += 1
                    nc.vector.wait_ge(sT, tt)
                    nc.vector.scalar_tensor_tensor(ostg[:, ssl], a1[:, ssl], scl,
                                                   pbc[bq][:], OP.mult, OP.mult).then_inc(sV, 1)
                    vv += 1
                    self._cq_qh[sub] = vv
                nc.scalar.wait_ge(sV, vv)
                nc.scalar.wait_ge(sO, oo * 16)
                half = (S // 512) // 2
                for g in range(2):
                    nc.scalar.dma_start(
                        out=self.qkhz.rearrange("c p m -> p c m")[:, g * half:(g + 1) * half,
                                                                  tile * 512:(tile + 1) * 512],
                        in_=ostg[:].rearrange("p (c s) -> p c s", c=S // 512)[:, g * half:(g + 1) * half, :]).then_inc(sO, 16)
                    oo += 1
        nc.sync.wait_ge(sO, oo * 16)

    def ph_delta(self):
        nc = self.nc
        woB = self.sb("d_wo", [128, 12 * D], BF16)
        onesg = self.sb("d_one", [12, TC], F32)
        sW, = self.phase_sems("d_W")
        for g in range(2):
            nc.scalar.dma_start(out=woB[:, g * 6 * D:(g + 1) * 6 * D].rearrange("p (k m) -> p k m", k=6),
                                in_=self.Wo_b.rearrange("k p m -> p k m")[:, g * 6:(g + 1) * 6, :]).then_inc(sW, 16)
        nc.vector.memset(onesg[:], 1.0)
        nc.vector.memset(self.Sst[:], 0.0).then_inc(sW, 1)
        nc.tensor.wait_ge(sW, 33)
        nc.gpsimd.wait_ge(sW, 33)
        self.bar()
        gkb = self.sb("d_gkb", [24, 512], F32)
        gkbb = self.sb("d_gkbb", [12, 512], F32)
        qkc = self.sb("d_qk", [128, 12 * 512], BF16)
        vgc = self.sb("d_vg", [128, 24 * 512], BF16)
        gam = self.sb("d_gam", [12, TC], F32)
        gl12 = self.sb("d_gl12", [12, TC], F32)
        dec = self.sb("d_dec", [12, TC], F32)
        expdec = self.sb("d_ed", [12, TC], F32)
        lnb = self.sb("d_lnb", [12, TC], F32)
        expg = self.sb("d_eg", [12, TC], F32)
        gTb = self.sb("d_gT", [128, 12], F32)
        bTb = self.sb("d_bT", [128, 12], F32)
        e1b = self.sb("d_e1", [128, TC], F32)
        e2b = self.sb("d_e2", [128, TC], F32)
        Wb = self.sb("d_Wb", [128, TC], BF16)
        NTq = [self.sb("d_nt%d" % j, [128, TC], F32) for j in range(2)]
        Tq = self.sb("d_tq", [128, TC], F32)
        Uj = [self.sb("d_u%d" % j, [128, DVH], F32) for j in range(2)]
        Ub = self.sb("d_ub", [128, DVH], BF16)
        Kb = self.sb("d_kb", [64, TC], BF16)
        Qb = self.sb("d_qb", [64, TC], BF16)
        S0b = self.sb("d_s0", [64, DVH], BF16)
        Vt = self.sb("d_vt", [128, DVH], BF16)
        dcs = self.sb("d_dc", [128, 1], F32)
        eGb = self.sb("d_egb", [128, 1], F32)
        KtT = self.sb("d_kt", [128, 64], BF16)
        o2b = self.sb("d_o2", [128, TC], F32)
        of32 = self.sb("d_of", [128, TC], F32)
        osb = self.sb("d_os", [128, TC], BF16)
        onb = self.sb("d_on", [128, TC], F32)
        rb2 = self.sb("d_rb", [1, TC], F32)
        ostage = self.sb("d_ost", [128, 12 * TC], BF16)
        s1stg = self.sb("d_s1", [128, 8 * 512], F32)
        pk = [self.ps("d_p%d" % j, [128, 512], F32) for j in range(6)]
        pX = self.ps("d_px", [128, 512], BF16)
        pY = self.ps("d_py", [128, 512], F32)
        sL, sT, sV, sA, sO = self.phase_sems("d_L", "d_T", "d_V", "d_A", "d_O")
        PA, PB, PC, PD, PE, PF = range(6)

        with self.fori(0, S // 512) as c:
            bk = {}

            def guard(bank):
                if bank in bk:
                    sem, val = bk[bank]
                    nc.tensor.wait_ge(sem, val)

            nc.sync.dma_start(out=gkb[:], in_=self.gl_d[0:24, ts(c, 512)]).then_inc(sL, 16)
            nc.sync.dma_start(out=qkc[:], in_=self.qkhz[ts(c, 1), :, :]).then_inc(sL, 16)
            nc.scalar.dma_start(out=vgc[:], in_=self.vgz[ts(c, 1), :, :]).then_inc(sLA2 := self.sem("d_LA"), 16)
            nc.scalar.wait_ge(sL, 32)
            nc.scalar.dma_start(out=gkbb[0:12, :], in_=gkb[12:24, :]).then_inc(sLA2, 16)
            vv = tt = aa = 0
            nc.vector.wait_ge(sL, 32)
            nc.scalar.wait_ge(sL, 32)
            nc.tensor.wait_ge(sL, 32)
            nc.vector.wait_ge(sLA2, 32)
            nc.scalar.wait_ge(sLA2, 32)
            nc.tensor.wait_ge(sLA2, 32)
            for cb in range(4):
              csl = slice(cb * 128, (cb + 1) * 128)
              if True:
                nc.vector.wait_ge(sT, tt)
                nc.vector.tensor_tensor_scan(gam[:], onesg[:], gkb[0:12, csl], 0.0,
                                             OP.mult, OP.add).then_inc(sV, 1)
              vv += 1
              nc.vector.wait_ge(sV, vv)
              nc.vector.tensor_scalar(dec[:], gam[:], gam[:, TC - 1:TC], None,
                                      OP.subtract).then_inc(sV, 1)
              vv += 1
              nc.scalar.wait_ge(sV, vv)
              nc.scalar.activation(expdec[:], dec[:], AF.Exp, bias=0.0, scale=-1.0).then_inc(sA, 1)
              nc.scalar.activation(lnb[:], gkbb[0:12, csl], AF.Ln, bias=0.0, scale=1.0).then_inc(sA, 1)
              nc.scalar.activation(expg[:], gam[:], AF.Exp, bias=0.0, scale=1.0).then_inc(sA, 1)
              aa += 3
              nc.vector.wait_ge(sA, aa)
              nc.vector.tensor_add(gl12[:], gam[:], lnb[:]).then_inc(sV, 1)
              vv += 1
              nc.tensor.wait_ge(sV, vv)
              nc.tensor.transpose(pY[0:128, 0:12], gam[:], self.c_identf[0:12, 0:12]).then_inc(sT, 1)
              tt += 1
              nc.vector.wait_ge(sT, tt)
              nc.vector.tensor_copy(gTb[:], pY[0:128, 0:12]).then_inc(sV, 1)
              vv += 1
              nc.tensor.wait_ge(sV, vv)
              nc.tensor.transpose(pY[0:128, 0:12], gkbb[0:12, csl], self.c_identf[0:12, 0:12]).then_inc(sT, 1)
              tt += 1
              nc.vector.wait_ge(sT, tt)
              nc.vector.tensor_copy(bTb[:], pY[0:128, 0:12]).then_inc(sV, 1)
              vv += 1
              bk_pY = (sV, vv)
              for h in range(12):
                  oh = self.c_oneh12[:, h * 128:(h + 1) * 128]
                  rq = (h % 2) * 64
                  qt = (h // 2)
                  qh = qkc[rq:rq + 64, qt * 512 + cb * 128:qt * 512 + (cb + 1) * 128]
                  kh = qkc[rq:rq + 64, (qt + 6) * 512 + cb * 128:(qt + 6) * 512 + (cb + 1) * 128]
                  vh = vgc[:, h * 512 + cb * 128:h * 512 + (cb + 1) * 128]
                  gh = vgc[:, (12 + h) * 512 + cb * 128:(12 + h) * 512 + (cb + 1) * 128]
                  guard(PA)
                  nc.tensor.matmul(pk[PA][:, 0:TC], oh, gam[:], start=True, stop=True).then_inc(sT, 1)
                  tt += 1
                  nc.vector.wait_ge(sT, tt)
                  if h == 0:
                      nc.vector.wait_ge(sV, bk_pY[1])
                  nc.vector.scalar_tensor_tensor(e1b[:], pk[PA][:, 0:TC], gTb[:, h:h + 1],
                                                 self.c_negIr[:], OP.subtract, OP.add).then_inc(sV, 1)
                  vv += 1
                  bk[PA] = (sV, vv)
                  nc.scalar.wait_ge(sV, vv)
                  nc.scalar.activation(e1b[:], e1b[:], AF.Exp, bias=0.0, scale=1.0).then_inc(sA, 1)
                  aa += 1
                  e1a = aa
                  guard(PB)
                  nc.tensor.matmul(pk[PB][:, 0:TC], oh, gl12[:], start=True, stop=True).then_inc(sT, 1)
                  tt += 1
                  nc.vector.wait_ge(sT, tt)
                  nc.vector.scalar_tensor_tensor(e2b[:], pk[PB][:, 0:TC], gTb[:, h:h + 1],
                                                 self.c_negSr[:], OP.subtract, OP.add).then_inc(sV, 1)
                  vv += 1
                  bk[PB] = (sV, vv)
                  nc.scalar.wait_ge(sV, vv)
                  nc.scalar.activation(e2b[:], e2b[:], AF.Exp, bias=0.0, scale=1.0).then_inc(sA, 1)
                  aa += 1
                  e2a = aa
                  guard(PC)
                  nc.tensor.matmul(pk[PC][:, 0:TC], kh, qh, start=True, stop=True).then_inc(sT, 1)
                  tt += 1
                  guard(PD)
                  nc.tensor.matmul(pk[PD][:, 0:TC], kh, kh, start=True, stop=True).then_inc(sT, 1)
                  tt += 1
                  nc.vector.wait_ge(sT, tt)
                  nc.vector.wait_ge(sA, e1a)
                  nc.vector.tensor_mul(Wb[:], pk[PC][:, 0:TC], e1b[:]).then_inc(sV, 1)
                  vv += 1
                  bk[PC] = (sV, vv)
                  nc.vector.wait_ge(sV, vv)
                  nc.vector.wait_ge(sA, e2a)
                  nc.vector.scalar_tensor_tensor(NTq[0][:], pk[PD][:, 0:TC], -1.0, e2b[:],
                                                 OP.mult, OP.mult).then_inc(sV, 1)
                  vv += 1
                  bk[PD] = (sV, vv)
                  guard(PE)
                  nc.tensor.matmul(pk[PE][:, 0:TC], oh, expg[:], start=True, stop=True).then_inc(sT, 1)
                  tt += 1
                  nc.vector.wait_ge(sT, tt)
                  nc.vector.tensor_mul(Kb[0:64, :], kh, pk[PE][rq:rq + 64, 0:TC]).then_inc(sV, 1)
                  vv += 1
                  nc.vector.wait_ge(sV, vv)
                  nc.vector.tensor_mul(Qb[0:64, :], qh, pk[PE][rq:rq + 64, 0:TC]).then_inc(sV, 1)
                  vv += 1
                  bk[PE] = (sV, vv)
                  nc.vector.wait_ge(sV, vv)
                  nc.vector.tensor_copy(S0b[:], self.Sst[:, h * DVH:(h + 1) * DVH]).then_inc(sV, 1)
                  vv += 1
                  guard(PF)
                  nc.tensor.wait_ge(sV, vv)
                  nc.tensor.matmul(pk[PF][:, 0:DVH], Kb[0:64, :], S0b[:], start=True, stop=True).then_inc(sT, 1)
                  tt += 1
                  nc.tensor.transpose(pX[:, 0:TC], vh, self.c_identb[:]).then_inc(sT, 1)
                  tt += 1
                  nc.vector.wait_ge(sT, tt)
                  nc.vector.tensor_copy(Vt[:], pX[:, 0:TC]).then_inc(sV, 1)
                  vv += 1
                  nc.vector.wait_ge(sV, vv)
                  nc.vector.tensor_sub(Uj[0][:], Vt[:], pk[PF][:, 0:DVH]).then_inc(sV, 1)
                  vv += 1
                  bk[PF] = (sV, vv)
                  nc.vector.wait_ge(sV, vv)
                  nc.vector.tensor_scalar(Uj[0][:], Uj[0][:], bTb[:, h:h + 1], None,
                                          OP.mult).then_inc(sV, 1)
                  vv += 1
                  upar = 0
                  qpar = 0
                  for j in range(7):
                      nc.tensor.wait_ge(sV, vv)
                      nc.tensor.matmul(pY[:, 0:DVH], NTq[qpar][:], Uj[upar][:],
                                       start=True, stop=True).then_inc(sT, 1)
                      tt += 1
                      nc.vector.wait_ge(sT, tt)
                      nc.vector.tensor_add(Uj[1 - upar][:], Uj[upar][:], pY[:, 0:DVH]).then_inc(sV, 1)
                      vv += 1
                      upar = 1 - upar
                      if j < 6:
                          nc.tensor.wait_ge(sV, vv)
                          nc.tensor.transpose(pY[:, 0:TC], NTq[qpar][:], self.c_identf[:]).then_inc(sT, 1)
                          tt += 1
                          nc.vector.wait_ge(sT, tt)
                          nc.vector.tensor_copy(Tq[:], pY[:, 0:TC]).then_inc(sV, 1)
                          vv += 1
                          nc.tensor.wait_ge(sV, vv)
                          nc.tensor.matmul(pY[:, 0:TC], Tq[:], NTq[qpar][:],
                                           start=True, stop=True).then_inc(sT, 1)
                          tt += 1
                          nc.vector.wait_ge(sT, tt)
                          nc.vector.tensor_copy(NTq[1 - qpar][:], pY[:, 0:TC]).then_inc(sV, 1)
                          vv += 1
                          qpar = 1 - qpar
                  nc.vector.wait_ge(sV, vv)
                  nc.vector.tensor_copy(Ub[:], Uj[upar][:]).then_inc(sV, 1)
                  vv += 1
                  guard(PA)
                  nc.tensor.wait_ge(sV, vv)
                  nc.tensor.matmul(pk[PA][:, 0:TC], Ub[:], Wb[:], start=True, stop=False)
                  nc.tensor.matmul(pk[PA][:, 0:TC], S0b[:], Qb[0:64, :],
                                   start=False, stop=True).then_inc(sT, 1)
                  tt += 1
                  nc.vector.wait_ge(sT, tt)
                  nc.vector.tensor_copy(of32[:], pk[PA][:, 0:TC]).then_inc(sV, 1)
                  vv += 1
                  bk[PA] = (sV, vv)
                  nc.vector.wait_ge(sV, vv)
                  nc.vector.tensor_mul(o2b[:], of32[:], of32[:]).then_inc(sV, 1)
                  vv += 1
                  nc.vector.wait_ge(sV, vv)
                  nc.vector.tensor_copy(osb[:], of32[:]).then_inc(sV, 1)
                  vv += 1
                  guard(PB)
                  nc.tensor.wait_ge(sV, vv)
                  nc.tensor.matmul(pk[PB][0:1, 0:TC], self.c_ones128c[:], o2b[:],
                                   start=True, stop=True).then_inc(sT, 1)
                  tt += 1
                  nc.vector.wait_ge(sT, tt)
                  nc.vector.tensor_scalar(rb2[:], pk[PB][0:1, 0:TC], 1.0 / DVH, 1e-5,
                                          OP.mult, OP.add).then_inc(sV, 1)
                  vv += 1
                  bk[PB] = (sV, vv)
                  nc.scalar.wait_ge(sV, vv)
                  nc.scalar.sqrt(rb2[:], rb2[:]).then_inc(sA, 1)
                  aa += 1
                  nc.vector.wait_ge(sA, aa)
                  nc.vector.reciprocal(rb2[:], rb2[:]).then_inc(sV, 1)
                  vv += 1
                  guard(PC)
                  nc.tensor.wait_ge(sV, vv)
                  nc.tensor.matmul(pk[PC][:, 0:TC], self.c_ones1[:], rb2[:],
                                   start=True, stop=True).then_inc(sT, 1)
                  tt += 1
                  nc.vector.wait_ge(sT, tt)
                  nc.vector.scalar_tensor_tensor(onb[:], osb[:], self.bc_gnw[:, 0:1],
                                                 pk[PC][:, 0:TC], OP.mult, OP.mult).then_inc(sV, 1)
                  vv += 1
                  bk[PC] = (sV, vv)
                  nc.vector.wait_ge(sV, vv)
                  nc.vector.tensor_mul(ostage[:, h * TC:(h + 1) * TC], onb[:], gh).then_inc(sV, 1)
                  vv += 1
                  guard(PD)
                  nc.tensor.matmul(pk[PD][:, 0:1], expdec[:], self.c_eye12[:, h:h + 1],
                                   start=True, stop=True).then_inc(sT, 1)
                  tt += 1
                  nc.vector.wait_ge(sT, tt)
                  nc.vector.tensor_copy(dcs[:], pk[PD][:, 0:1]).then_inc(sV, 1)
                  vv += 1
                  bk[PD] = (sV, vv)
                  nc.tensor.wait_ge(sV, vv)
                  nc.tensor.transpose(pX[:, 0:64], kh, self.c_identb[rq:rq + 64, rq:rq + 64]).then_inc(sT, 1)
                  tt += 1
                  nc.vector.wait_ge(sT, tt)
                  nc.vector.tensor_scalar(KtT[:], pX[:, 0:64], dcs[:, 0:1], None,
                                          OP.mult).then_inc(sV, 1)
                  vv += 1
                  guard(PF)
                  nc.tensor.wait_ge(sV, vv)
                  nc.tensor.matmul(pk[PF][0:64, 0:DVH], KtT[:], Ub[:], start=True, stop=True).then_inc(sT, 1)
                  tt += 1
                  guard(PE)
                  nc.tensor.matmul(pk[PE][:, 0:1], oh, expg[:, TC - 1:TC], start=True, stop=True).then_inc(sT, 1)
                  tt += 1
                  nc.vector.wait_ge(sT, tt)
                  nc.vector.tensor_copy(eGb[:], pk[PE][:, 0:1]).then_inc(sV, 1)
                  vv += 1
                  bk[PE] = (sV, vv)
                  nc.vector.wait_ge(sV, vv)
                  nc.vector.scalar_tensor_tensor(self.Sst[:, h * DVH:(h + 1) * DVH],
                                                 self.Sst[:, h * DVH:(h + 1) * DVH],
                                                 eGb[0:64, 0:1], pk[PF][0:64, 0:DVH],
                                                 OP.mult, OP.add).then_inc(sV, 1)
                  vv += 1
                  bk[PF] = (sV, vv)
              nc.vector.wait_ge(sV, vv)
              wop = 0
              for m in range(8):
                  b = PA if m % 2 == 0 else PB
                  guard(b)
                  for k in range(12):
                      mm = nc.tensor.matmul(pk[b][:, 0:TC], woB[:, k * D + m * 128:k * D + (m + 1) * 128],
                                            ostage[:, k * TC:(k + 1) * TC],
                                            start=(k == 0), stop=(k == 11))
                  mm.then_inc(sT, 1)
                  tt += 1
                  nc.scalar.wait_ge(sT, tt)
                  nc.scalar.copy(s1stg[:, m * 512 + cb * 128:m * 512 + (cb + 1) * 128], pk[b][:, 0:TC]).then_inc(sA, 1)
                  aa += 1
                  bk[b] = (sA, aa)
            nc.scalar.wait_ge(sA, aa)
            nc.scalar.dma_start(out=self.s1z[ts(c, 1), :, :], in_=s1stg[:]).then_inc(sO, 16)
            nc.gpsimd.wait_ge(sO, 16)
            self.clears(sL, self.sem("d_LA"), sT, sV, sA, sO)

    def ph_route(self):
        nc = self.nc
        lg = self.sb("rt_lg", [8, 128], F32)
        pr = self.sb("rt_pr", [128, 8], F32)
        ex = self.sb("rt_ex", [128, 8], F32)
        t8 = self.sb("rt_t8", [128, 8], F32)
        msk = self.sb("rt_m", [128, 8 * 4], F32)
        dmp = self.sb("rt_d", [128, 8], F32)
        st_ = self.sb("rt_s", [128, 8], F32)
        ones8 = self.sb("rt_o8", [128, 8], F32)
        wes = self.sb("rt_we", [128, 8], F32)
        wstg = self.sb("rt_ws", [8, 128], F32)
        pR = self.ps("rt_p", [128, 512], F32)
        sL, sV, sA, sT, sO = self.phase_sems("rt_L", "rt_V", "rt_A", "rt_T", "rt_O")
        nc.vector.memset(ones8[:], 1.0).then_inc(sV, 1)
        NV = 24
        for i in range(S // 128):
            vv = 1 + i * NV
            aa = i * 2
            tb = i * 2
            if i > 0:
                nc.sync.wait_ge(sA, aa)
            nc.sync.wait_ge(sL, i * 16)
            nc.sync.dma_start(out=lg[:], in_=self.gl_d[24:32, i * 128:(i + 1) * 128]).then_inc(sL, 16)
            nc.tensor.wait_ge(sL, i * 16 + 16)
            if i > 0:
                nc.tensor.wait_ge(sA, aa - 1)
                nc.tensor.wait_ge(sV, vv)
            nc.tensor.transpose(pR[:, 0:8], lg[:], self.c_identf[0:8, 0:8]).then_inc(sT, 1)
            nc.scalar.wait_ge(sT, tb + 1)
            nc.scalar.copy(pr[:], pR[:, 0:8]).then_inc(sA, 1)
            aa += 1
            nc.vector.wait_ge(sA, aa)
            nc.vector.reduce_max(st_[:, 0:1], pr[:], AX.X, negate=True).then_inc(sV, 1)
            vv += 1
            nc.scalar.wait_ge(sV, vv)
            nc.scalar.activation(ex[:], pr[:], AF.Exp, bias=st_[:, 0:1], scale=1.0,
                                 accum_out=st_[:, 1:2]).then_inc(sA, 1)
            aa += 1
            nc.vector.wait_ge(sA, aa)
            nc.vector.reciprocal(st_[:, 2:3], st_[:, 1:2]).then_inc(sV, 1)
            vv += 1
            nc.vector.wait_ge(sV, vv)
            nc.vector.tensor_scalar(pr[:], ex[:], st_[:, 2:3], None, OP.mult).then_inc(sV, 1)
            vv += 1
            nc.vector.wait_ge(sV, vv)
            nc.vector.max(t8[:], pr[:]).then_inc(sV, 1)
            vv += 1
            nc.vector.wait_ge(sV, vv)
            nc.vector.tensor_scalar(msk[:, 0:8], pr[:], t8[:, 0:1], None, OP.is_equal).then_inc(sV, 1)
            vv += 1
            nc.vector.wait_ge(sV, vv)
            nc.vector.tensor_tensor_scan(dmp[:], ones8[:], msk[:, 0:8], 0.0,
                                         OP.mult, OP.add).then_inc(sV, 1)
            vv += 1
            nc.vector.wait_ge(sV, vv)
            nc.vector.tensor_scalar(dmp[:], dmp[:], 1.0, None, OP.is_equal).then_inc(sV, 1)
            vv += 1
            nc.vector.wait_ge(sV, vv)
            nc.vector.tensor_mul(msk[:, 8:16], msk[:, 0:8], dmp[:]).then_inc(sV, 1)
            vv += 1
            nc.vector.wait_ge(sV, vv)
            nc.vector.tensor_scalar(msk[:, 16:24], pr[:], t8[:, 1:2], None, OP.is_equal).then_inc(sV, 1)
            vv += 1
            nc.vector.wait_ge(sV, vv)
            nc.vector.scalar_tensor_tensor(msk[:, 16:24], msk[:, 8:16], -1.0, msk[:, 16:24],
                                           OP.mult, OP.add).then_inc(sV, 1)
            vv += 1
            nc.vector.wait_ge(sV, vv)
            nc.vector.tensor_tensor_scan(dmp[:], ones8[:], msk[:, 16:24], 0.0,
                                         OP.mult, OP.add).then_inc(sV, 1)
            vv += 1
            nc.vector.wait_ge(sV, vv)
            nc.vector.tensor_scalar(dmp[:], dmp[:], 1.0, None, OP.is_equal).then_inc(sV, 1)
            vv += 1
            nc.vector.wait_ge(sV, vv)
            nc.vector.tensor_mul(msk[:, 24:32], msk[:, 16:24], dmp[:]).then_inc(sV, 1)
            vv += 1
            nc.vector.wait_ge(sV, vv)
            nc.vector.tensor_add(st_[:, 3:4], t8[:, 0:1], t8[:, 1:2]).then_inc(sV, 1)
            vv += 1
            nc.vector.wait_ge(sV, vv)
            nc.vector.tensor_scalar(st_[:, 4:5], st_[:, 3:4], 1.0, 1e-9, OP.mult, OP.add).then_inc(sV, 1)
            vv += 1
            nc.vector.wait_ge(sV, vv)
            nc.vector.reciprocal(st_[:, 4:5], st_[:, 4:5]).then_inc(sV, 1)
            vv += 1
            nc.vector.wait_ge(sV, vv)
            nc.vector.tensor_scalar(st_[:, 5:6], st_[:, 3:4], st_[:, 4:5], None, OP.mult).then_inc(sV, 1)
            vv += 1
            nc.vector.wait_ge(sV, vv)
            nc.vector.tensor_scalar(st_[:, 6:7], st_[:, 5:6], 1.0, 1e-9, OP.mult, OP.add).then_inc(sV, 1)
            vv += 1
            nc.vector.wait_ge(sV, vv)
            nc.vector.reciprocal(st_[:, 6:7], st_[:, 6:7]).then_inc(sV, 1)
            vv += 1
            nc.vector.wait_ge(sV, vv)
            nc.vector.tensor_scalar(st_[:, 7:8], st_[:, 4:5], st_[:, 6:7], None, OP.mult).then_inc(sV, 1)
            vv += 1
            nc.vector.wait_ge(sV, vv)
            nc.vector.tensor_scalar(wes[:], msk[:, 8:16], t8[:, 0:1], None, OP.mult).then_inc(sV, 1)
            vv += 1
            nc.vector.wait_ge(sV, vv)
            nc.vector.scalar_tensor_tensor(wes[:], msk[:, 24:32], t8[:, 1:2], wes[:],
                                           OP.mult, OP.add).then_inc(sV, 1)
            vv += 1
            nc.vector.wait_ge(sV, vv)
            nc.vector.tensor_scalar(wes[:], wes[:], st_[:, 7:8], None, OP.mult).then_inc(sV, 1)
            vv += 1
            nc.tensor.wait_ge(sV, vv)
            nc.tensor.transpose(pR[0:8, 8:136], wes[:], self.c_identf[:]).then_inc(sT, 1)
            nc.vector.wait_ge(sT, tb + 2)
            if i > 0:
                nc.vector.wait_ge(sO, i * 16)
            nc.vector.tensor_copy(wstg[:], pR[0:8, 8:136]).then_inc(sV, 1)
            vv += 1
            assert vv == 1 + (i + 1) * NV, (vv, i)
            nc.scalar.wait_ge(sV, vv)
            nc.scalar.wait_ge(sO, i * 16)
            nc.scalar.dma_start(out=self.weT_d[:, i * 128:(i + 1) * 128], in_=wstg[:]).then_inc(sO, 16)
        nc.sync.wait_ge(sO, (S // 128) * 16)

    def ph_moe(self):
        nc = self.nc
        xnb = self.sb("m_x", [128, 8 * 512], BF16)
        s1c = self.sb("m_s1", [128, 8 * 512], F32)
        x1c = self.sb("m_x1", [128, 4 * 1024], F32)
        acc = self.sb("m_ac", [128, 8 * 512], F32)
        w1b = self.sb("m_w1", [128, 8 * 2048], BF16)
        w2b = self.sb("m_w2", [128, 32 * 512], BF16)
        h1b = self.sb("m_h1", [128, 32 * 512], BF16)
        webr = self.sb("m_wr", [1, 512], F32)
        weB = self.sb("m_wb", [128, 512], F32)
        mbc = self.sb("m_mb", [128, 40], F32)
        ptok = self.sb("m_pt", [128, 4 * 1024], F32)
        pz = [self.ps("m_p%d" % j, [128, 512], F32) for j in range(4)]
        p2 = [self.ps("m_q%d" % j, [128, 512], F32) for j in range(4)]
        oL, oV, oO = self.phase_sems("m_oL", "m_oV", "m_oO")
        iL, iT, iE, iV, iA, iW = self.phase_sems("m_iL", "m_iT", "m_iE", "m_iV", "m_iA", "m_iW")
        oTf, oEf, oA = self.phase_sems("m_oTf", "m_oEf", "m_oA")
        with self.fori(0, S // 512) as c:
            nc.sync.dma_start(out=xnb[:], in_=self.xn1z[ts(c, 1), :, :]).then_inc(oL, 16)
            nc.sync.dma_start(out=s1c[:], in_=self.s1z[ts(c, 1), :, :]).then_inc(oL, 16)
            nc.scalar.dma_start(out=x1c[:].rearrange("p (t d) -> p t d", t=4),
                                in_=self.x1s.rearrange("(n p) d -> p n d", p=128)[:, ts(c, 4), :]).then_inc(oLA := self.sem("m_oLA"), 16)
            nc.vector.wait_ge(oL, 32)
            nc.vector.memset(acc[:], 0.0).then_inc(oV, 1)
            with self.fori(0, NEXP, depth=1) as e:
                nc.sync.dma_start(out=webr[:], in_=self.weT_d[ts(e, 1), ts(c, 512)]).then_inc(iL, 16)
                nc.scalar.dma_start(out=mbc[:], in_=self.mbboth_d[:, ts(e, 1), :].rearrange("p e n -> p (e n)")).then_inc(iLA := self.sem("m_iLA"), 16)
                for mh in range(2):
                    if mh == 1:
                        nc.sync.wait_ge(iW, 16)
                    nc.sync.dma_start(out=w1b[:].rearrange("p (k m) -> p k m", k=8),
                                      in_=self.mW1_b[ts(e, 8), :, mh * 2048:(mh + 1) * 2048].rearrange("k p m -> p k m")).then_inc(iW, 16)
                    nc.tensor.wait_ge(iW, (mh + 1) * 16)
                    nc.tensor.wait_ge(oV, 1)
                    for mm_ in range(16):
                        m = mh * 16 + mm_
                        b = mm_ % 4
                        gi = m
                        if m >= 4:
                            nc.tensor.wait_ge(iE, m - 3)
                        for k in range(8):
                            mmi = nc.tensor.matmul(pz[b][:], w1b[:, k * 2048 + mm_ * 128:k * 2048 + (mm_ + 1) * 128],
                                                   xnb[:, k * 512:(k + 1) * 512],
                                                   start=(k == 0), stop=(k == 7))
                        mmi.then_inc(iT, 1)
                        nc.scalar.wait_ge(iT, mh * 16 + mm_ + 1)
                        nc.scalar.wait_ge(iLA, 16)
                        nc.scalar.activation(h1b[:, m * 512:(m + 1) * 512], pz[b][:], AF.Gelu,
                                             bias=mbc[:, m:m + 1], scale=1.0).then_inc(iE, 1)
                    if mh == 0:
                        nc.sync.wait_ge(iT, 16)
                nc.tensor.wait_ge(iL, 16)
                nc.tensor.matmul(p2[3][:], self.c_ones1[:], webr[:], start=True, stop=True).then_inc(iT, 1)
                nc.scalar.wait_ge(iT, 33)
                nc.scalar.copy(weB[:], p2[3][:]).then_inc(iA, 1)
                for m2h in range(2):
                    if m2h == 1:
                        nc.sync.wait_ge(iT, 37)
                    nc.sync.wait_ge(iW, 32 + m2h * 16)
                    nc.sync.dma_start(out=w2b[:].rearrange("p (k m) -> p k m", k=32),
                                      in_=self.mW2_b[ts(e, 32), :, m2h * 512:(m2h + 1) * 512].rearrange("k p m -> p k m")).then_inc(iW, 16)
                    nc.tensor.wait_ge(iW, 32 + (m2h + 1) * 16)
                    nc.tensor.wait_ge(iE, 32)
                    for m2 in range(4):
                        d = m2h * 4 + m2
                        b = m2 % 4
                        if d == 3:
                            nc.tensor.wait_ge(iA, 1)
                        if d >= 4:
                            nc.tensor.wait_ge(iV, 2 * d - 7)
                        for k in range(32):
                            mmi = nc.tensor.matmul(p2[b][:], w2b[:, k * 512 + m2 * 128:k * 512 + (m2 + 1) * 128],
                                                   h1b[:, k * 512:(k + 1) * 512],
                                                   start=(k == 0), stop=(k == 31))
                        mmi.then_inc(iT, 1)
                        nc.vector.wait_ge(iT, 33 + m2h * 4 + m2 + 1)
                        nc.vector.wait_ge(iA, 1)
                        if d >= 1:
                            nc.vector.wait_ge(iV, 2 * d)
                        nc.vector.scalar_tensor_tensor(ptok[:, 0:512], p2[b][:], mbc[:, 32 + d:33 + d],
                                                       weB[:], OP.add, OP.mult).then_inc(iV, 1)
                        nc.vector.wait_ge(iV, m2h * 8 + m2 * 2 + 1)
                        nc.vector.tensor_add(acc[:, d * 512:(d + 1) * 512],
                                             acc[:, d * 512:(d + 1) * 512], ptok[:, 0:512]).then_inc(iV, 1)
                nc.gpsimd.wait_ge(iV, 16)
                self.clears(iL, self.sem("m_iLA"), iT, iE, iV, iA, iW)
            nc.scalar.wait_ge(oV, 1)
            nc.scalar.activation(acc[:], acc[:], AF.Silu, bias=0.0, scale=1.0).then_inc(oA, 1)
            nc.vector.wait_ge(oA, 1)
            nc.vector.tensor_mul(s1c[:], acc[:], s1c[:]).then_inc(oV, 1)
            nc.tensor.wait_ge(oV, 2)
            ntr = 0
            for tt2 in range(4):
                for e2 in range(8):
                    b = ntr % 2
                    if ntr >= 2:
                        nc.tensor.wait_ge(oEf, ntr - 1)
                    nc.tensor.transpose(pz[b][:, 0:128],
                                        s1c[:, e2 * 512 + tt2 * 128:e2 * 512 + (tt2 + 1) * 128],
                                        self.c_identf[:]).then_inc(oTf, 1)
                    nc.scalar.wait_ge(oTf, ntr + 1)
                    nc.scalar.copy(ptok[:, tt2 * 1024 + e2 * 128:tt2 * 1024 + (e2 + 1) * 128],
                                   pz[b][:, 0:128]).then_inc(oEf, 1)
                    ntr += 1
            nc.vector.wait_ge(oEf, 32)
            nc.vector.wait_ge(self.sem("m_oLA"), 16)
            nc.vector.tensor_add(x1c[:], x1c[:], ptok[:]).then_inc(oV, 1)
            nc.scalar.wait_ge(oV, 3)
            nc.scalar.dma_start(out=self.out.rearrange("(n p) d -> p n d", p=128)[:, ts(c, 4), :],
                                in_=x1c[:].rearrange("p (t d) -> p t d", t=4)).then_inc(oO, 16)
            nc.gpsimd.wait_ge(oO, 16)
            self.clears(oL, self.sem("m_oLA"), oV, oO, oTf, oEf, oA)

    def finish(self, dbg=()):
        nc = self.nc
        self._semmap = {}
        self._semidx = 0
        sF = self.sem("finS")
        n = 0
        for name in dbg:
            t = getattr(self, name)
            o = nc.dram_tensor("dbg_" + name, list(t.shape), t.dtype, kind="ExternalOutput").ap()
            nc.sync.dma_start(out=o[:], in_=t[:]).then_inc(sF, 16)
            n += 1
        if n:
            nc.sync.wait_ge(sF, 16 * n)


def build(segment="all", dbg=()):
    nc = bass.Bass("TRN2", target_bir_lowering=False)
    b = Builder(nc)
    b.declare_io()
    phases = Builder.PHASES if segment == "all" else segment
    for s in b._pool:
        nc.gpsimd.sem_clear(s)
    b.bar()
    for p in phases:
        b.begin_phase()
        with nc.named_scope(p):
            getattr(b, "ph_" + p)()
            b.bar()
        b.end_phase()
    b.finish(dbg)
    return nc


def _constants():
    import ml_dtypes
    c = {}
    c["c_identf"] = np.eye(128, dtype=np.float32)
    c["c_identb"] = np.eye(128, dtype=ml_dtypes.bfloat16)
    p = np.arange(128)[:, None]
    f = np.arange(128)[None, :]
    negI = np.where(p <= f, 0.0, -1e30).astype(np.float32)
    negS = np.where(p < f, 0.0, -1e30).astype(np.float32)
    c["c_negtI"] = negI.T.copy()
    c["c_negtS"] = negS.T.copy()
    c["c_negIr"] = negI
    c["c_negSr"] = negS
    for nm, n in [("c_oneh16", 16), ("c_oneh12", 12), ("c_oneh8", 8)]:
        oh = np.zeros((n, n * 128), np.float32)
        for h in range(n):
            oh[h, h * 128:(h + 1) * 128] = 1.0
        c[nm] = oh
    c["c_oneh16b"] = c["c_oneh16"].astype(ml_dtypes.bfloat16)
    c["c_eye12"] = np.eye(12, dtype=np.float32)
    c["c_ones1"] = np.ones((1, 128), np.float32)
    c["c_ones128c"] = np.ones((128, 1), np.float32)
    s2 = np.zeros((128, 2), np.float32)
    s2[:64, 0] = 1.0
    s2[64:, 1] = 1.0
    c["c_sel2"] = s2
    b2 = np.zeros((2, 128), np.float32)
    b2[0, :64] = 1.0
    b2[1, 64:] = 1.0
    c["c_bc2"] = b2
    return c


def make_in_maps(inputs):
    consts = _constants()
    in_maps = []
    for b in range(8):
        m = {}
        for k, v in inputs.items():
            if k == "L0_attn_x":
                m[k] = np.ascontiguousarray(v[b])
            else:
                m[k] = np.ascontiguousarray(v)
        m.update(consts)
        in_maps.append(m)
    return in_maps


def kernel(**inputs):
    nc = build("all")
    res = run_bass_kernel_spmd(nc, make_in_maps(inputs), core_ids=list(range(8)))
    return np.stack([res.results[b]["out"] for b in range(8)], axis=0)
